# version 02
# baseline (speedup 1.0000x reference)
import math
import os as _os2
import os as _os3
import contextlib
import numpy as np
import ml_dtypes
import concourse.bass as bass
import concourse.mybir as mybir
from concourse.bass_utils import run_bass_kernel_spmd

F32 = mybir.dt.float32
BF16 = mybir.dt.bfloat16
I32 = mybir.dt.int32
AF = mybir.ActivationFunctionType
ALU = mybir.AluOpType
AX = mybir.AxisListType
BF = ml_dtypes.bfloat16

NCORES = 8
TOK = 2048
D = 2048
HALO = 1024
LTOK = TOK + HALO
NCH = 16
LCH = 24
QB = 256
NQB = 8
SEQ = 8192
CAP = 2048
NE = 8
SQD = math.sqrt(128.0)
_AB = 2.0 ** (-(2.0 ** (-(math.log2(16.0) - 3.0))))
SLOPES = np.array([_AB ** p for p in range(1, 17)], dtype=np.float64)

N_PHASES = 4
import os
DEV_STOP = os.environ.get('DEV_STOP', 'full')


class S:
    __slots__ = ("h", "n")

    def __init__(self, h):
        self.h = h
        self.n = 0

    def inc(self, instr, k=1):
        instr.then_inc(self.h, k)
        self.n += k
        return self.n


class Ctx:
    def __init__(self, nc, block, stack):
        self.nc = nc
        self.block = block
        self.stack = stack
        self.sems = {}

    def sem(self, name):
        key = name.split("_", 1)[1] if "_" in name else name
        if key not in self.sems:
            h = self.stack.enter_context(self.nc.semaphore(key))
            self.sems[key] = S(h)
        return self.sems[key]

    def sbuf(self, name, shape, dt, stack=None):
        return (stack or self.stack).enter_context(
            self.nc.sbuf_tensor(name, shape, dt))

    def psum(self, name, shape, stack=None):
        return (stack or self.stack).enter_context(
            self.nc.psum_tensor(name, shape, F32))



def emit_ln(C, tag, n_chunks, x_src, xn_dram, row0, wait, comp_out=None,
            stack=None, prev=None):
    block = C.block
    NB = 4
    xb = [C.sbuf(f"{tag}_x{i}", [128, D], F32, stack) for i in range(NB)]
    xnb = [C.sbuf(f"{tag}_n{i}", [128, D], BF16, stack) for i in range(NB)]
    sq = C.sbuf(f"{tag}_sq", [128, D], F32, stack)
    st = C.sbuf(f"{tag}_st", [128, 8], F32, stack)
    cs = C.sbuf(f"{tag}_cs", [128, 1], F32, stack) if comp_out is not None else None

    s_ld = [C.sem(f"{tag}_ld{i}") for i in range(NB)]
    s_a = C.sem(f"{tag}_a")
    s_v = C.sem(f"{tag}_v")
    s_xs = [C.sem(f"{tag}_xs{i}") for i in range(NB)]

    xb_free = [0] * NB
    xnb_st = [0] * NB
    xn_at = []
    ex2_done = 0

    for k in range(n_chunks):
        bi = k % NB
        kw = list(wait) if k == 0 else []
        gate = xb_free[bi]

        def ld(sy, k=k, kw=kw, gate=gate, bi=bi):
            for (ws, wn) in kw:
                sy.wait_ge(ws.h, wn)
            if k == 0 and prev is not None:
                prev.barrier(sy)
            if gate:
                sy.wait_ge(s_v.h, gate)
            s_ld[bi].inc(sy.dma_start(out=xb[bi][:], in_=x_src(k)), 16)

        block.sync(ld)
        ldn = s_ld[bi].n

        def act1(s, ldn=ldn, bi=bi, exd=ex2_done):
            s.wait_ge(s_ld[bi].h, ldn)
            if exd:
                s.wait_ge(s_v.h, exd)
            s_a.inc(s.activation(sq[:], xb[bi][:], AF.Square,
                                 accum_out=st[:, 1:2]))

        block.scalar(act1)
        an1 = s_a.n

        def vec1(v, ldn=ldn, an1=an1, bi=bi):
            v.wait_ge(s_ld[bi].h, ldn)
            if comp_out is not None:
                s_v.inc(v.tensor_reduce(cs[:], xb[bi][:], AX.X, ALU.add,
                                        apply_absolute_value=True))
            s_v.inc(v.tensor_reduce(st[:, 0:1], xb[bi][:], AX.X, ALU.add))
            v.wait_ge(s_v.h, s_v.n)
            v.wait_ge(s_a.h, an1)
            s_v.inc(v.tensor_scalar(st[:, 2:3], st[:, 0:1], 1.0 / D, None, ALU.mult))
            s_v.inc(v.tensor_scalar(st[:, 3:4], st[:, 1:2], 1.0 / D, None, ALU.mult))
            v.wait_ge(s_v.h, s_v.n)
            s_v.inc(v.tensor_mul(st[:, 4:5], st[:, 2:3], st[:, 2:3]))
            v.wait_ge(s_v.h, s_v.n)
            s_v.inc(v.tensor_sub(st[:, 5:6], st[:, 3:4], st[:, 4:5]))
            v.wait_ge(s_v.h, s_v.n)
            s_v.inc(v.tensor_scalar(st[:, 5:6], st[:, 5:6], 1e-5, None, ALU.add))

        block.vector(vec1)
        ex2_done = s_v.n - 3
        vn1 = s_v.n

        def act2(s, vn1=vn1, k=k):
            s.wait_ge(s_v.h, vn1)
            s_a.inc(s.activation(st[:, 6:7], st[:, 5:6], AF.Sqrt))
            if comp_out is not None:
                s_a.inc(s.activation(comp_out[:, k:k + 1], cs[:], AF.Sigmoid))

        block.scalar(act2)
        an2 = s_a.n

        def vec2(v, an2=an2, bi=bi, xng=xnb_st[bi]):
            v.wait_ge(s_a.h, an2)
            s_v.inc(v.reciprocal(st[:, 7:8], st[:, 6:7]))
            v.wait_ge(s_v.h, s_v.n)
            if xng:
                v.wait_ge(s_xs[bi].h, xng)
            s_v.inc(v.tensor_scalar(xnb[bi][:], xb[bi][:],
                                    st[:, 2:3], st[:, 7:8],
                                    ALU.subtract, ALU.mult))

        block.vector(vec2)
        xb_free[bi] = s_v.n
        vn2 = s_v.n

        def sto(sy, vn2=vn2, k=k, bi=bi):
            sy.wait_ge(s_v.h, vn2)
            s_xs[bi].inc(sy.dma_start(
                out=xn_dram[row0 + k * 128: row0 + (k + 1) * 128, :],
                in_=xnb[bi][:]), 16)

        block.sync(sto)
        xnb_st[bi] = s_xs[bi].n
        xn_at.append((bi, s_xs[bi].n))

    class LNOut:
        def barrier(self, eng):
            for s in s_xs:
                if s.n:
                    eng.wait_ge(s.h, s.n)

        def chunk_barrier(self, eng, k):
            bi, n = xn_at[k]
            eng.wait_ge(s_xs[bi].h, n)

    return LNOut()



def emit_attn_a(C, io):
    block = C.block
    wq_a, wk_a, wv_a, wo_a = io["wq_a"], io["wk_a"], io["wv_a"], io["wo_a"]
    xn_dram, q_dram, out_x = io["xn_dram"], io["q_dram"], io["out_x"]
    oner, onerb = io["oner"], io["onerb"]
    maska, valida, bo = io["maska"], io["valida"], io["bo_a"]
    ln = io["ln1"]

    stA = contextlib.ExitStack()
    attT = C.sbuf("a1attT", [128, 16 * 2048], BF16, stA)
    st1 = contextlib.ExitStack()
    stKV = contextlib.ExitStack()
    kT = C.sbuf("a1kT", [128, 4 * 3072], BF16, st1)
    vv = C.sbuf("a1v", [128, 24 * 512], BF16, st1)
    xnt = [C.sbuf(f"a1xnt{i}", [128, 16 * 512], BF16, st1) for i in range(2)]
    wks = C.sbuf("a1wk", [128, 16 * 512], BF16, stKV)
    wvs = C.sbuf("a1wv", [128, 16 * 512], BF16, stKV)

    pj = C.psum("a1pj", [128, 512], st1)
    pj2 = C.psum("a1pj2", [128, 512], st1)
    psc = [C.psum(f"a1psc{i}", [128, 512], st1) for i in range(2)]
    pat = C.psum("a1pat", [128, 512], st1)
    psm = C.psum("a1psm", [128, 512], st1)
    prb = C.psum("a1prb", [128, 512], st1)

    s_t = [C.sem(f"a1_t{i}") for i in range(2)]
    s_w = C.sem("a1_w")
    s_wq = [C.sem(f"a1_wq{i}") for i in range(2)]
    s_kve = C.sem("a1_kve")
    s_mm = C.sem("a1_mm")
    s_qev = C.sem("a1_qev")
    s_qst = [C.sem(f"a1_qst{i}") for i in range(2)]
    s_ld = [C.sem(f"a1_ld{i}") for i in range(2)]
    s_sc = C.sem("a1_sc")
    s_ex = C.sem("a1_ex")
    s_mk = C.sem("a1_mk")
    s_pv = C.sem("a1_pv")
    s_sm = C.sem("a1_sm")
    s_rc = C.sem("a1_rc")
    s_bc = C.sem("a1_bc")
    s_rb = C.sem("a1_rb")
    s_ev = C.sem("a1_ev")
    s_wo = C.sem("a1_wo")
    s_ymm = C.sem("a1_ymm")
    s_yev = C.sem("a1_yev")
    s_res = [C.sem(f"a1_res{i}") for i in range(2)]

    def ldw(g):
        ln.barrier(g)
        s_w.inc(g.dma_start(out=wks[:], in_=wk_a), 16)
        s_w.inc(g.dma_start(out=wvs[:], in_=wv_a), 16)

    block.gpsimd(ldw)

    xnt_free = [0, 0]
    pjf = 0
    pj2f = 0
    for T in range(6):
        def ldt(sy, T=T, gate=xnt_free[T % 2]):
            if T == 0:
                ln.barrier(sy)
            if gate:
                sy.wait_ge(s_mm.h, gate)
            s_t[T % 2].inc(sy.dma_start(
                out=xnt[T % 2][:].rearrange("p (dc r) -> p dc r", dc=16),
                in_=xn_dram[T * 512:(T + 1) * 512, :]
                    .rearrange("r (dc p) -> r dc p", p=128),
                transpose=True), 16)

        block.sync(ldt)
        tw = s_t[T % 2].n

        def mmkv(t, T=T, tw=tw, pjf=pjf, pj2f=pj2f):
            if T == 0:
                t.wait_ge(s_w.h, 32)
            t.wait_ge(s_t[T % 2].h, tw)
            wk3 = wks[:].rearrange("p (dc n) -> p dc n", dc=16)
            wv3 = wvs[:].rearrange("p (dc n) -> p dc n", dc=16)
            x3 = xnt[T % 2][:].rearrange("p (dc r) -> p dc r", dc=16)
            for g4 in range(4):
                if pjf + g4 > 0:
                    t.wait_ge(s_kve.h, pjf + g4)
                for dc in range(16):
                    ins = t.matmul(pj[:], wk3[:, dc, g4 * 128:(g4 + 1) * 128],
                                   x3[:, dc, :], start=(dc == 0), stop=(dc == 15))
                s_mm.inc(ins)
            for tc in range(4):
                if pj2f + tc > 0:
                    t.wait_ge(s_kve.h, pj2f + tc + 4)
                for dc in range(16):
                    ins = t.matmul(pj2[:], x3[:, dc, tc * 128:(tc + 1) * 128],
                                   wv3[:, dc, :],
                                   start=(dc == 0), stop=(dc == 15))
                s_mm.inc(ins)

        block.tensor(mmkv)
        mmn = s_mm.n
        xnt_free[T % 2] = mmn

        def evkv(v, T=T, mmn=mmn):
            base = mmn - 8
            for g4 in range(4):
                v.wait_ge(s_mm.h, base + g4 + 1)
                s_kve.inc(v.tensor_copy(
                    kT[:, g4 * 3072 + T * 512: g4 * 3072 + (T + 1) * 512], pj[:]))
            for tc in range(4):
                v.wait_ge(s_mm.h, base + 4 + tc + 1)
                s_kve.inc(v.tensor_copy(
                    vv[:, (4 * T + tc) * 512:(4 * T + tc + 1) * 512], pj2[:]))

        block.vector(evkv)
        pjf = s_kve.n
        pj2f = s_kve.n

    KV_EV = s_kve.n
    KV_MM = s_mm.n
    stKV.close()
    wqb = [C.sbuf(f"a1wq{i}", [128, 16 * 128], BF16, st1) for i in range(2)]
    qev = [C.sbuf(f"a1qe{i}", [128, 512], BF16, st1) for i in range(2)]

    mm_at = {}
    u = 0
    for pash, tiles in ((0, (2, 3)), (1, (4, 5))):
        for bi, T in enumerate(tiles):
            def ldt2(sy, T=T, bi=bi, gate=xnt_free[bi]):
                sy.wait_ge(s_mm.h, gate)
                s_t[bi].inc(sy.dma_start(
                    out=xnt[bi][:].rearrange("p (dc r) -> p dc r", dc=16),
                    in_=xn_dram[T * 512:(T + 1) * 512, :]
                        .rearrange("r (dc p) -> r dc p", p=128),
                    transpose=True), 16)

            block.sync(ldt2)
        twq = [s_t[0].n, s_t[1].n]

        for h in range(16):
            def ldq(g, h=h, u=u, kvn=KV_MM, wg=mm_at.get(u - 2, 0)):
                if u == 0 and h == 0:
                    g.wait_ge(s_mm.h, kvn)
                if wg:
                    g.wait_ge(s_mm.h, wg)
                s_wq[h % 2].inc(g.dma_start(
                    out=wqb[h % 2][:], in_=wq_a[h]), 16)

            block.gpsimd(ldq)
            wn = s_wq[h % 2].n

            for bi, T in enumerate(tiles):
                i = 2 * u + bi

                def mmq(t, h=h, bi=bi, wn=wn, tn=twq[bi], i=i, qf=s_qev.n):
                    t.wait_ge(s_wq[h % 2].h, wn)
                    t.wait_ge(s_t[bi].h, tn)
                    if i == 0:
                        t.wait_ge(s_kve.h, KV_EV)
                    if qf >= 2:
                        t.wait_ge(s_qev.h, qf - 1)
                    wq3 = wqb[h % 2][:].rearrange("p (dc f) -> p dc f", dc=16)
                    x3 = xnt[bi][:].rearrange("p (dc r) -> p dc r", dc=16)
                    for dc in range(16):
                        ins = t.matmul(pj[:] if i % 2 == 0 else pj2[:],
                                       wq3[:, dc, :], x3[:, dc, :],
                                       start=(dc == 0), stop=(dc == 15))
                    s_mm.inc(ins)

                block.tensor(mmq)
                mmn = s_mm.n

                def evq(v, mmn=mmn, i=i, stn=s_qst[i % 2].n):
                    v.wait_ge(s_mm.h, mmn)
                    if stn:
                        v.wait_ge(s_qst[i % 2].h, stn)
                    s_qev.inc(v.tensor_copy(qev[i % 2][:],
                                            pj[:] if i % 2 == 0 else pj2[:]))

                block.vector(evq)
                qn = s_qev.n

                def stq(sy, h=h, T=T, qn=qn, i=i):
                    sy.wait_ge(s_qev.h, qn)
                    s_qst[i % 2].inc(sy.dma_start(
                        out=q_dram[h * 128:(h + 1) * 128,
                                   (T - 2) * 512:(T - 1) * 512],
                        in_=qev[i % 2][:]), 16)

                block.sync(stq)
            mm_at[u] = s_mm.n
            u += 1
        xnt_free[0] = s_mm.n
        xnt_free[1] = s_mm.n

    QST = [s_qst[0].n, s_qst[1].n]
    import os as _os
    if _os.environ.get('DEV_STOP', 'full') in ('kv', 'q'):
        st1.close()
        stA.close()

        class _O:
            def barrier(self, eng):
                eng.wait_ge(s_qst[0].h, QST[0])
                eng.wait_ge(s_qst[1].h, QST[1])
        return _O()

    qsl = [C.sbuf(f"a1qsl{i}", [128, 16 * 256], BF16, st1) for i in range(2)]
    P = C.sbuf("a1P", [128, 10 * 256], BF16, st1)
    sum_sb = C.sbuf("a1sum", [1, 256], F32, st1)
    rec_sb = C.sbuf("a1rec", [1, 256], F32, st1)
    recB = C.sbuf("a1recB", [128, 256], F32, st1)

    qsl_free = [0, 0]
    _attlim = int(_os.environ.get('DEV_ATT_LIM', '999'))

    for jb in range(NQB):
        if jb * 16 >= _attlim:
            break

        def ldqs(sy, jb=jb, gate=qsl_free[jb % 2]):
            if jb == 0:
                sy.wait_ge(s_qst[0].h, QST[0])
                sy.wait_ge(s_qst[1].h, QST[1])
            if gate:
                sy.wait_ge(s_sc.h, gate)
            s_ld[jb % 2].inc(sy.dma_start(
                out=qsl[jb % 2][:].rearrange("p (hh r) -> p hh r", hh=16),
                in_=q_dram[:, jb * 256:(jb + 1) * 256]
                    .rearrange("(hh p) r -> p hh r", p=128)), 16)

        block.sync(ldqs)
        ldn = s_ld[jb % 2].n

        for h in range(16):
            g4 = h // 4
            hq = jb * 16 + h
            if hq >= _attlim:
                break

            def mmsc(t, jb=jb, h=h, g4=g4, hq=hq, ldn=ldn):
                t.wait_ge(s_ld[jb % 2].h, ldn)
                q_ap = qsl[jb % 2][:].rearrange(
                    "p (hh r) -> p hh r", hh=16)[:, h, :]
                for c in range(10):
                    g = hq * 10 + c
                    if g >= 2:
                        t.wait_ge(s_ex.h, g - 1)
                    kc = 2 * jb + c
                    ins = t.matmul(psc[c % 2][:, 0:256],
                                   kT[:, g4 * 3072 + kc * 128:
                                      g4 * 3072 + (kc + 1) * 128],
                                   q_ap, start=True, stop=True)
                    s_sc.inc(ins)

            block.tensor(mmsc)
            scn = s_sc.n

            def exps(s, scn=scn, pvgate=s_pv.n):
                if pvgate:
                    s.wait_ge(s_pv.h, pvgate)
                for c in range(10):
                    s.wait_ge(s_sc.h, scn - 10 + c + 1)
                    s_ex.inc(s.activation(P[:, c * 256:(c + 1) * 256],
                                          psc[c % 2][:, 0:256],
                                          AF.Exp, scale=1.0 / SQD))

            block.scalar(exps)
            exn = s_ex.n
            if _os.environ.get('DEV_ATT_OPS', 'full') == 'sc':
                continue

            def msk(v, exn=exn):
                for mi, c in enumerate((0, 1, 8, 9)):
                    v.wait_ge(s_ex.h, exn - 10 + c + 1)
                    s_mk.inc(v.tensor_mul(P[:, c * 256:(c + 1) * 256],
                                          P[:, c * 256:(c + 1) * 256],
                                          maska[:, mi * 256:(mi + 1) * 256]))

            block.vector(msk)
            mkn = s_mk.n
            if _os.environ.get('DEV_ATT_OPS', 'full') == 'msk':
                continue

            def mmpv(t, jb=jb, g4=g4, exn=exn, mkn=mkn, evgate=s_ev.n):
                if evgate:
                    t.wait_ge(s_ev.h, evgate)
                t.wait_ge(s_ex.h, exn)
                t.wait_ge(s_mk.h, mkn)
                for c in range(10):
                    kc = 2 * jb + c
                    t.matmul(pat[:, 0:256], vv[:, kc * 512 + g4 * 128:
                                               kc * 512 + (g4 + 1) * 128],
                             P[:, c * 256:(c + 1) * 256],
                             start=(c == 0), stop=(c == 9))
                    ins = t.matmul(psm[0:1, 0:256], valida[:, kc:kc + 1],
                                   P[:, c * 256:(c + 1) * 256],
                                   start=(c == 0), stop=(c == 9))
                s_pv.inc(ins)

            block.tensor(mmpv)
            pvn = s_pv.n
            if _os.environ.get('DEV_ATT_OPS', 'full') == 'pv':
                continue

            def cps(s, pvn=pvn):
                s.wait_ge(s_pv.h, pvn)
                s_sm.inc(s.copy(sum_sb[:], psm[0:1, 0:256]))

            block.scalar(cps)
            smn = s_sm.n

            def rcp(v, smn=smn):
                v.wait_ge(s_sm.h, smn)
                s_rc.inc(v.reciprocal(rec_sb[:], sum_sb[:]))

            block.vector(rcp)
            rcn = s_rc.n

            def bca(t, rcn=rcn, rbgate=s_rb.n):
                t.wait_ge(s_rc.h, rcn)
                if rbgate:
                    t.wait_ge(s_rb.h, rbgate)
                s_bc.inc(t.matmul(prb[:, 0:256], oner[:], rec_sb[:],
                                  start=True, stop=True))

            block.tensor(bca)
            bcn = s_bc.n

            def evat(v, bcn=bcn, jb=jb, h=h):
                v.wait_ge(s_bc.h, bcn)
                s_rb.inc(v.tensor_copy(recB[:], prb[:, 0:256]))
                v.wait_ge(s_rb.h, s_rb.n)
                s_ev.inc(v.tensor_mul(
                    attT[:, h * 2048 + jb * 256: h * 2048 + (jb + 1) * 256],
                    pat[:, 0:256], recB[:]))

            block.vector(evat)
        qsl_free[jb % 2] = s_sc.n

    ATT_ALL = s_ev.n
    st1.close()
    if _os.environ.get('DEV_STOP', 'full') == 'att':
        stA.close()
        ops = _os.environ.get('DEV_ATT_OPS', 'full')
        sf = {'sc': s_ex, 'msk': s_mk, 'pv': s_pv, 'full': s_ev}[ops]
        FN = sf.n

        class _O:
            def barrier(self, eng):
                eng.wait_ge(sf.h, FN)
        return _O()

    st2 = contextlib.ExitStack()
    wo_sb = C.sbuf("a1wo", [128, 16 * 2048], BF16, st2)
    ysb = [C.sbuf(f"a1y{i}", [128, 2048], F32, st2) for i in range(2)]
    pyb = [C.psum(f"a1py{i}", [128, 512], st2) for i in range(2)]

    def ldwo(g):
        g.wait_ge(s_ev.h, ATT_ALL)
        for mc in range(4):
            s_wo.inc(g.dma_start(
                out=wo_sb[:, mc * 8192:(mc + 1) * 8192],
                in_=wo_a[:, mc * 8192:(mc + 1) * 8192]), 16)

    block.gpsimd(ldwo)

    res_at = {}
    for tc in range(16):
        for fb in range(4):
            i = 4 * tc + fb

            def mmy(t, tc=tc, fb=fb, i=i, yegate=s_yev.n):
                if i == 0:
                    t.wait_ge(s_wo.h, 64)
                if yegate >= 2:
                    t.wait_ge(s_yev.h, yegate - 1)
                wo3 = wo_sb[:].rearrange("p (m f) -> p m f", m=16)
                pb = pyb[i % 2]
                t.matmul(pb[:], onerb[:], bo[:, fb * 512:(fb + 1) * 512],
                         start=True, stop=False)
                for mc in range(16):
                    ins = t.matmul(
                        pb[:],
                        attT[:, mc * 2048 + tc * 128: mc * 2048 + (tc + 1) * 128],
                        wo3[:, mc, fb * 512:(fb + 1) * 512],
                        start=False, stop=(mc == 15))
                s_ymm.inc(ins)

            block.tensor(mmy)
            ymn = s_ymm.n

            def evy(v, tc=tc, fb=fb, i=i, ymn=ymn,
                    rgate=res_at.get(tc - 2, 0)):
                v.wait_ge(s_ymm.h, ymn)
                if fb == 0 and rgate:
                    v.wait_ge(s_res[tc % 2].h, rgate)
                s_yev.inc(v.tensor_copy(
                    ysb[tc % 2][:, fb * 512:(fb + 1) * 512], pyb[i % 2][:]))

            block.vector(evy)

        yn = s_yev.n

        def acc(g, tc=tc, yn=yn):
            g.wait_ge(s_yev.h, yn)
            s_res[tc % 2].inc(g.dma_start(
                out=out_x[tc * 128:(tc + 1) * 128, :],
                in_=ysb[tc % 2][:], accum_op=ALU.add), 16)

        block.gpsimd(acc)
        res_at[tc] = s_res[tc % 2].n

    st2.close()
    stA.close()
    R0, R1 = s_res[0].n, s_res[1].n

    class ResOut:
        def barrier(self, eng):
            eng.wait_ge(s_res[0].h, R0)
            eng.wait_ge(s_res[1].h, R1)

    return ResOut()




def emit_moe(C, io, tag, prev):
    block = C.block
    out_x, xn_dram = io["out_x"], io["xn_dram"]
    gate_wT, w1T, w3T, w2T = io["gate_wT"], io["w1T"], io["w3T"], io["w2T"]
    seg_x, seg_meta, y01 = io["seg_x"], io["seg_meta"], io["y01"]
    oner, onec = io["oner"], io["onec"]
    ltri, tokf, base_r, mfill = io["ltri"], io["tokf"], io["base_r"], io["mfill"]

    st = contextlib.ExitStack()
    gw = C.sbuf(f"{tag}gw", [128, 16 * 8], BF16, st)
    xnc = [C.sbuf(f"{tag}xnc{i}", [128, 16 * 128], BF16, st) for i in range(2)]
    xsc = [C.sbuf(f"{tag}xsc{i}", [128, D], BF16, st) for i in range(2)]
    gsc = C.sbuf(f"{tag}gsc", [128, 8], F32, st)
    gex = C.sbuf(f"{tag}gex", [128, 8], F32, st)
    gst = C.sbuf(f"{tag}gst", [128, 8], F32, st)
    probs = C.sbuf(f"{tag}pr", [128, 8], F32, st)
    s0t = C.sbuf(f"{tag}s0", [128, 8], F32, st)
    rt = C.sbuf(f"{tag}rt", [128, 8], F32, st)
    s1t = C.sbuf(f"{tag}s1", [128, 8], F32, st)
    rb16 = C.sbuf(f"{tag}rb", [128, 8], BF16, st)
    lc = C.sbuf(f"{tag}lc", [128, 8], F32, st)
    obB = C.sbuf(f"{tag}obB", [128, 8], F32, st)
    dst = C.sbuf(f"{tag}dst", [128, 8], F32, st)
    toti = C.sbuf(f"{tag}toti", [1, 16 * 8], F32, st)
    offs = C.sbuf(f"{tag}offs", [1, 16 * 8], F32, st)
    obk = C.sbuf(f"{tag}obk", [1, 8], F32, st)
    sm1 = C.sbuf(f"{tag}sm1", [128, 1], F32, st)
    sm2 = C.sbuf(f"{tag}sm2", [128, 1], F32, st)
    nmx = C.sbuf(f"{tag}nmx", [128, 1], F32, st)
    rs = C.sbuf(f"{tag}rs", [128, 1], F32, st)
    rc1 = C.sbuf(f"{tag}rc1", [128, 1], F32, st)
    di0 = [C.sbuf(f"{tag}dia{i}", [128, 1], I32, st) for i in range(2)]
    di1 = [C.sbuf(f"{tag}dib{i}", [128, 1], I32, st) for i in range(2)]
    dd = C.sbuf(f"{tag}dd", [128, 2], F32, st)
    mv0 = [C.sbuf(f"{tag}mva{i}", [128, 2], F32, st) for i in range(2)]
    mv1 = [C.sbuf(f"{tag}mvb{i}", [128, 2], F32, st) for i in range(2)]
    mfl = C.sbuf(f"{tag}mfl", [128, 256], F32, st)

    pg = C.psum(f"{tag}pg", [128, 512], st)
    pl = C.psum(f"{tag}pl", [128, 512], st)
    pt = C.psum(f"{tag}pt", [128, 512], st)
    pbk = C.psum(f"{tag}pbk", [128, 512], st)

    lnstack = contextlib.ExitStack()
    ln = emit_ln(C, f"{tag}ln", NCH,
                 lambda k: out_x[k * 128:(k + 1) * 128, :],
                 xn_dram, HALO, [], stack=lnstack, prev=prev)
    lnstack.close()

    s_w = C.sem(f"{tag}_w")
    s_mi = C.sem(f"{tag}_mi")
    s_tl = [C.sem(f"{tag}_tl{i}") for i in range(2)]
    s_xl = [C.sem(f"{tag}_xl{i}") for i in range(2)]
    s_mm = C.sem(f"{tag}_mm")
    s_a = C.sem(f"{tag}_a")
    s_v = C.sem(f"{tag}_v")
    s_sg = [C.sem(f"{tag}_sg{i}") for i in range(2)]

    def ldg(g):
        prev.barrier(g)
        s_w.inc(g.dma_start(out=gw[:], in_=gate_wT), 16)
        s_w.inc(g.dma_start(out=mfl[:], in_=mfill), 16)

    block.gpsimd(ldg)

    def mini(sy):
        sy.wait_ge(s_w.h, 32)
        s_mi.inc(sy.dma_start(
            out=seg_meta.rearrange("(n p) c -> p n c", p=128),
            in_=mfl[:].rearrange("p (n c) -> p n c", c=2)), 16)

    block.sync(mini)

    small_free = [0, 0]
    xnc_free = [0, 0]

    for k in range(NCH):
        bi = k % 2

        def tld(sy, k=k, bi=bi, gate=xnc_free[bi], sgate=small_free[bi]):
            ln.chunk_barrier(sy, k)
            if gate:
                sy.wait_ge(s_mm.h, gate)
            if sgate:
                sy.wait_ge(s_sg[bi].h, sgate)
            s_tl[bi].inc(sy.dma_start(
                out=xnc[bi][:].rearrange("p (dc r) -> p dc r", dc=16),
                in_=xn_dram[HALO + k * 128: HALO + (k + 1) * 128, :]
                    .rearrange("r (dc p) -> r dc p", p=128),
                transpose=True), 16)
            s_xl[bi].inc(sy.dma_start(
                out=xsc[bi][:],
                in_=xn_dram[HALO + k * 128: HALO + (k + 1) * 128, :]), 16)

        block.sync(tld)
        tln = s_tl[bi].n

        def gmm(t, k=k, bi=bi, tln=tln, an=s_a.n, vn=s_v.n):
            t.wait_ge(s_tl[bi].h, tln)
            if k == 0:
                t.wait_ge(s_w.h, 32)
            if an:
                t.wait_ge(s_a.h, an)
            x3 = xnc[bi][:].rearrange("p (dc r) -> p dc r", dc=16)
            g3 = gw[:].rearrange("p (dc e) -> p dc e", dc=16)
            for dc in range(16):
                ins = t.matmul(pg[0:128, 0:8], x3[:, dc, :], g3[:, dc, :],
                               start=(dc == 0), stop=(dc == 15))
            s_mm.inc(ins)

        block.tensor(gmm)
        gmn = s_mm.n
        xnc_free[bi] = gmn

        def gev(s, gmn=gmn):
            s.wait_ge(s_mm.h, gmn)
            s_a.inc(s.copy(gsc[:], pg[0:128, 0:8]))

        block.scalar(gev)
        gan = s_a.n

        def rt1(v, gan=gan, vprior=s_v.n):
            v.wait_ge(s_v.h, vprior)
            v.wait_ge(s_a.h, gan)
            s_v.inc(v.reduce_max(sm1[:], gsc[:], AX.X))
            v.wait_ge(s_v.h, s_v.n)
            s_v.inc(v.tensor_scalar(nmx[:], sm1[:], -1.0, None, ALU.mult))

        block.vector(rt1)
        vn1 = s_v.n

        def rt2(s, vn1=vn1):
            s.wait_ge(s_v.h, vn1)
            s_a.inc(s.activation(gex[:], gsc[:], AF.Exp, bias=nmx[:]))

        block.scalar(rt2)
        an2 = s_a.n

        def rt3(v, an2=an2, k=k, vprior=s_v.n):
            v.wait_ge(s_v.h, vprior)
            v.wait_ge(s_a.h, an2)
            s_v.inc(v.tensor_reduce(rs[:], gex[:], AX.X, ALU.add))
            v.wait_ge(s_v.h, s_v.n)
            s_v.inc(v.reciprocal(rc1[:], rs[:]))
            v.wait_ge(s_v.h, s_v.n)
            s_v.inc(v.tensor_scalar(probs[:], gex[:], rc1[:], None, ALU.mult))
            v.wait_ge(s_v.h, s_v.n)
            s_v.inc(v.reduce_max(sm1[:], probs[:], AX.X))
            v.wait_ge(s_v.h, s_v.n)
            s_v.inc(v.tensor_scalar(s0t[:], probs[:], sm1[:], None, ALU.is_ge))
            v.wait_ge(s_v.h, s_v.n)
            s_v.inc(v.scalar_tensor_tensor(gst[:], s0t[:], -1.0e30, probs[:],
                                           ALU.mult, ALU.add))
            v.wait_ge(s_v.h, s_v.n)
            s_v.inc(v.reduce_max(sm2[:], gst[:], AX.X))
            v.wait_ge(s_v.h, s_v.n)
            s_v.inc(v.tensor_scalar(rt[:], probs[:], sm2[:], None, ALU.is_ge))
            v.wait_ge(s_v.h, s_v.n)
            s_v.inc(v.tensor_sub(s1t[:], rt[:], s0t[:]))
            s_v.inc(v.tensor_copy(rb16[:], rt[:]))

        block.vector(rt3)
        vn3 = s_v.n

        def lmm(t, vn3=vn3, an=s_a.n):
            t.wait_ge(s_v.h, vn3)
            if an > 2:
                t.wait_ge(s_a.h, an)
            s_mm.inc(t.matmul(pl[0:128, 0:8], ltri[:], rb16[:],
                              start=True, stop=True))
            s_mm.inc(t.matmul(pt[0:1, 0:8], onec[:], rb16[:],
                              start=True, stop=True))

        block.tensor(lmm)
        lmn = s_mm.n

        def lev(s, lmn=lmn, k=k):
            s.wait_ge(s_mm.h, lmn)
            s_a.inc(s.copy(lc[:], pl[0:128, 0:8]))
            s_a.inc(s.copy(toti[0:1, k * 8:(k + 1) * 8], pt[0:1, 0:8]))

        block.scalar(lev)
        lan = s_a.n

        def pfx(v, lan=lan, k=k, vprior=s_v.n):
            v.wait_ge(s_v.h, vprior)
            v.wait_ge(s_a.h, lan)
            if k == 0:
                s_v.inc(v.memset(offs[0:1, 0:8], 0.0))
            else:
                s_v.inc(v.tensor_add(offs[0:1, k * 8:(k + 1) * 8],
                                     offs[0:1, (k - 1) * 8:k * 8],
                                     toti[0:1, (k - 1) * 8:k * 8]))
            v.wait_ge(s_v.h, s_v.n)
            s_v.inc(v.tensor_add(obk[:], offs[0:1, k * 8:(k + 1) * 8],
                                 base_r[:]))

        block.vector(pfx)
        pfn = s_v.n

        def bmm(t, pfn=pfn, vn=s_v.n):
            t.wait_ge(s_v.h, pfn)
            s_mm.inc(t.matmul(pbk[0:128, 0:8], oner[:], obk[:],
                              start=True, stop=True))

        block.tensor(bmm)
        bmn = s_mm.n

        def dcmp(v, bmn=bmn, k=k, bi=bi, gate=small_free[bi], vprior=s_v.n):
            v.wait_ge(s_v.h, vprior)
            v.wait_ge(s_mm.h, bmn)
            s_v.inc(v.tensor_copy(obB[:], pbk[0:128, 0:8]))
            v.wait_ge(s_v.h, s_v.n)
            s_v.inc(v.tensor_add(dst[:], lc[:], obB[:]))
            v.wait_ge(s_v.h, s_v.n)
            s_v.inc(v.tensor_scalar(gst[:], dst[:], 1.0e5, None, ALU.subtract))
            v.wait_ge(s_v.h, s_v.n)
            s_v.inc(v.tensor_mul(gst[:], gst[:], rt[:]))
            v.wait_ge(s_v.h, s_v.n)
            s_v.inc(v.tensor_scalar(dst[:], gst[:], 1.0e5, None, ALU.add))
            v.wait_ge(s_v.h, s_v.n)
            if gate:
                v.wait_ge(s_sg[bi].h, gate)
            s_v.inc(v.tensor_mul(gst[:], s0t[:], dst[:]))
            v.wait_ge(s_v.h, s_v.n)
            s_v.inc(v.tensor_reduce(dd[:, 0:1], gst[:], AX.X, ALU.add))
            v.wait_ge(s_v.h, s_v.n)
            s_v.inc(v.tensor_mul(gst[:], s1t[:], dst[:]))
            v.wait_ge(s_v.h, s_v.n)
            s_v.inc(v.tensor_reduce(dd[:, 1:2], gst[:], AX.X, ALU.add))
            v.wait_ge(s_v.h, s_v.n)
            s_v.inc(v.tensor_copy(di0[bi][:], dd[:, 0:1]))
            s_v.inc(v.tensor_copy(di1[bi][:], dd[:, 1:2]))
            s_v.inc(v.tensor_scalar(mv0[bi][:, 0:1], tokf[:, k:k + 1],
                                    0.0, None, ALU.add))
            s_v.inc(v.tensor_scalar(mv1[bi][:, 0:1], tokf[:, k:k + 1],
                                    float(TOK), None, ALU.add))
            s_v.inc(v.tensor_copy(mv0[bi][:, 1:2], sm1[:]))
            s_v.inc(v.tensor_copy(mv1[bi][:, 1:2], sm2[:]))

        block.vector(dcmp)
        dcn = s_v.n

        def sct(g, k=k, bi=bi, dcn=dcn, xln=s_xl[bi].n, min_=s_mi.n):
            g.wait_ge(s_v.h, dcn)
            g.wait_ge(s_xl[bi].h, xln)
            if k < 2:
                g.wait_ge(s_mi.h, min_)
                if tag == "m2":
                    g.wait_ge(io["s_zi"].h, io["zi_n"])
            s_sg[bi].inc(g.indirect_dma_start(
                seg_x, bass.IndirectOffsetOnAxis(di0[bi][:], 0),
                xsc[bi][:], None,
                bounds_check=io['bnd_seg'], oob_is_err=False), 16)
            s_sg[bi].inc(g.indirect_dma_start(
                seg_x, bass.IndirectOffsetOnAxis(di1[bi][:], 0),
                xsc[bi][:], None,
                bounds_check=io['bnd_seg'], oob_is_err=False), 16)
            s_sg[bi].inc(g.indirect_dma_start(
                seg_meta, bass.IndirectOffsetOnAxis(di0[bi][:], 0),
                mv0[bi][:], None,
                bounds_check=io['bnd_seg'], oob_is_err=False), 16)

        block.gpsimd(sct)

        def sct2(g, bi=bi):
            s_sg[bi].inc(g.indirect_dma_start(
                seg_meta, bass.IndirectOffsetOnAxis(di1[bi][:], 0),
                mv1[bi][:], None,
                bounds_check=io['bnd_seg'], oob_is_err=False), 16)

        block.gpsimd(sct2)
        small_free[bi] = s_sg[bi].n

    SG = [s_sg[0].n, s_sg[1].n]
    st.close()

    stE = contextlib.ExitStack()
    hT = C.sbuf(f"{tag}hT", [128, 16 * CAP], BF16, stE)
    sil = C.sbuf(f"{tag}sil", [128, 512], BF16, stE)
    s_xg = C.sem(f"{tag}_xg")
    s_wc = [C.sem(f"{tag}_wc{i}") for i in range(2)]
    s_w2 = C.sem(f"{tag}_w2")
    s_he = C.sem(f"{tag}_he")
    s_hs = C.sem(f"{tag}_hs")
    s_ml = [C.sem(f"{tag}_ml{i}") for i in range(2)]
    s_ye = C.sem(f"{tag}_ye")
    s_ysc = [C.sem(f"{tag}_ysc{i}") for i in range(2)]

    ph1 = C.psum(f"{tag}ph1", [128, 512], stE)
    ph3 = C.psum(f"{tag}ph3", [128, 512], stE)
    py = [C.psum(f"{tag}py{i}", [128, 512], stE) for i in range(2)]

    for e in range(NE):
        stX = contextlib.ExitStack()
        xgT = C.sbuf(f"{tag}xg{e}", [128, 16 * CAP], BF16, stX)
        w1cb = [C.sbuf(f"{tag}w1c{e}{i}", [128, 16 * 128], BF16, stX)
                for i in range(2)]
        w3cb = [C.sbuf(f"{tag}w3c{e}{i}", [128, 16 * 128], BF16, stX)
                for i in range(2)]

        def xgl(sy, e=e, hen=s_he.n, mmn=s_mm.n,
                y0=s_ysc[0].n, y1=s_ysc[1].n):
            if e == 0:
                sy.wait_ge(s_sg[0].h, SG[0])
                sy.wait_ge(s_sg[1].h, SG[1])
            if hen:
                sy.wait_ge(s_he.h, hen)
            if e > 0:
                sy.wait_ge(s_mm.h, mmn)
                sy.wait_ge(s_ysc[0].h, y0)
                sy.wait_ge(s_ysc[1].h, y1)
            s_xg.inc(sy.dma_start(
                out=xgT[:].rearrange("p (dc r) -> p dc r", dc=16),
                in_=seg_x[e * CAP:(e + 1) * CAP, :]
                    .rearrange("r (dc p) -> r dc p", p=128),
                transpose=True), 16)

        block.sync(xgl)
        xgn = s_xg.n

        for fc in range(16):
            wb = fc % 2

            def wld(g, e=e, fc=fc, wb=wb, gate=s_mm.n,
                    y0=s_ysc[0].n, y1=s_ysc[1].n):
                if fc >= 2:
                    g.wait_ge(s_mm.h, gate)
                if e > 0 and fc < 2:
                    g.wait_ge(s_ysc[0].h, y0)
                    g.wait_ge(s_ysc[1].h, y1)
                s_wc[wb].inc(g.dma_start(out=w1cb[wb][:],
                                         in_=w1T[e, fc]), 16)
                s_wc[wb].inc(g.dma_start(out=w3cb[wb][:],
                                         in_=w3T[e, fc]), 16)

            block.gpsimd(wld)
            wcn = s_wc[wb].n

            for rc in range(CAP // 512):
                def hmm(t, fc=fc, rc=rc, wb=wb, wcn=wcn, xgn=xgn,
                        heg=s_he.n, hsg=s_hs.n):
                    t.wait_ge(s_wc[wb].h, wcn)
                    if fc == 0 and rc == 0:
                        t.wait_ge(s_xg.h, xgn)
                    if hsg:
                        t.wait_ge(s_hs.h, hsg)
                    if heg:
                        t.wait_ge(s_he.h, heg)
                    w13 = w1cb[wb][:].rearrange("p (dc f) -> p dc f", dc=16)
                    w33 = w3cb[wb][:].rearrange("p (dc f) -> p dc f", dc=16)
                    x3 = xgT[:].rearrange("p (dc r) -> p dc r", dc=16)
                    for dc in range(16):
                        t.matmul(ph1[:], w13[:, dc, :],
                                 x3[:, dc, rc * 512:(rc + 1) * 512],
                                 start=(dc == 0), stop=(dc == 15))
                    for dc in range(16):
                        ins = t.matmul(ph3[:], w33[:, dc, :],
                                       x3[:, dc, rc * 512:(rc + 1) * 512],
                                       start=(dc == 0), stop=(dc == 15))
                    s_mm.inc(ins)

                block.tensor(hmm)
                hmn = s_mm.n

                def hsl(s, hmn=hmn):
                    s.wait_ge(s_mm.h, hmn)
                    s_hs.inc(s.activation(sil[:], ph1[:], AF.Silu))

                block.scalar(hsl)
                hsn = s_hs.n

                def hev(v, hsn=hsn, fc=fc, rc=rc):
                    v.wait_ge(s_hs.h, hsn)
                    s_he.inc(v.tensor_mul(
                        hT[:, fc * CAP + rc * 512: fc * CAP + (rc + 1) * 512],
                        ph3[:], sil[:]))

                block.vector(hev)

        HDONE = s_he.n
        stX.close()
        stW2 = contextlib.ExitStack()
        w2sb = C.sbuf(f"{tag}w2{e}", [128, 16 * D], BF16, stW2)
        ysb = [C.sbuf(f"{tag}ys{e}{i}", [128, D], F32, stW2) for i in range(2)]
        mrow = [C.sbuf(f"{tag}mr{e}{i}", [128, 2], F32, stW2) for i in range(2)]
        dix = [C.sbuf(f"{tag}dx{e}{i}", [128, 1], I32, stW2) for i in range(2)]

        def w2l(g, e=e, hd=HDONE):
            g.wait_ge(s_he.h, hd)
            for sl in range(4):
                s_w2.inc(g.dma_start(
                    out=w2sb[:, sl * 8192:(sl + 1) * 8192],
                    in_=w2T[e][:, sl * 8192:(sl + 1) * 8192]), 16)

        block.gpsimd(w2l)
        w2n = s_w2.n

        for rc in range(CAP // 128):
            bi = rc % 2

            def mld(sy, e=e, rc=rc, bi=bi, gate=s_ysc[bi].n, min_=s_mi.n):
                if gate:
                    sy.wait_ge(s_ysc[bi].h, gate)
                sy.wait_ge(s_mi.h, min_)
                s_ml[bi].inc(sy.dma_start(
                    out=mrow[bi][:],
                    in_=seg_meta[e * CAP + rc * 128: e * CAP + (rc + 1) * 128, :]),
                    16)

            block.sync(mld)
            mln = s_ml[bi].n

            def mcv(v, bi=bi, mln=mln, vprior=s_v.n):
                v.wait_ge(s_v.h, vprior)
                v.wait_ge(s_ml[bi].h, mln)
                s_v.inc(v.tensor_copy(dix[bi][:], mrow[bi][:, 0:1]))

            block.vector(mcv)
            mcn = s_v.n

            for db in range(4):
                i = 4 * rc + db

                def ymm(t, rc=rc, db=db, i=i, w2n=w2n, yeg=s_ye.n):
                    if i == 0:
                        t.wait_ge(s_w2.h, w2n)
                    if yeg >= 2:
                        t.wait_ge(s_ye.h, yeg - 1)
                    w23 = w2sb[:].rearrange("p (Fc d) -> p Fc d", Fc=16)
                    h3 = hT[:].rearrange("p (Fc r) -> p Fc r", Fc=16)
                    for Fc in range(16):
                        ins = t.matmul(py[i % 2][:],
                                       h3[:, Fc, rc * 128:(rc + 1) * 128],
                                       w23[:, Fc, db * 512:(db + 1) * 512],
                                       start=(Fc == 0), stop=(Fc == 15))
                    s_mm.inc(ins)

                block.tensor(ymm)
                ymn = s_mm.n

                def yev(v, bi=bi, db=db, i=i, ymn=ymn, mcn=mcn):
                    v.wait_ge(s_mm.h, ymn)
                    v.wait_ge(s_v.h, mcn)
                    s_ye.inc(v.tensor_scalar(
                        ysb[bi][:, db * 512:(db + 1) * 512], py[i % 2][:],
                        mrow[bi][:, 1:2], None, ALU.mult))

                block.vector(yev)

            yen = s_ye.n

            def ysct(g, bi=bi, yen=yen):
                g.wait_ge(s_ye.h, yen)
                s_ysc[bi].inc(g.indirect_dma_start(
                    y01, bass.IndirectOffsetOnAxis(dix[bi][:], 0),
                    ysb[bi][:], None,
                    bounds_check=io['bnd_y'], oob_is_err=False), 16)

            block.gpsimd(ysct)

        stW2.close()

    YS = [s_ysc[0].n, s_ysc[1].n]
    stE.close()

    stR = contextlib.ExitStack()
    y0b = [C.sbuf(f"{tag}y0{i}", [128, D], F32, stR) for i in range(2)]
    y1b = [C.sbuf(f"{tag}y1{i}", [128, D], F32, stR) for i in range(2)]
    s_yl = [C.sem(f"{tag}_yl{i}") for i in range(2)]
    s_ra = C.sem(f"{tag}_ra")
    s_res = [C.sem(f"{tag}_res{i}") for i in range(2)]
    res_at = [0, 0]

    for k in range(NCH):
        bi = k % 2

        def yld(sy, k=k, bi=bi, gate=res_at[bi]):
            if k == 0:
                sy.wait_ge(s_ysc[0].h, YS[0])
                sy.wait_ge(s_ysc[1].h, YS[1])
            if gate:
                sy.wait_ge(s_res[bi].h, gate)
            s_yl[bi].inc(sy.dma_start(out=y0b[bi][:],
                                      in_=y01[k * 128:(k + 1) * 128, :]), 16)
            s_yl[bi].inc(sy.dma_start(
                out=y1b[bi][:], in_=y01[TOK + k * 128: TOK + (k + 1) * 128, :]),
                16)

        block.sync(yld)
        yln = s_yl[bi].n

        def radd(v, bi=bi, yln=yln):
            v.wait_ge(s_yl[bi].h, yln)
            s_ra.inc(v.tensor_add(y0b[bi][:], y0b[bi][:], y1b[bi][:]))

        block.vector(radd)
        ran = s_ra.n

        def racc(g, k=k, bi=bi, ran=ran):
            g.wait_ge(s_ra.h, ran)
            s_res[bi].inc(g.dma_start(
                out=out_x[k * 128:(k + 1) * 128, :],
                in_=y0b[bi][:], accum_op=ALU.add), 16)

        block.gpsimd(racc)
        res_at[bi] = s_res[bi].n

    stR.close()
    R = [s_res[0].n, s_res[1].n]

    class ResOut:
        def barrier(self, eng):
            eng.wait_ge(s_res[0].h, R[0])
            eng.wait_ge(s_res[1].h, R[1])

    return ResOut()




def emit_attn_b(C, io, prev):
    block = C.block
    out_x, xn_dram, q_dram = io["out_x"], io["xn_dram"], io["q_dram"]
    wq_b, wkv_b, wo_b = io["wq_b"], io["wkv_b"], io["wo_b"]
    kv_contrib, kv_all, kv_win = io["kv_contrib"], io["kv_all"], io["kv_win"]
    oner, onec, onerb = io["oner"], io["onec"], io["onerb"]
    bo = io["bo_b_sb"]

    stP = contextlib.ExitStack()
    comp = C.sbuf("b3comp", [128, NCH], F32, stP)
    ropes = C.sbuf("b3rsin", [128, 16 * 64], F32, stP)
    ropec = C.sbuf("b3rcos", [128, 16 * 64], F32, stP)
    wrow = C.sbuf("b3w", [128, 16 * 2048], BF16, stP)
    xnc = [C.sbuf(f"b3xnc{i}", [128, 16 * 128], BF16, stP) for i in range(2)]
    qf = C.sbuf("b3qf", [128, 2048], F32, stP)
    t1 = C.sbuf("b3t1", [128, 64], F32, stP)
    t2 = C.sbuf("b3t2", [128, 64], F32, stP)
    orow = [C.sbuf(f"b3or{i}", [128, 2048], BF16, stP) for i in range(2)]

    pq = [C.psum("b3pq%d" % i, [128, 512], stP) for i in range(2)]

    s_rl = C.sem("b3_rl")
    s_wl = C.sem("b3_wl")
    s_tl = [C.sem(f"b3_tl{i}") for i in range(2)]
    s_mm = C.sem("b3_mm")
    s_a = C.sem("b3_a")
    s_v = C.sem("b3_v")
    s_p = C.sem("b3_p")
    s_st = [C.sem(f"b3_st{i}") for i in range(2)]
    s_cc = C.sem("b3_cc")
    s_wg = C.sem("b3_wg")
    s_ws = C.sem("b3_ws")

    lnstack = contextlib.ExitStack()
    ln = emit_ln(C, "b3ln", NCH,
                 lambda k: out_x[k * 128:(k + 1) * 128, :],
                 xn_dram, HALO, [], comp_out=comp, stack=lnstack, prev=prev)
    lnstack.close()

    def ldr(sy):
        s_rl.inc(sy.dma_start(out=ropes[:], in_=io["rope_sin"]), 16)
        s_rl.inc(sy.dma_start(out=ropec[:], in_=io["rope_cos"]), 16)

    block.sync(ldr)

    orow_st = [0, 0]
    for pas in range(2):
        def ldwb(g, pas=pas, mmn=s_mm.n):
            if pas == 0:
                ln.barrier(g)
            else:
                g.wait_ge(s_mm.h, mmn)
            s_wl.inc(g.dma_start(out=wrow[:],
                                 in_=(wq_b if pas == 0 else wkv_b)), 16)

        block.gpsimd(ldwb)
        wn = s_wl.n

        for k in range(NCH):
            bi = k % 2

            def tldb(sy, k=k, bi=bi, pas=pas, gate=s_mm.n):
                if k >= 2 or pas == 1:
                    sy.wait_ge(s_mm.h, gate)
                ln.chunk_barrier(sy, k)
                s_tl[bi].inc(sy.dma_start(
                    out=xnc[bi][:].rearrange("p (dc r) -> p dc r", dc=16),
                    in_=xn_dram[HALO + k * 128: HALO + (k + 1) * 128, :]
                        .rearrange("r (dc p) -> r dc p", p=128),
                    transpose=True), 16)

            block.sync(tldb)
            tln = s_tl[bi].n

            ev_at = [0, 0]
            for fb in range(4):
                def qmmf(t, k=k, bi=bi, fb=fb, tln=tln, wn=wn,
                         gate=ev_at[fb % 2]):
                    if fb == 0:
                        t.wait_ge(s_tl[bi].h, tln)
                        t.wait_ge(s_wl.h, wn)
                    if gate:
                        t.wait_ge(s_v.h, gate)
                    x3 = xnc[bi][:].rearrange("p (dc r) -> p dc r", dc=16)
                    w3 = wrow[:].rearrange("p (dc f) -> p dc f", dc=16)
                    for dc in range(16):
                        ins = t.matmul(pq[fb % 2][:], x3[:, dc, :],
                                       w3[:, dc, fb * 512:(fb + 1) * 512],
                                       start=(dc == 0), stop=(dc == 15))
                    s_mm.inc(ins)

                block.tensor(qmmf)
                mmn = s_mm.n

                if pas == 0:
                    def qevf(v, k=k, fb=fb, mmn=mmn, vprior=s_v.n):
                        v.wait_ge(s_v.h, vprior)
                        v.wait_ge(s_mm.h, mmn)
                        s_v.inc(v.tensor_scalar(
                            qf[:, fb * 512:(fb + 1) * 512], pq[fb % 2][:],
                            comp[:, k:k + 1], None, ALU.mult))
                else:
                    def qevf(v, k=k, fb=fb, mmn=mmn, vprior=s_v.n,
                             og=orow_st[k % 2]):
                        v.wait_ge(s_v.h, vprior)
                        v.wait_ge(s_mm.h, mmn)
                        if fb < 2:
                            s_v.inc(v.tensor_copy(
                                qf[:, fb * 512:(fb + 1) * 512], pq[fb % 2][:]))
                        else:
                            if fb == 2 and og:
                                v.wait_ge(s_st[k % 2].h, og)
                            s_v.inc(v.tensor_copy(
                                orow[k % 2][:, fb * 512:(fb + 1) * 512],
                                pq[fb % 2][:]))

                block.vector(qevf)
                ev_at[fb % 2] = s_v.n

            if pas == 0:
                def qev(v, k=k, vprior=s_v.n, og=orow_st[k % 2]):
                    v.wait_ge(s_v.h, vprior)
                    v.wait_ge(s_rl.h, 32)
                    if og:
                        v.wait_ge(s_st[k % 2].h, og)
                    for h in range(16):
                        q1 = qf[:, h * 128:h * 128 + 64]
                        q2 = qf[:, h * 128 + 64:h * 128 + 128]
                        ck = ropec[:, k * 64:(k + 1) * 64]
                        sk = ropes[:, k * 64:(k + 1) * 64]
                        v.wait_ge(s_v.h, s_v.n)
                        s_v.inc(v.tensor_mul(t1[:], q1, ck))
                        s_v.inc(v.tensor_mul(t2[:], q2, sk))
                        v.wait_ge(s_v.h, s_v.n)
                        s_v.inc(v.tensor_sub(
                            orow[k % 2][:, h * 128:h * 128 + 64], t1[:], t2[:]))
                        v.wait_ge(s_v.h, s_v.n)
                        s_v.inc(v.tensor_mul(t1[:], q1, sk))
                        s_v.inc(v.tensor_mul(t2[:], q2, ck))
                        v.wait_ge(s_v.h, s_v.n)
                        s_v.inc(v.tensor_add(
                            orow[k % 2][:, h * 128 + 64:h * 128 + 128],
                            t1[:], t2[:]))

                block.vector(qev)
                evn = s_v.n

                def qst(sy, k=k, evn=evn, bi=k % 2, stn=s_st[k % 2].n):
                    sy.wait_ge(s_v.h, evn)
                    s_st[bi].inc(sy.dma_start(
                        out=q_dram[k * 128:(k + 1) * 128, :],
                        in_=orow[bi][:]), 16)

                block.sync(qst)
                orow_st[k % 2] = s_st[k % 2].n
            else:
                def kev(v, k=k, vprior=s_v.n, og=orow_st[k % 2]):
                    v.wait_ge(s_v.h, vprior)
                    v.wait_ge(s_rl.h, 32)
                    if og:
                        v.wait_ge(s_st[k % 2].h, og)
                    for h in range(8):
                        q1 = qf[:, h * 128:h * 128 + 64]
                        q2 = qf[:, h * 128 + 64:h * 128 + 128]
                        ck = ropec[:, k * 64:(k + 1) * 64]
                        sk = ropes[:, k * 64:(k + 1) * 64]
                        v.wait_ge(s_v.h, s_v.n)
                        s_v.inc(v.tensor_mul(t1[:], q1, ck))
                        s_v.inc(v.tensor_mul(t2[:], q2, sk))
                        v.wait_ge(s_v.h, s_v.n)
                        s_v.inc(v.tensor_sub(
                            orow[k % 2][:, h * 128:h * 128 + 64], t1[:], t2[:]))
                        v.wait_ge(s_v.h, s_v.n)
                        s_v.inc(v.tensor_mul(t1[:], q1, sk))
                        s_v.inc(v.tensor_mul(t2[:], q2, ck))
                        v.wait_ge(s_v.h, s_v.n)
                        s_v.inc(v.tensor_add(
                            orow[k % 2][:, h * 128 + 64:h * 128 + 128],
                            t1[:], t2[:]))

                block.vector(kev)
                evn = s_v.n

                def kst(sy, k=k, evn=evn, bi=k % 2):
                    sy.wait_ge(s_v.h, evn)
                    s_st[bi].inc(sy.dma_start(
                        out=kv_contrib[k * 128:(k + 1) * 128, :],
                        in_=orow[bi][:]), 16)

                block.sync(kst)
                orow_st[k % 2] = s_st[k % 2].n

    QKV_ST = [s_st[0].n, s_st[1].n]

    dump = _os2.environ.get("DEV_B_DUMP")
    if dump in ("q", "kv"):
        srcd = q_dram if dump == "q" else kv_contrib
        s_dp = C.sem("b3_dp")

        def dmp(g):
            g.wait_ge(s_st[0].h, QKV_ST[0])
            g.wait_ge(s_st[1].h, QKV_ST[1])
            for r in range(16):
                s_dp.inc(g.dma_start(
                    out=out_x[r * 128:(r + 1) * 128, :],
                    in_=srcd[r * 128:(r + 1) * 128, :]), 16)

        block.gpsimd(dmp)
        DN = s_dp.n

        class _O:
            def barrier(self, eng):
                eng.wait_ge(s_dp.h, DN)
        stP.close()
        return _O()

    gbuf = C.sbuf("b3gb", [128, 2048], BF16, stP)
    widx = C.sbuf("b3wi", [128, (TOK + 640) // 128], I32, stP)
    s_wi = C.sem("b3_wi")

    def ldwi(sy):
        s_wi.inc(sy.dma_start(out=widx[:], in_=io["kvwin_idx"]), 16)

    block.sync(ldwi)

    def coll(g):
        g.wait_ge(s_st[0].h, QKV_ST[0])
        g.wait_ge(s_st[1].h, QKV_ST[1])
        s_cc.inc(g.collective_compute(
            "AllGather", ALU.bypass,
            replica_groups=[[0, 1, 2, 3], [4, 5, 6, 7]],
            ins=[kv_contrib.opt()], outs=[kv_all.opt()]))

    block.gpsimd(coll)

    WINC = (TOK + 640) // 128
    for r in range(WINC):
        def wgat(g, r=r, wsn=s_ws.n):
            if r == 0:
                g.wait_ge(s_cc.h, 1)
                g.wait_ge(s_wi.h, 16)
            else:
                g.wait_ge(s_ws.h, wsn)
            s_wg.inc(g.indirect_dma_start(
                gbuf[:], None, kv_all,
                bass.IndirectOffsetOnAxis(widx[:, r:r + 1], 0)), 16)

        block.gpsimd(wgat)
        wgn = s_wg.n

        def wsto(sy, r=r, wgn=wgn):
            sy.wait_ge(s_wg.h, wgn)
            s_ws.inc(sy.dma_start(
                out=kv_win[r * 128:(r + 1) * 128, :], in_=gbuf[:]), 16)

        block.sync(wsto)

    WS_ALL = s_ws.n

    if _os2.environ.get("DEV_B_DUMP") == "kvw":
        s_dp = C.sem("b3_dp")

        def dmpw(g):
            g.wait_ge(s_ws.h, WS_ALL)
            for r in range(16):
                s_dp.inc(g.dma_start(
                    out=out_x[r * 128:(r + 1) * 128, :],
                    in_=kv_win[r * 128:(r + 1) * 128, :]), 16)

        block.gpsimd(dmpw)
        DNW = s_dp.n

        class _OW:
            def barrier(self, eng):
                eng.wait_ge(s_dp.h, DNW)
        stP.close()
        return _OW()

    if _os2.environ.get("DEV_B_DUMP") == "kva":
        s_dp = C.sem("b3_dp")

        def dmpk(g):
            g.wait_ge(s_ws.h, WS_ALL)
            for r in range(16):
                s_dp.inc(g.dma_start(
                    out=out_x[r * 128:(r + 1) * 128, :],
                    in_=kv_all[TOK + r * 128: TOK + (r + 1) * 128, :]), 16)

        block.gpsimd(dmpk)
        DNK = s_dp.n

        class _OK:
            def barrier(self, eng):
                eng.wait_ge(s_dp.h, DNK)
        stP.close()
        return _OK()
    stP.close()

    att_dram = io["seg_x"]
    stA = contextlib.ExitStack()
    att_st = [C.sbuf(f"b3as{i}", [128, 256], BF16, stA) for i in range(2)]
    kTh = [C.sbuf(f"b3kT{i}", [128, SEQ], BF16, stA) for i in range(2)]
    vth = [C.sbuf(f"b3vt{i}", [128, SEQ], BF16, stA) for i in range(2)]
    kTw = [C.sbuf(f"b3kTw{i}", [128, TOK + 640], BF16, stA) for i in range(2)]
    vtw = [C.sbuf(f"b3vtw{i}", [128, TOK + 640], BF16, stA) for i in range(2)]
    qTt = [C.sbuf(f"b3qT{i}", [128, 256], BF16, stA) for i in range(2)]
    tabs = C.sbuf("b3tab", [3, 68 * 256], F32, stA)
    ramph = C.sbuf("b3ramp", [3, 16 * 128], F32, stA)
    killj = C.sbuf("b3kj", [128, 1], F32, stA)
    mrel = C.sbuf("b3mrel", [128, 4 * 256], F32, stA)
    ub64 = C.sbuf("b3ub", [128, 64], F32, stA)
    mb64 = C.sbuf("b3mb", [128, 64], F32, stA)
    urow = C.sbuf("b3ur", [1, 64], F32, stA)
    usum = C.sbuf("b3us", [1, 64], F32, stA)
    sumv8 = C.sbuf("b3sv", [128, 2], F32, stA)
    P = C.sbuf("b3P", [128, 4 * 256], BF16, stA)
    sum_sb = C.sbuf("b3sum", [1, 256], F32, stA)
    sum2 = C.sbuf("b3sum2", [1, 256], F32, stA)
    rec_sb = C.sbuf("b3rec", [1, 256], F32, stA)
    recB = C.sbuf("b3recB", [128, 256], F32, stA)
    o1t = C.sbuf("b3o1", [128, 64], F32, stA)

    psc = [C.psum(f"b3psc{i}", [128, 512], stA) for i in range(2)]
    pat = C.psum("b3pat", [128, 512], stA)
    psm = C.psum("b3psm", [128, 512], stA)
    prb = C.psum("b3prb", [128, 512], stA)
    psv = C.psum("b3psv", [128, 512], stA)

    s_kl = [C.sem(f"b3_kl{i}") for i in range(2)]
    s_wk = [C.sem(f"b3_wk{i}") for i in range(2)]
    s_ql = [C.sem(f"b3_ql{i}") for i in range(2)]
    s_tb = [C.sem(f"b3_tb{i}") for i in range(2)]
    s_cl = C.sem("b3_cl")
    s_sc = C.sem("b3_sc")
    s_ex = C.sem("b3_ex")
    s_mk = C.sem("b3_mk")
    s_pv = C.sem("b3_pv")
    s_sv = C.sem("b3_sv")
    s_sve = C.sem("b3_sve")
    s_sm = C.sem("b3_sm")
    s_rc = C.sem("b3_rc")
    s_bc = C.sem("b3_bc")
    s_rb = C.sem("b3_rb")
    s_ev = C.sem("b3_ev")

    def ldcb(sy):
        sy.wait_ge(s_ws.h, WS_ALL)
        for dst, srcn in [(ramph[:], "ramp3h"), (killj[:], "kill_j64"),
                          (ub64[:], "uB64"),
                          (mb64[:], "mB64"), (urow[:], "uni_flag"),
                          (usum[:], "uni_sum")]:
            s_cl.inc(sy.dma_start(out=dst, in_=io[srcn]), 16)

    block.sync(ldcb)
    CLN = s_cl.n

    for kvh in range(8):
        bi = kvh % 2

        def klod(sy, kvh=kvh, bi=bi, gate=s_pv.n, svg=s_sv.n):
            if kvh < 2:
                sy.wait_ge(s_cl.h, CLN)
            if gate:
                sy.wait_ge(s_pv.h, gate)
            s_kl[bi].inc(sy.dma_start(
                out=kTh[bi][:].rearrange("p (o j) -> p o j", o=1),
                in_=kv_all[:, kvh * 128:(kvh + 1) * 128]
                    .rearrange("j (o p) -> j o p", o=1),
                transpose=True), 16)
            for q4 in range(4):
                s_kl[bi].inc(sy.dma_start(
                    out=vth[bi][:].rearrange("p (c d) -> p c d", c=64)
                        [:, q4 * 16:(q4 + 1) * 16, :],
                    in_=kv_all[q4 * 2048:(q4 + 1) * 2048,
                               1024 + kvh * 128:1024 + (kvh + 1) * 128]
                        .rearrange("(c p) d -> p c d", p=128)), 16)
            s_wk[bi].inc(sy.dma_start(
                out=kTw[bi][:].rearrange("p (o j) -> p o j", o=1),
                in_=kv_win[:, kvh * 128:(kvh + 1) * 128]
                    .rearrange("j (o p) -> j o p", o=1),
                transpose=True), 16)
            s_wk[bi].inc(sy.dma_start(
                out=vtw[bi][:].rearrange("p (c d) -> p c d", c=WINC),
                in_=kv_win[:, 1024 + kvh * 128:1024 + (kvh + 1) * 128]
                    .rearrange("(c p) d -> p c d", p=128)), 16)

        block.sync(klod)
        kln = s_kl[bi].n
        wkn = s_wk[bi].n

        def svmm(t, kvh=kvh, bi=bi, kln=kln, sveg=s_sve.n):
            t.wait_ge(s_kl[bi].h, kln)
            if sveg:
                t.wait_ge(s_sve.h, sveg)
            v3 = vth[bi][:].rearrange("p (c d) -> p c d", c=64)
            for c in range(64):
                ins = t.matmul(psv[:, 0:1], v3[:, c, :], onec[:],
                               start=(c == 0), stop=(c == 63))
            s_sv.inc(ins)

        block.tensor(svmm)
        svn = s_sv.n

        def svev(v, svn=svn, vprior=s_v.n):
            v.wait_ge(s_v.h, vprior)
            v.wait_ge(s_sv.h, svn)
            s_sve.inc(v.tensor_scalar(sumv8[:, 0:1], psv[:, 0:1],
                                      1.0 / SEQ, None, ALU.mult))

        block.vector(svev)
        sven = s_sve.n

        for hh in range(2):
            h = 2 * kvh + hh
            for jb in range(NQB):
                hj = (h, jb)
                qbi = jb % 2

                def qlod(sy, h=h, jb=jb, qbi=qbi, gate=s_sc.n, mkg=s_mk.n):
                    if gate:
                        sy.wait_ge(s_sc.h, gate)
                    if mkg:
                        sy.wait_ge(s_mk.h, mkg)
                    s_ql[qbi].inc(sy.dma_start(
                        out=qTt[qbi][:].rearrange("p (o j) -> p o j", o=1),
                        in_=q_dram[jb * 256:(jb + 1) * 256,
                                   h * 128:(h + 1) * 128]
                            .rearrange("j (o p) -> j o p", o=1),
                        transpose=True), 16)
                    s_tb[0].inc(sy.dma_start(
                        out=tabs[:].rearrange("r (c t) -> r c t", c=68),
                        in_=io["atab"][jb].rearrange("c r t -> r c t")), 16)
                    s_tb[1].inc(sy.dma_start(
                        out=mrel[:],
                        in_=io["maskrel"][:, jb * 1024:(jb + 1) * 1024]), 16)

                block.sync(qlod)
                qln = s_ql[qbi].n
                tbn = s_tb[0].n

                exp_at = {}
                pv_at = {}
                base_sc = s_sc.n
                base_ex = s_ex.n
                base_pv = s_pv.n
                t3v = None
                for c in range(68):
                    def cqkt(t, h=h, jb=jb, bi=bi, qbi=qbi, c=c, qln=qln,
                             tbn=tbn, kln=kln, wkn=wkn,
                             exg=exp_at.get(c - 2, 0)):
                        if c == 0:
                            t.wait_ge(s_ql[qbi].h, qln)
                            t.wait_ge(s_tb[0].h, tbn)
                            t.wait_ge(s_kl[bi].h, kln)
                            t.wait_ge(s_wk[bi].h, wkn)
                        if exg:
                            t.wait_ge(s_ex.h, exg)
                        t3 = tabs[:].rearrange("r (ct) -> r ct")
                        if c < 64:
                            lhs = kTh[bi][:, c * 128:(c + 1) * 128]
                        else:
                            r = c - 64
                            off = jb * 256 + (128, 256, 640, 768)[r]
                            lhs = kTw[bi][:, off:off + 128]
                        t.matmul(psc[c % 2][:, 0:256], lhs, qTt[qbi][:, 0:256],
                                 start=True, stop=False)
                        ins = t.matmul(psc[c % 2][:, 0:256],
                                       ramph[:, h * 128:(h + 1) * 128],
                                       tabs[:, c * 256:(c + 1) * 256],
                                       start=False, stop=True)
                        s_sc.inc(ins)

                    block.tensor(cqkt)
                    scn = s_sc.n

                    if c >= 64:
                        def cmka(v, jb=jb, c=c, scn=scn, vprior=s_v.n,
                                 mreln=s_tb[1].n):
                            v.wait_ge(s_v.h, vprior)
                            v.wait_ge(s_sc.h, scn)
                            r = c - 64
                            v.wait_ge(s_tb[1].h, mreln)
                            s_mk.inc(v.tensor_add(psc[c % 2][:, 0:256],
                                                  psc[c % 2][:, 0:256],
                                                  mrel[:, r * 256:(r + 1) * 256]))

                        block.vector(cmka)
                    mka = s_mk.n

                    def cexp(s, c=c, scn=scn, mka=mka,
                             pvg=pv_at.get(c - 4, 0), pvp=s_pv.n):
                        s.wait_ge(s_sc.h, scn)
                        if c >= 64:
                            s.wait_ge(s_mk.h, mka)
                        if pvg:
                            s.wait_ge(s_pv.h, pvg)
                        elif c < 4 and pvp:
                            s.wait_ge(s_pv.h, pvp)
                        s_ex.inc(s.activation(P[:, (c % 4) * 256:(c % 4 + 1) * 256],
                                              psc[c % 2][:, 0:256],
                                              AF.Exp, scale=1.0 / SQD))

                    block.scalar(cexp)
                    exp_at[c] = s_ex.n

                    if c == 0:
                        def cmsk(v, c=c, exn=s_ex.n, vprior=s_v.n):
                            v.wait_ge(s_v.h, vprior)
                            v.wait_ge(s_ex.h, exn)
                            sl = (c % 4) * 256
                            s_mk.inc(v.tensor_scalar(
                                P[:, sl:sl + 256], P[:, sl:sl + 256],
                                killj[:], None, ALU.mult))

                        block.vector(cmsk)
                    mkn = s_mk.n

                    def cpv(t, jb=jb, bi=bi, c=c, exn=exp_at[c], mkn=mkn,
                            evg=s_ev.n):
                        if c == 0 and evg:
                            t.wait_ge(s_ev.h, evg)
                        if c == 0:
                            t.wait_ge(s_mk.h, mkn)
                        else:
                            t.wait_ge(s_ex.h, exn)
                        sl = (c % 4) * 256
                        if c < 64:
                            lhs = vth[bi][:].rearrange(
                                "p (cc d) -> p cc d", cc=64)[:, c, :]
                        else:
                            r = c - 64
                            off = jb * 256 + (128, 256, 640, 768)[r]
                            lhs = vtw[bi][:, off:off + 128]
                        t.matmul(pat[:, 0:256], lhs, P[:, sl:sl + 256],
                                 start=(c == 0), stop=(c == 67))
                        ins = t.matmul(psm[0:1, 0:256], onec[:],
                                       P[:, sl:sl + 256],
                                       start=(c == 0), stop=(c == 67))
                        s_pv.inc(ins)

                    block.tensor(cpv)
                    pv_at[c] = s_pv.n

                pvn = s_pv.n

                def bsum(s, pvn=pvn):
                    s.wait_ge(s_pv.h, pvn)
                    s_sm.inc(s.copy(sum_sb[:], psm[0:1, 0:256]))

                block.scalar(bsum)
                smn = s_sm.n

                def brcp(v, jb=jb, smn=smn, sven=sven, vprior=s_v.n):
                    v.wait_ge(s_v.h, vprior)
                    v.wait_ge(s_sm.h, smn)
                    if jb == 0:
                        s_v.inc(v.tensor_mul(sum2[:, 0:64], sum_sb[:, 0:64],
                                             mb64[0:1, 0:64]))
                        v.wait_ge(s_v.h, s_v.n)
                        s_v.inc(v.tensor_add(sum_sb[:, 0:64], sum2[:, 0:64],
                                             usum[:]))
                        v.wait_ge(s_v.h, s_v.n)
                    s_rc.inc(v.reciprocal(rec_sb[:], sum_sb[:]))

                block.vector(brcp)
                rcn = s_rc.n

                def bbc(t, rcn=rcn, rbg=s_rb.n):
                    t.wait_ge(s_rc.h, rcn)
                    if rbg:
                        t.wait_ge(s_rb.h, rbg)
                    s_bc.inc(t.matmul(prb[:, 0:256], oner[:], rec_sb[:],
                                      start=True, stop=True))

                block.tensor(bbc)
                bcn = s_bc.n

                abi = (h * NQB + jb) % 2

                def bev(v, h=h, jb=jb, abi=abi, bcn=bcn, sven=sven,
                        vprior=s_v.n, stg=s_ev.n):
                    v.wait_ge(s_v.h, vprior)
                    v.wait_ge(s_bc.h, bcn)
                    s_rb.inc(v.tensor_copy(recB[:], prb[:, 0:256]))
                    v.wait_ge(s_v.h, s_v.n)
                    if stg >= 32:
                        v.wait_ge(s_ev.h, stg - 16)
                    if jb == 0:
                        v.wait_ge(s_sve.h, sven)
                        s_v.inc(v.tensor_mul(o1t[:], pat[:, 0:64], recB[:, 0:64]))
                        v.wait_ge(s_v.h, s_v.n)
                        s_v.inc(v.tensor_mul(o1t[:], o1t[:], mb64[:, 0:64]))
                        v.wait_ge(s_v.h, s_v.n)
                        s_v.inc(v.scalar_tensor_tensor(
                            att_st[abi][:, 0:64],
                            ub64[:, 0:64], sumv8[:, 0:1], o1t[:],
                            ALU.mult, ALU.add))
                        v.wait_ge(s_v.h, s_v.n)
                        s_v.inc(v.tensor_mul(att_st[abi][:, 64:256],
                                             pat[:, 64:256], recB[:, 64:256]))
                    else:
                        s_v.inc(v.tensor_mul(att_st[abi][:, 0:256],
                                             pat[:, 0:256], recB[:, 0:256]))

                block.vector(bev)
                evn = s_v.n

                def asto(sy, h=h, jb=jb, abi=abi, evn=evn):
                    sy.wait_ge(s_v.h, evn)
                    s_ev.inc(sy.dma_start(
                        out=att_dram[h * 128:(h + 1) * 128,
                                     jb * 256:(jb + 1) * 256],
                        in_=att_st[abi][:]), 16)

                block.sync(asto)

    ATT_ALL = s_ev.n
    stA.close()

    if _os2.environ.get("DEV_B_DUMP") == "att":
        s_dp = C.sem("b3_dp")

        def dmpa(g):
            g.wait_ge(s_ev.h, ATT_ALL)
            for r in range(16):
                s_dp.inc(g.dma_start(
                    out=out_x[r * 128:(r + 1) * 128, :],
                    in_=att_dram[r * 128:(r + 1) * 128, :]), 16)

        block.gpsimd(dmpa)
        DN2 = s_dp.n

        class _O2:
            def barrier(self, eng):
                eng.wait_ge(s_dp.h, DN2)
        return _O2()

    st2 = contextlib.ExitStack()
    attT = C.sbuf("b3attT", [128, 16 * 2048], BF16, st2)
    s_al = C.sem("b3_al")

    def lda(sy):
        sy.wait_ge(s_ev.h, ATT_ALL)
        for mc in range(16):
            s_al.inc(sy.dma_start(
                out=attT[:, mc * 2048:(mc + 1) * 2048],
                in_=att_dram[mc * 128:(mc + 1) * 128, :]), 16)

    block.sync(lda)
    ALN = s_al.n
    wo_sb = C.sbuf("b3wo", [128, 16 * 2048], BF16, st2)
    ysb = [C.sbuf(f"b3y{i}", [128, 2048], F32, st2) for i in range(2)]
    pyb = [C.psum(f"b3py{i}", [128, 512], st2) for i in range(2)]
    s_wo = C.sem("b3_wo")
    s_ymm = C.sem("b3_ymm")
    s_yev = C.sem("b3_yev")
    s_res = [C.sem(f"b3_res{i}") for i in range(2)]

    def ldwo(g):
        g.wait_ge(s_ev.h, ATT_ALL)
        for mc in range(4):
            s_wo.inc(g.dma_start(
                out=wo_sb[:, mc * 8192:(mc + 1) * 8192],
                in_=wo_b[:, mc * 8192:(mc + 1) * 8192]), 16)

    block.gpsimd(ldwo)

    res_at = {}
    for tc in range(16):
        for fb in range(4):
            i = 4 * tc + fb

            def mmy(t, tc=tc, fb=fb, i=i, yegate=s_yev.n):
                if i == 0:
                    t.wait_ge(s_wo.h, 64)
                    t.wait_ge(s_al.h, ALN)
                if yegate >= 2:
                    t.wait_ge(s_yev.h, yegate - 1)
                wo3 = wo_sb[:].rearrange("p (m f) -> p m f", m=16)
                pb = pyb[i % 2]
                t.matmul(pb[:], onerb[:], bo[:, fb * 512:(fb + 1) * 512],
                         start=True, stop=False)
                for mc in range(16):
                    ins = t.matmul(
                        pb[:],
                        attT[:, mc * 2048 + tc * 128: mc * 2048 + (tc + 1) * 128],
                        wo3[:, mc, fb * 512:(fb + 1) * 512],
                        start=False, stop=(mc == 15))
                s_ymm.inc(ins)

            block.tensor(mmy)
            ymn = s_ymm.n

            def evy(v, tc=tc, fb=fb, i=i, ymn=ymn, rgate=res_at.get(tc - 2, 0),
                    vprior=s_v.n):
                v.wait_ge(s_v.h, vprior)
                v.wait_ge(s_ymm.h, ymn)
                if fb == 0 and rgate:
                    v.wait_ge(s_res[tc % 2].h, rgate)
                s_yev.inc(v.tensor_copy(
                    ysb[tc % 2][:, fb * 512:(fb + 1) * 512], pyb[i % 2][:]))

            block.vector(evy)

        yn = s_yev.n

        def acc(g, tc=tc, yn=yn):
            g.wait_ge(s_yev.h, yn)
            s_res[tc % 2].inc(g.dma_start(
                out=out_x[tc * 128:(tc + 1) * 128, :],
                in_=ysb[tc % 2][:], accum_op=ALU.add), 16)

        block.gpsimd(acc)
        res_at[tc] = s_res[tc % 2].n

    st2.close()
    R = [s_res[0].n, s_res[1].n]

    class ResOut:
        def barrier(self, eng):
            eng.wait_ge(s_res[0].h, R[0])
            eng.wait_ge(s_res[1].h, R[1])

    return ResOut()



def _host_consts(core):
    b, q = divmod(core, 4)
    kk = np.arange(128)[:, None]
    tt = np.arange(QB)[None, :]
    c = {}

    va = np.ones((LTOK,), np.float32)
    if q == 0:
        va[:HALO] = 0.0
    c["valid_a"] = np.ascontiguousarray(
        va.reshape(LCH, 128).T).astype(BF)

    m = np.zeros((4, 128, QB), np.float32)
    m[0] = (kk > tt)
    m[1] = (tt <= kk + 127)
    m[2] = (kk <= tt)
    m[3] = (tt >= kk + 128)
    c["mask_a"] = m.astype(BF)

    c["ones_col"] = np.ones((128, 1), np.float32).astype(BF)
    c["ones_row"] = np.ones((1, 128), np.float32)
    c["ones_row_bf"] = np.ones((1, 128), np.float32).astype(BF)
    base = q * TOK
    t64 = np.arange(QB, dtype=np.float64)
    kk64 = np.arange(128, dtype=np.float64)
    KILL = 1.0e9
    tab = np.zeros((NQB, 68, 3, QB), np.float32)
    reldelta = (256, 128, -256, -384)
    for jb in range(NQB):
        Q0 = base + QB * jb
        for ch in range(64):
            delta = Q0 - 128 * ch
            if -512 < delta < 384:
                tab[jb, ch, 0, :] = -KILL
                continue
            sg = 1.0 if delta >= 384 else -1.0
            tab[jb, ch, 0, :] = -sg * (Q0 + t64) + 257.0
            tab[jb, ch, 1, :] = sg
            tab[jb, ch, 2, :] = sg * 128.0 * ch
        for r, dl in enumerate(reldelta):
            sg = 1.0 if dl > 0 else -1.0
            tab[jb, 64 + r, 0, :] = -sg * (Q0 + t64) + 257.0
            tab[jb, 64 + r, 1, :] = sg
            tab[jb, 64 + r, 2, :] = sg * float(Q0 - dl)
    c["atab"] = tab

    mrel = np.zeros((128, NQB * 4 * QB), np.float32)
    for jb in range(NQB):
        Q0 = base + QB * jb
        for r, dl in enumerate(reldelta):
            j0 = Q0 - dl
            jj = j0 + np.arange(128)[:, None]
            dmat = (Q0 + np.arange(QB)[None, :]) - jj
            ok = (np.abs(dmat) > 256) & (jj >= 64) & (jj < SEQ)
            mrel[:, (jb * 4 + r) * QB:(jb * 4 + r + 1) * QB] = \
                np.where(ok, 0.0, -1.0e9)
    c["maskrel"] = mrel

    rows = base - 384 + np.arange(TOK + 640)
    c["kvwin_idx"] = np.ascontiguousarray(
        np.clip(rows, 0, SEQ - 1).astype(np.int32)
        .reshape((TOK + 640) // 128, 128).T)

    half = 64
    freqs = 1.0 / 10000.0 ** (np.arange(half, dtype=np.float64) / half)
    pos = base + np.arange(TOK, dtype=np.float64)
    ang = np.outer(pos, freqs)
    c["rope_sin"] = np.ascontiguousarray(
        np.sin(ang).astype(np.float32).reshape(NCH, 128, 64)
        .transpose(1, 0, 2).reshape(128, NCH * 64))
    c["rope_cos"] = np.ascontiguousarray(
        np.cos(ang).astype(np.float32).reshape(NCH, 128, 64)
        .transpose(1, 0, 2).reshape(128, NCH * 64))

    r3 = np.zeros((3, 16 * 128), np.float32)
    for h in range(16):
        s = float(SLOPES[h]) * SQD
        r3[0, h * 128:(h + 1) * 128] = s
        r3[1, h * 128:(h + 1) * 128] = s * np.arange(128)
        r3[2, h * 128:(h + 1) * 128] = s
    c["ramp3h"] = r3

    kj = np.ones((128, 1), np.float32)
    kj[:64] = 0.0
    c["kill_j64"] = kj

    u = np.zeros(64, np.float32)
    if q == 0:
        u[:] = 1.0
    c["uB64"] = np.broadcast_to(u, (128, 64)).astype(np.float32).copy()
    c["mB64"] = np.broadcast_to(1.0 - u, (128, 64)).astype(np.float32).copy()
    c["uni_flag"] = u.reshape(1, 64).copy()
    c["uni_sum"] = (u * float(SEQ)).reshape(1, 64).copy()

    ltri = (np.arange(128)[:, None] < np.arange(128)[None, :])
    c["ltri"] = ltri.astype(np.float32).astype(BF)
    c["tokf"] = (np.arange(NCH)[None, :] * 128
                 + np.arange(128)[:, None]).astype(np.float32)
    c["base_r"] = (np.arange(NE, dtype=np.float32) * CAP).reshape(1, NE)
    c["mfill"] = np.full((128, 256), 1.0e5, np.float32)
    return c



def build_program():
    nc = bass.Bass("TRN2", target_bir_lowering=False)

    def din(name, shape, dt=F32):
        return nc.dram_tensor(name, shape, dt, kind="ExternalInput").ap()

    io = {}
    io["x_own"] = din("x_own", [TOK, D])
    io["x_halo"] = din("x_halo", [HALO, D])
    io["wq_a"] = din("wq_a", [16, 128, D])
    io["wk_a"] = din("wk_a", [128, 16 * 512])
    io["wv_a"] = din("wv_a", [128, 16 * 512])
    io["wo_a"] = din("wo_a", [128, 16 * 2048])
    bo_a = din("bo_a", [1, D])
    valid_a = din("valid_a", [128, LCH], BF16)
    mask_a = din("mask_a", [4, 128, QB], BF16)
    ones_col = din("ones_col", [128, 1], BF16)
    ones_row = din("ones_row", [1, 128])
    ones_row_bf = din("ones_row_bf", [1, 128], BF16)

    io["wq_b"] = din("wq_b", [128, 16 * D])
    io["wkv_b"] = din("wkv_b", [128, 16 * D])
    io["wo_b"] = din("wo_b", [128, 16 * D])
    bo_b = din("bo_b", [1, D])
    io["rope_sin"] = din("rope_sin", [128, 16 * 64])
    io["rope_cos"] = din("rope_cos", [128, 16 * 64])
    io["atab"] = din("atab", [NQB, 68, 3, QB])
    io["maskrel"] = din("maskrel", [128, NQB * 4 * QB])
    io["kvwin_idx"] = din("kvwin_idx", [128, (TOK + 640) // 128], I32)
    io["ramp3h"] = din("ramp3h", [3, 16 * 128])
    io["kill_j64"] = din("kill_j64", [128, 1])
    io["uB64"] = din("uB64", [128, 64])
    io["mB64"] = din("mB64", [128, 64])
    io["uni_flag"] = din("uni_flag", [1, 64])
    io["uni_sum"] = din("uni_sum", [1, 64])
    io["gate_wT"] = din("gate_wT", [128, 16 * NE])
    io["w1T"] = din("w1T", [NE, 16, 128, 2048])
    io["w3T"] = din("w3T", [NE, 16, 128, 2048])
    io["w2T"] = din("w2T", [NE, 128, 16 * 2048])
    ltri = din("ltri", [128, 128], BF16)
    tokf = din("tokf", [128, NCH])
    base_r = din("base_r", [1, NE])
    mfill = din("mfill", [128, 256])

    io["out_x"] = nc.dram_tensor("out_x", [TOK, D], F32,
                                 kind="ExternalOutput").ap()
    io["xn_dram"] = nc.dram_tensor("xn_dram", [LTOK, D], BF16).ap()
    io["q_dram"] = nc.dram_tensor("q_dram", [D, D], BF16).ap()
    io["kv_contrib"] = nc.dram_tensor("kv_contrib", [TOK, D], BF16).ap()
    io["kv_all"] = nc.dram_tensor("kv_all", [4 * TOK, D], BF16).ap()
    io["kv_win"] = nc.dram_tensor("kv_win", [TOK + 640, D], BF16).ap()
    io["seg_x"] = nc.dram_tensor("seg_x", [NE * CAP, D], BF16).ap()
    io["seg_meta"] = nc.dram_tensor("seg_meta", [NE * CAP, 2], F32).ap()
    io["y01"] = nc.dram_tensor("y01", [2 * TOK, D], F32).ap()

    stack = contextlib.ExitStack()
    with nc.Block() as block:
        C = Ctx(nc, block, stack)

        onec = C.sbuf("onec", [128, 1], BF16)
        oner = C.sbuf("oner", [1, 128], F32)
        onerb = C.sbuf("onerb", [1, 128], BF16)
        maska_sb = C.sbuf("maska_sb", [128, 4 * QB], BF16)
        valida_sb = C.sbuf("valida_sb", [128, LCH], BF16)
        bo_a_sb = C.sbuf("bo_a_sb", [1, D], BF16)
        ltri_sb = C.sbuf("ltri_sb", [128, 128], BF16)
        tokf_sb = C.sbuf("tokf_sb", [128, NCH], F32)
        base_sb = C.sbuf("base_sb", [1, NE], F32)

        s_c = C.sem("consts")

        def ldc(sy):
            for dst, src in [
                (onec[:], ones_col), (oner[:], ones_row), (onerb[:], ones_row_bf),
                (maska_sb[:].rearrange("p (m t) -> p m t", m=4),
                 mask_a.rearrange("m p t -> p m t")),
                (valida_sb[:], valid_a),
                (ltri_sb[:], ltri), (tokf_sb[:], tokf), (base_sb[:], base_r),
            ]:
                s_c.inc(sy.dma_start(out=dst, in_=src), 16)

        block.sync(ldc)

        s_cg = C.sem("constsg")

        bo_b_sb = C.sbuf("bo_b_sb", [1, D], BF16)
        io["bo_b_sb"] = bo_b_sb

        def ldc2(g):
            s_cg.inc(g.dma_start(out=bo_a_sb[:], in_=bo_a), 16)
            s_cg.inc(g.dma_start(out=bo_b_sb[:], in_=bo_b), 16)

        block.gpsimd(ldc2)

        def regs(g):
            r1 = C.stack.enter_context(g.register("bnd_seg"))
            g.reg_mov(r1, NE * CAP - 1)
            io["bnd_seg"] = g.snap(r1)
            r2 = C.stack.enter_context(g.register("bnd_y"))
            g.reg_mov(r2, 2 * TOK - 1)
            io["bnd_y"] = g.snap(r2)

        block.gpsimd(regs)

        def ldc3(sy):
            s_c.inc(sy.dma_start(out=io["out_x"], in_=io["x_own"]), 16)

        block.sync(ldc3)

        zsb = C.sbuf("zsb", [128, D], BF16)
        s_zi = C.sem("zinit")
        io["s_zi"] = s_zi

        def zi(v):
            s_zi.inc(v.memset(zsb[:], 0.0))

        block.vector(zi)

        def zid(sy):
            sy.wait_ge(s_zi.h, 1)
            for r in range(NE * CAP // 128):
                s_zi.inc(sy.dma_start(
                    out=io["seg_x"][r * 128:(r + 1) * 128, :], in_=zsb[:]), 16)

        block.sync(zid)
        io["zi_n"] = s_zi.n
        CWAIT = s_c.n

        io.update(dict(onec=onec, oner=oner, onerb=onerb, maska=maska_sb,
                       valida=valida_sb, bo_a=bo_a_sb, ltri=ltri_sb,
                       tokf=tokf_sb, base_r=base_sb, mfill=mfill))

        lnstack = contextlib.ExitStack()
        io["ln1"] = emit_ln(
            C, "ln1", LCH,
            lambda k: (io["x_halo"][k * 128:(k + 1) * 128, :] if k < 8
                       else io["x_own"][(k - 8) * 128:(k - 7) * 128, :]),
            io["xn_dram"], 0, [(s_c, CWAIT), (s_cg, s_cg.n)], stack=lnstack)
        lnstack.close()
        if DEV_STOP == "ln":
            fin_o = io["ln1"]
        elif _os3.environ.get("DEV_SKIP_A"):
            fin_o = emit_moe(C, io, "m2", io["ln1"])
        elif _os3.environ.get("DEV_ONLY_B"):
            fin_o = emit_attn_b(C, io, io["ln1"])
        else:
            fin_o = emit_attn_a(C, io)
            if DEV_STOP == "full" and N_PHASES >= 2:
                fin_o = emit_moe(C, io, "m2", fin_o)
            if DEV_STOP == "full" and N_PHASES >= 3:
                fin_o = emit_attn_b(C, io, fin_o)
            if DEV_STOP == "full" and N_PHASES >= 4:
                fin_o = emit_moe(C, io, "m4", fin_o)

        def fin(g):
            fin_o.barrier(g)

        block.gpsimd(fin)

    stack.close()
    return nc


def build_in_maps(L0_attn_a_x, L0_attn_a_Wq, L0_attn_a_Wk, L0_attn_a_Wv,
                  L0_attn_a_Wo, L0_attn_a_bo, L0_attn_b_Wq, L0_attn_b_Wkv,
                  L0_attn_b_Wo, L0_attn_b_bo, L0_ffn_gate_w, L0_ffn_w1,
                  L0_ffn_w3, L0_ffn_w2):
    x = np.asarray(L0_attn_a_x, dtype=np.float32)
    wq = np.asarray(L0_attn_a_Wq, np.float32)
    wk = np.asarray(L0_attn_a_Wk, np.float32)
    wv = np.asarray(L0_attn_a_Wv, np.float32)
    wo = np.asarray(L0_attn_a_Wo, np.float32)
    gw = np.asarray(L0_ffn_gate_w, np.float32)
    w1 = np.asarray(L0_ffn_w1, np.float32)
    w3 = np.asarray(L0_ffn_w3, np.float32)
    w2 = np.asarray(L0_ffn_w2, np.float32)
    shared = {
        "wq_a": np.ascontiguousarray(
            wq.reshape(16, 128, 16, 128).transpose(2, 1, 0, 3)
              .reshape(16, 128, D)),
        "wk_a": np.ascontiguousarray(
            wk.reshape(16, 128, 512).transpose(1, 0, 2).reshape(128, 16 * 512)),
        "wv_a": np.ascontiguousarray(
            wv.reshape(16, 128, 512).transpose(1, 0, 2).reshape(128, 16 * 512)),
        "wo_a": np.ascontiguousarray(
            wo.reshape(16, 128, D).transpose(1, 0, 2).reshape(128, 16 * D)),
        "bo_a": np.ascontiguousarray(L0_attn_a_bo, np.float32).reshape(1, D),
        "wq_b": np.ascontiguousarray(
            np.asarray(L0_attn_b_Wq, np.float32).reshape(16, 128, D)
              .transpose(1, 0, 2).reshape(128, 16 * D)),
        "wkv_b": np.ascontiguousarray(
            np.asarray(L0_attn_b_Wkv, np.float32).reshape(16, 128, D)
              .transpose(1, 0, 2).reshape(128, 16 * D)),
        "wo_b": np.ascontiguousarray(
            np.asarray(L0_attn_b_Wo, np.float32).reshape(16, 128, D)
              .transpose(1, 0, 2).reshape(128, 16 * D)),
        "bo_b": np.ascontiguousarray(L0_attn_b_bo, np.float32).reshape(1, D),
        "gate_wT": np.ascontiguousarray(
            gw.T.reshape(16, 128, NE).transpose(1, 0, 2).reshape(128, 16 * NE)),
        "w1T": np.ascontiguousarray(
            w1.reshape(NE, 16, 128, 16, 128).transpose(0, 1, 4, 3, 2)
              .reshape(NE, 16, 128, D)),
        "w3T": np.ascontiguousarray(
            w3.reshape(NE, 16, 128, 16, 128).transpose(0, 1, 4, 3, 2)
              .reshape(NE, 16, 128, D)),
        "w2T": np.ascontiguousarray(
            w2.transpose(0, 2, 1).reshape(NE, 16, 128, D)
              .transpose(0, 2, 1, 3).reshape(NE, 128, 16 * D)),
    }

    in_maps = []
    for core in range(NCORES):
        b, q = divmod(core, 4)
        m = dict(shared)
        m["x_own"] = np.ascontiguousarray(x[b, q * TOK:(q + 1) * TOK])
        halo = np.zeros((HALO, D), np.float32)
        if q > 0:
            halo[:] = x[b, q * TOK - HALO:q * TOK]
        m["x_halo"] = halo
        m.update(_host_consts(core))
        in_maps.append(m)
    return in_maps


def kernel(**kw):
    in_maps = build_in_maps(**kw)
    nc = build_program()
    res = run_bass_kernel_spmd(nc, in_maps, core_ids=list(range(NCORES)))
    out = np.empty((2, SEQ, D), np.float32)
    for core in range(NCORES):
        b, q = divmod(core, 4)
        out[b, q * TOK:(q + 1) * TOK] = res.results[core]["out_x"]
    return out




# version 04
# speedup vs baseline: 1.1454x; 1.1454x over previous
import math
import os as _os2
import os as _os3
import contextlib
import numpy as np
import ml_dtypes
import concourse.bass as bass
import concourse.mybir as mybir
from concourse.bass_utils import run_bass_kernel_spmd

F32 = mybir.dt.float32
BF16 = mybir.dt.bfloat16
I32 = mybir.dt.int32
AF = mybir.ActivationFunctionType
ALU = mybir.AluOpType
AX = mybir.AxisListType
BF = ml_dtypes.bfloat16

NCORES = 8
TOK = 2048
D = 2048
HALO = 1024
LTOK = TOK + HALO
NCH = 16
LCH = 24
QB = 256
NQB = 8
SEQ = 8192
CAP = 2048
NE = 8
SQD = math.sqrt(128.0)
_AB = 2.0 ** (-(2.0 ** (-(math.log2(16.0) - 3.0))))
SLOPES = np.array([_AB ** p for p in range(1, 17)], dtype=np.float64)

N_PHASES = 4
import os
DEV_STOP = os.environ.get('DEV_STOP', 'full')


class S:
    __slots__ = ("h", "n")

    def __init__(self, h):
        self.h = h
        self.n = 0

    def inc(self, instr, k=1):
        instr.then_inc(self.h, k)
        self.n += k
        return self.n


class Ctx:
    def __init__(self, nc, block, stack):
        self.nc = nc
        self.block = block
        self.stack = stack
        self.sems = {}

    def sem(self, name):
        key = name.split("_", 1)[1] if "_" in name else name
        if key not in self.sems:
            h = self.stack.enter_context(self.nc.semaphore(key))
            self.sems[key] = S(h)
        return self.sems[key]

    def sbuf(self, name, shape, dt, stack=None):
        return (stack or self.stack).enter_context(
            self.nc.sbuf_tensor(name, shape, dt))

    def psum(self, name, shape, stack=None):
        return (stack or self.stack).enter_context(
            self.nc.psum_tensor(name, shape, F32))



def emit_ln(C, tag, n_chunks, x_src, xn_dram, row0, wait, comp_out=None,
            stack=None, prev=None):
    block = C.block
    NB = 4
    xb = [C.sbuf(f"{tag}_x{i}", [128, D], F32, stack) for i in range(NB)]
    xnb = [C.sbuf(f"{tag}_n{i}", [128, D], BF16, stack) for i in range(NB)]
    sq = C.sbuf(f"{tag}_sq", [128, D], F32, stack)
    st = C.sbuf(f"{tag}_st", [128, 8], F32, stack)
    cs = C.sbuf(f"{tag}_cs", [128, 1], F32, stack) if comp_out is not None else None

    s_ld = [C.sem(f"{tag}_ld{i}") for i in range(NB)]
    s_a = C.sem(f"{tag}_a")
    s_v = C.sem(f"{tag}_v")
    s_xs = [C.sem(f"{tag}_xs{i}") for i in range(NB)]

    xb_free = [0] * NB
    xnb_st = [0] * NB
    xn_at = []
    ex2_done = 0

    for k in range(n_chunks):
        bi = k % NB
        kw = list(wait) if k == 0 else []
        gate = xb_free[bi]

        def ld(sy, k=k, kw=kw, gate=gate, bi=bi):
            for (ws, wn) in kw:
                sy.wait_ge(ws.h, wn)
            if k == 0 and prev is not None:
                prev.barrier(sy)
            if gate:
                sy.wait_ge(s_v.h, gate)
            s_ld[bi].inc(sy.dma_start(out=xb[bi][:], in_=x_src(k)), 16)

        block.sync(ld)
        ldn = s_ld[bi].n

        def act1(s, ldn=ldn, bi=bi, exd=ex2_done):
            s.wait_ge(s_ld[bi].h, ldn)
            if exd:
                s.wait_ge(s_v.h, exd)
            s_a.inc(s.activation(sq[:], xb[bi][:], AF.Square,
                                 accum_out=st[:, 1:2]))

        block.scalar(act1)
        an1 = s_a.n

        def vec1(v, ldn=ldn, an1=an1, bi=bi):
            v.wait_ge(s_ld[bi].h, ldn)
            if comp_out is not None:
                s_v.inc(v.tensor_reduce(cs[:], xb[bi][:], AX.X, ALU.add,
                                        apply_absolute_value=True))
            s_v.inc(v.tensor_reduce(st[:, 0:1], xb[bi][:], AX.X, ALU.add))
            v.wait_ge(s_v.h, s_v.n)
            v.wait_ge(s_a.h, an1)
            s_v.inc(v.tensor_scalar(st[:, 2:3], st[:, 0:1], 1.0 / D, None, ALU.mult))
            s_v.inc(v.tensor_scalar(st[:, 3:4], st[:, 1:2], 1.0 / D, None, ALU.mult))
            v.wait_ge(s_v.h, s_v.n)
            s_v.inc(v.tensor_mul(st[:, 4:5], st[:, 2:3], st[:, 2:3]))
            v.wait_ge(s_v.h, s_v.n)
            s_v.inc(v.tensor_sub(st[:, 5:6], st[:, 3:4], st[:, 4:5]))
            v.wait_ge(s_v.h, s_v.n)
            s_v.inc(v.tensor_scalar(st[:, 5:6], st[:, 5:6], 1e-5, None, ALU.add))

        block.vector(vec1)
        ex2_done = s_v.n - 3
        vn1 = s_v.n

        def act2(s, vn1=vn1, k=k):
            s.wait_ge(s_v.h, vn1)
            s_a.inc(s.activation(st[:, 6:7], st[:, 5:6], AF.Sqrt))
            if comp_out is not None:
                s_a.inc(s.activation(comp_out[:, k:k + 1], cs[:], AF.Sigmoid))

        block.scalar(act2)
        an2 = s_a.n

        def vec2(v, an2=an2, bi=bi, xng=xnb_st[bi]):
            v.wait_ge(s_a.h, an2)
            s_v.inc(v.reciprocal(st[:, 7:8], st[:, 6:7]))
            v.wait_ge(s_v.h, s_v.n)
            if xng:
                v.wait_ge(s_xs[bi].h, xng)
            s_v.inc(v.tensor_scalar(xnb[bi][:], xb[bi][:],
                                    st[:, 2:3], st[:, 7:8],
                                    ALU.subtract, ALU.mult))

        block.vector(vec2)
        xb_free[bi] = s_v.n
        vn2 = s_v.n

        def sto(sy, vn2=vn2, k=k, bi=bi):
            sy.wait_ge(s_v.h, vn2)
            s_xs[bi].inc(sy.dma_start(
                out=xn_dram[row0 + k * 128: row0 + (k + 1) * 128, :],
                in_=xnb[bi][:]), 16)

        block.sync(sto)
        xnb_st[bi] = s_xs[bi].n
        xn_at.append((bi, s_xs[bi].n))

    class LNOut:
        def barrier(self, eng):
            for s in s_xs:
                if s.n:
                    eng.wait_ge(s.h, s.n)

        def chunk_barrier(self, eng, k):
            bi, n = xn_at[k]
            eng.wait_ge(s_xs[bi].h, n)

    return LNOut()



def emit_attn_a(C, io):
    block = C.block
    wq_a, wk_a, wv_a, wo_a = io["wq_a"], io["wk_a"], io["wv_a"], io["wo_a"]
    xn_dram, q_dram, out_x = io["xn_dram"], io["q_dram"], io["out_x"]
    oner, onerb = io["oner"], io["onerb"]
    maska, valida, bo = io["maska"], io["valida"], io["bo_a"]
    ln = io["ln1"]

    stA = contextlib.ExitStack()
    attT = C.sbuf("a1attT", [128, 16 * 2048], BF16, stA)
    st1 = contextlib.ExitStack()
    stKV = contextlib.ExitStack()
    kT = C.sbuf("a1kT", [128, 4 * 3072], BF16, st1)
    vv = C.sbuf("a1v", [128, 24 * 512], BF16, st1)
    xnt = [C.sbuf(f"a1xnt{i}", [128, 16 * 512], BF16, st1) for i in range(2)]
    wks = C.sbuf("a1wk", [128, 16 * 512], BF16, stKV)
    wvs = C.sbuf("a1wv", [128, 16 * 512], BF16, stKV)

    pj = C.psum("a1pj", [128, 512], st1)
    pj2 = C.psum("a1pj2", [128, 512], st1)
    psc = [C.psum(f"a1psc{i}", [128, 512], st1) for i in range(2)]
    pat = C.psum("a1pat", [128, 512], st1)
    psm = C.psum("a1psm", [128, 512], st1)
    prb = C.psum("a1prb", [128, 512], st1)

    s_t = [C.sem(f"a1_t{i}") for i in range(2)]
    s_w = C.sem("a1_w")
    s_wq = [C.sem(f"a1_wq{i}") for i in range(2)]
    s_kve = C.sem("a1_kve")
    s_mm = C.sem("a1_mm")
    s_qev = C.sem("a1_qev")
    s_qst = [C.sem(f"a1_qst{i}") for i in range(2)]
    s_ld = [C.sem(f"a1_ld{i}") for i in range(2)]
    s_sc = C.sem("a1_sc")
    s_ex = C.sem("a1_ex")
    s_mk = C.sem("a1_mk")
    s_pv = C.sem("a1_pv")
    s_sm = C.sem("a1_sm")
    s_rc = C.sem("a1_rc")
    s_bc = C.sem("a1_bc")
    s_rb = C.sem("a1_rb")
    s_ev = C.sem("a1_ev")
    s_wo = C.sem("a1_wo")
    s_ymm = C.sem("a1_ymm")
    s_yev = C.sem("a1_yev")
    s_res = [C.sem(f"a1_res{i}") for i in range(2)]

    def ldw(g):
        ln.barrier(g)
        s_w.inc(g.dma_start(out=wks[:], in_=wk_a), 16)
        s_w.inc(g.dma_start(out=wvs[:], in_=wv_a), 16)

    block.gpsimd(ldw)

    xnt_free = [0, 0]
    pjf = 0
    pj2f = 0
    for T in range(6):
        def ldt(sy, T=T, gate=xnt_free[T % 2]):
            if T == 0:
                ln.barrier(sy)
            if gate:
                sy.wait_ge(s_mm.h, gate)
            s_t[T % 2].inc(sy.dma_start(
                out=xnt[T % 2][:].rearrange("p (dc r) -> p dc r", dc=16),
                in_=xn_dram[T * 512:(T + 1) * 512, :]
                    .rearrange("r (dc p) -> r dc p", p=128),
                transpose=True), 16)

        block.sync(ldt)
        tw = s_t[T % 2].n

        def mmkv(t, T=T, tw=tw, pjf=pjf, pj2f=pj2f):
            if T == 0:
                t.wait_ge(s_w.h, 32)
            t.wait_ge(s_t[T % 2].h, tw)
            wk3 = wks[:].rearrange("p (dc n) -> p dc n", dc=16)
            wv3 = wvs[:].rearrange("p (dc n) -> p dc n", dc=16)
            x3 = xnt[T % 2][:].rearrange("p (dc r) -> p dc r", dc=16)
            for g4 in range(4):
                if pjf + g4 > 0:
                    t.wait_ge(s_kve.h, pjf + g4)
                for dc in range(16):
                    ins = t.matmul(pj[:], wk3[:, dc, g4 * 128:(g4 + 1) * 128],
                                   x3[:, dc, :], start=(dc == 0), stop=(dc == 15))
                s_mm.inc(ins)
            for tc in range(4):
                if pj2f + tc > 0:
                    t.wait_ge(s_kve.h, pj2f + tc + 4)
                for dc in range(16):
                    ins = t.matmul(pj2[:], x3[:, dc, tc * 128:(tc + 1) * 128],
                                   wv3[:, dc, :],
                                   start=(dc == 0), stop=(dc == 15))
                s_mm.inc(ins)

        block.tensor(mmkv)
        mmn = s_mm.n
        xnt_free[T % 2] = mmn

        def evkv(v, T=T, mmn=mmn):
            base = mmn - 8
            for g4 in range(4):
                v.wait_ge(s_mm.h, base + g4 + 1)
                s_kve.inc(v.tensor_copy(
                    kT[:, g4 * 3072 + T * 512: g4 * 3072 + (T + 1) * 512], pj[:]))
            for tc in range(4):
                v.wait_ge(s_mm.h, base + 4 + tc + 1)
                s_kve.inc(v.tensor_copy(
                    vv[:, (4 * T + tc) * 512:(4 * T + tc + 1) * 512], pj2[:]))

        block.vector(evkv)
        pjf = s_kve.n
        pj2f = s_kve.n

    KV_EV = s_kve.n
    KV_MM = s_mm.n
    stKV.close()
    wqb = [C.sbuf(f"a1wq{i}", [128, 16 * 128], BF16, st1) for i in range(2)]
    qev = [C.sbuf(f"a1qe{i}", [128, 512], BF16, st1) for i in range(2)]

    mm_at = {}
    u = 0
    for pash, tiles in ((0, (2, 3)), (1, (4, 5))):
        for bi, T in enumerate(tiles):
            def ldt2(sy, T=T, bi=bi, gate=xnt_free[bi]):
                sy.wait_ge(s_mm.h, gate)
                s_t[bi].inc(sy.dma_start(
                    out=xnt[bi][:].rearrange("p (dc r) -> p dc r", dc=16),
                    in_=xn_dram[T * 512:(T + 1) * 512, :]
                        .rearrange("r (dc p) -> r dc p", p=128),
                    transpose=True), 16)

            block.sync(ldt2)
        twq = [s_t[0].n, s_t[1].n]

        for h in range(16):
            def ldq(g, h=h, u=u, kvn=KV_MM, wg=mm_at.get(u - 2, 0)):
                if u == 0 and h == 0:
                    g.wait_ge(s_mm.h, kvn)
                if wg:
                    g.wait_ge(s_mm.h, wg)
                s_wq[h % 2].inc(g.dma_start(
                    out=wqb[h % 2][:], in_=wq_a[h]), 16)

            block.gpsimd(ldq)
            wn = s_wq[h % 2].n

            for bi, T in enumerate(tiles):
                i = 2 * u + bi

                def mmq(t, h=h, bi=bi, wn=wn, tn=twq[bi], i=i, qf=s_qev.n):
                    t.wait_ge(s_wq[h % 2].h, wn)
                    t.wait_ge(s_t[bi].h, tn)
                    if i == 0:
                        t.wait_ge(s_kve.h, KV_EV)
                    if qf >= 2:
                        t.wait_ge(s_qev.h, qf - 1)
                    wq3 = wqb[h % 2][:].rearrange("p (dc f) -> p dc f", dc=16)
                    x3 = xnt[bi][:].rearrange("p (dc r) -> p dc r", dc=16)
                    for dc in range(16):
                        ins = t.matmul(pj[:] if i % 2 == 0 else pj2[:],
                                       wq3[:, dc, :], x3[:, dc, :],
                                       start=(dc == 0), stop=(dc == 15))
                    s_mm.inc(ins)

                block.tensor(mmq)
                mmn = s_mm.n

                def evq(v, mmn=mmn, i=i, stn=s_qst[i % 2].n):
                    v.wait_ge(s_mm.h, mmn)
                    if stn:
                        v.wait_ge(s_qst[i % 2].h, stn)
                    s_qev.inc(v.tensor_copy(qev[i % 2][:],
                                            pj[:] if i % 2 == 0 else pj2[:]))

                block.vector(evq)
                qn = s_qev.n

                def stq(sy, h=h, T=T, qn=qn, i=i):
                    sy.wait_ge(s_qev.h, qn)
                    s_qst[i % 2].inc(sy.dma_start(
                        out=q_dram[h * 128:(h + 1) * 128,
                                   (T - 2) * 512:(T - 1) * 512],
                        in_=qev[i % 2][:]), 16)

                block.sync(stq)
            mm_at[u] = s_mm.n
            u += 1
        xnt_free[0] = s_mm.n
        xnt_free[1] = s_mm.n

    QST = [s_qst[0].n, s_qst[1].n]
    import os as _os
    if _os.environ.get('DEV_STOP', 'full') in ('kv', 'q'):
        st1.close()
        stA.close()

        class _O:
            def barrier(self, eng):
                eng.wait_ge(s_qst[0].h, QST[0])
                eng.wait_ge(s_qst[1].h, QST[1])
        return _O()

    qsl = [C.sbuf(f"a1qsl{i}", [128, 16 * 256], BF16, st1) for i in range(2)]
    P = C.sbuf("a1P", [128, 10 * 256], BF16, st1)
    sum_sb = C.sbuf("a1sum", [1, 256], F32, st1)
    rec_sb = C.sbuf("a1rec", [1, 256], F32, st1)
    recB = C.sbuf("a1recB", [128, 256], F32, st1)

    qsl_free = [0, 0]
    _attlim = int(_os.environ.get('DEV_ATT_LIM', '999'))

    for jb in range(NQB):
        if jb * 16 >= _attlim:
            break

        def ldqs(sy, jb=jb, gate=qsl_free[jb % 2]):
            if jb == 0:
                sy.wait_ge(s_qst[0].h, QST[0])
                sy.wait_ge(s_qst[1].h, QST[1])
            if gate:
                sy.wait_ge(s_sc.h, gate)
            s_ld[jb % 2].inc(sy.dma_start(
                out=qsl[jb % 2][:].rearrange("p (hh r) -> p hh r", hh=16),
                in_=q_dram[:, jb * 256:(jb + 1) * 256]
                    .rearrange("(hh p) r -> p hh r", p=128)), 16)

        block.sync(ldqs)
        ldn = s_ld[jb % 2].n

        for h in range(16):
            g4 = h // 4
            hq = jb * 16 + h
            if hq >= _attlim:
                break

            def mmsc(t, jb=jb, h=h, g4=g4, hq=hq, ldn=ldn):
                t.wait_ge(s_ld[jb % 2].h, ldn)
                q_ap = qsl[jb % 2][:].rearrange(
                    "p (hh r) -> p hh r", hh=16)[:, h, :]
                for c in range(10):
                    g = hq * 10 + c
                    if g >= 2:
                        t.wait_ge(s_ex.h, g - 1)
                    kc = 2 * jb + c
                    ins = t.matmul(psc[c % 2][:, 0:256],
                                   kT[:, g4 * 3072 + kc * 128:
                                      g4 * 3072 + (kc + 1) * 128],
                                   q_ap, start=True, stop=True)
                    s_sc.inc(ins)

            block.tensor(mmsc)
            scn = s_sc.n

            def exps(s, scn=scn, pvgate=s_pv.n):
                if pvgate:
                    s.wait_ge(s_pv.h, pvgate)
                for c in range(10):
                    s.wait_ge(s_sc.h, scn - 10 + c + 1)
                    s_ex.inc(s.activation(P[:, c * 256:(c + 1) * 256],
                                          psc[c % 2][:, 0:256],
                                          AF.Exp, scale=1.0 / SQD))

            block.scalar(exps)
            exn = s_ex.n
            if _os.environ.get('DEV_ATT_OPS', 'full') == 'sc':
                continue

            def msk(v, exn=exn):
                for mi, c in enumerate((0, 1, 8, 9)):
                    v.wait_ge(s_ex.h, exn - 10 + c + 1)
                    s_mk.inc(v.tensor_mul(P[:, c * 256:(c + 1) * 256],
                                          P[:, c * 256:(c + 1) * 256],
                                          maska[:, mi * 256:(mi + 1) * 256]))

            block.vector(msk)
            mkn = s_mk.n
            if _os.environ.get('DEV_ATT_OPS', 'full') == 'msk':
                continue

            def mmpv(t, jb=jb, g4=g4, exn=exn, mkn=mkn, evgate=s_ev.n):
                if evgate:
                    t.wait_ge(s_ev.h, evgate)
                t.wait_ge(s_ex.h, exn)
                t.wait_ge(s_mk.h, mkn)
                for c in range(10):
                    kc = 2 * jb + c
                    t.matmul(pat[:, 0:256], vv[:, kc * 512 + g4 * 128:
                                               kc * 512 + (g4 + 1) * 128],
                             P[:, c * 256:(c + 1) * 256],
                             start=(c == 0), stop=(c == 9))
                    ins = t.matmul(psm[0:1, 0:256], valida[:, kc:kc + 1],
                                   P[:, c * 256:(c + 1) * 256],
                                   start=(c == 0), stop=(c == 9))
                s_pv.inc(ins)

            block.tensor(mmpv)
            pvn = s_pv.n
            if _os.environ.get('DEV_ATT_OPS', 'full') == 'pv':
                continue

            def cps(s, pvn=pvn):
                s.wait_ge(s_pv.h, pvn)
                s_sm.inc(s.copy(sum_sb[:], psm[0:1, 0:256]))

            block.scalar(cps)
            smn = s_sm.n

            def rcp(v, smn=smn):
                v.wait_ge(s_sm.h, smn)
                s_rc.inc(v.reciprocal(rec_sb[:], sum_sb[:]))

            block.vector(rcp)
            rcn = s_rc.n

            def bca(t, rcn=rcn, rbgate=s_rb.n):
                t.wait_ge(s_rc.h, rcn)
                if rbgate:
                    t.wait_ge(s_rb.h, rbgate)
                s_bc.inc(t.matmul(prb[:, 0:256], oner[:], rec_sb[:],
                                  start=True, stop=True))

            block.tensor(bca)
            bcn = s_bc.n

            def evat(v, bcn=bcn, jb=jb, h=h):
                v.wait_ge(s_bc.h, bcn)
                s_rb.inc(v.tensor_copy(recB[:], prb[:, 0:256]))
                v.wait_ge(s_rb.h, s_rb.n)
                s_ev.inc(v.tensor_mul(
                    attT[:, h * 2048 + jb * 256: h * 2048 + (jb + 1) * 256],
                    pat[:, 0:256], recB[:]))

            block.vector(evat)
        qsl_free[jb % 2] = s_sc.n

    ATT_ALL = s_ev.n
    st1.close()
    if _os.environ.get('DEV_STOP', 'full') == 'att':
        stA.close()
        ops = _os.environ.get('DEV_ATT_OPS', 'full')
        sf = {'sc': s_ex, 'msk': s_mk, 'pv': s_pv, 'full': s_ev}[ops]
        FN = sf.n

        class _O:
            def barrier(self, eng):
                eng.wait_ge(sf.h, FN)
        return _O()

    st2 = contextlib.ExitStack()
    wo_sb = C.sbuf("a1wo", [128, 16 * 2048], BF16, st2)
    ysb = [C.sbuf(f"a1y{i}", [128, 2048], F32, st2) for i in range(2)]
    pyb = [C.psum(f"a1py{i}", [128, 512], st2) for i in range(2)]

    def ldwo(g):
        g.wait_ge(s_ev.h, ATT_ALL)
        for mc in range(4):
            s_wo.inc(g.dma_start(
                out=wo_sb[:, mc * 8192:(mc + 1) * 8192],
                in_=wo_a[:, mc * 8192:(mc + 1) * 8192]), 16)

    block.gpsimd(ldwo)

    res_at = {}
    for tc in range(16):
        for fb in range(4):
            i = 4 * tc + fb

            def mmy(t, tc=tc, fb=fb, i=i, yegate=s_yev.n):
                if i == 0:
                    t.wait_ge(s_wo.h, 64)
                if yegate >= 2:
                    t.wait_ge(s_yev.h, yegate - 1)
                wo3 = wo_sb[:].rearrange("p (m f) -> p m f", m=16)
                pb = pyb[i % 2]
                t.matmul(pb[:], onerb[:], bo[:, fb * 512:(fb + 1) * 512],
                         start=True, stop=False)
                for mc in range(16):
                    ins = t.matmul(
                        pb[:],
                        attT[:, mc * 2048 + tc * 128: mc * 2048 + (tc + 1) * 128],
                        wo3[:, mc, fb * 512:(fb + 1) * 512],
                        start=False, stop=(mc == 15))
                s_ymm.inc(ins)

            block.tensor(mmy)
            ymn = s_ymm.n

            def evy(v, tc=tc, fb=fb, i=i, ymn=ymn,
                    rgate=res_at.get(tc - 2, 0)):
                v.wait_ge(s_ymm.h, ymn)
                if fb == 0 and rgate:
                    v.wait_ge(s_res[tc % 2].h, rgate)
                s_yev.inc(v.tensor_copy(
                    ysb[tc % 2][:, fb * 512:(fb + 1) * 512], pyb[i % 2][:]))

            block.vector(evy)

        yn = s_yev.n

        def acc(g, tc=tc, yn=yn):
            g.wait_ge(s_yev.h, yn)
            s_res[tc % 2].inc(g.dma_start(
                out=out_x[tc * 128:(tc + 1) * 128, :],
                in_=ysb[tc % 2][:], accum_op=ALU.add), 16)

        block.gpsimd(acc)
        res_at[tc] = s_res[tc % 2].n

    st2.close()
    stA.close()
    R0, R1 = s_res[0].n, s_res[1].n

    class ResOut:
        def barrier(self, eng):
            eng.wait_ge(s_res[0].h, R0)
            eng.wait_ge(s_res[1].h, R1)

    return ResOut()




def emit_moe(C, io, tag, prev):
    block = C.block
    out_x, xn_dram = io["out_x"], io["xn_dram"]
    gate_wT, w1T, w3T, w2T = io["gate_wT"], io["w1T"], io["w3T"], io["w2T"]
    seg_x, seg_meta, y01 = io["seg_x"], io["seg_meta"], io["y01"]
    oner, onec = io["oner"], io["onec"]
    ltri, tokf, base_r, mfill = io["ltri"], io["tokf"], io["base_r"], io["mfill"]

    st = contextlib.ExitStack()
    gw = C.sbuf(f"{tag}gw", [128, 16 * 8], BF16, st)
    xnc = [C.sbuf(f"{tag}xnc{i}", [128, 16 * 128], BF16, st) for i in range(2)]
    xsc = [C.sbuf(f"{tag}xsc{i}", [128, D], BF16, st) for i in range(2)]
    gsc = C.sbuf(f"{tag}gsc", [128, 8], F32, st)
    gex = C.sbuf(f"{tag}gex", [128, 8], F32, st)
    gst = C.sbuf(f"{tag}gst", [128, 8], F32, st)
    probs = C.sbuf(f"{tag}pr", [128, 8], F32, st)
    s0t = C.sbuf(f"{tag}s0", [128, 8], F32, st)
    rt = C.sbuf(f"{tag}rt", [128, 8], F32, st)
    s1t = C.sbuf(f"{tag}s1", [128, 8], F32, st)
    rb16 = C.sbuf(f"{tag}rb", [128, 8], BF16, st)
    lc = C.sbuf(f"{tag}lc", [128, 8], F32, st)
    obB = C.sbuf(f"{tag}obB", [128, 8], F32, st)
    dst = C.sbuf(f"{tag}dst", [128, 8], F32, st)
    toti = C.sbuf(f"{tag}toti", [1, 16 * 8], F32, st)
    offs = C.sbuf(f"{tag}offs", [1, 16 * 8], F32, st)
    obk = C.sbuf(f"{tag}obk", [1, 8], F32, st)
    sm1 = C.sbuf(f"{tag}sm1", [128, 1], F32, st)
    sm2 = C.sbuf(f"{tag}sm2", [128, 1], F32, st)
    nmx = C.sbuf(f"{tag}nmx", [128, 1], F32, st)
    rs = C.sbuf(f"{tag}rs", [128, 1], F32, st)
    rc1 = C.sbuf(f"{tag}rc1", [128, 1], F32, st)
    di0 = [C.sbuf(f"{tag}dia{i}", [128, 1], I32, st) for i in range(2)]
    di1 = [C.sbuf(f"{tag}dib{i}", [128, 1], I32, st) for i in range(2)]
    dd = C.sbuf(f"{tag}dd", [128, 2], F32, st)
    mv0 = [C.sbuf(f"{tag}mva{i}", [128, 2], F32, st) for i in range(2)]
    mv1 = [C.sbuf(f"{tag}mvb{i}", [128, 2], F32, st) for i in range(2)]
    mfl = C.sbuf(f"{tag}mfl", [128, 256], F32, st)

    pg = C.psum(f"{tag}pg", [128, 512], st)
    pl = C.psum(f"{tag}pl", [128, 512], st)
    pt = C.psum(f"{tag}pt", [128, 512], st)
    pbk = C.psum(f"{tag}pbk", [128, 512], st)

    lnstack = contextlib.ExitStack()
    ln = emit_ln(C, f"{tag}ln", NCH,
                 lambda k: out_x[k * 128:(k + 1) * 128, :],
                 xn_dram, HALO, [], stack=lnstack, prev=prev)
    lnstack.close()

    s_w = C.sem(f"{tag}_w")
    s_mi = C.sem(f"{tag}_mi")
    s_tl = [C.sem(f"{tag}_tl{i}") for i in range(2)]
    s_xl = [C.sem(f"{tag}_xl{i}") for i in range(2)]
    s_mm = C.sem(f"{tag}_mm")
    s_a = C.sem(f"{tag}_a")
    s_v = C.sem(f"{tag}_v")
    s_sg = [C.sem(f"{tag}_sg{i}") for i in range(2)]

    def ldg(g):
        prev.barrier(g)
        s_w.inc(g.dma_start(out=gw[:], in_=gate_wT), 16)
        s_w.inc(g.dma_start(out=mfl[:], in_=mfill), 16)

    block.gpsimd(ldg)

    def mini(sy):
        sy.wait_ge(s_w.h, 32)
        s_mi.inc(sy.dma_start(
            out=seg_meta.rearrange("(n p) c -> p n c", p=128),
            in_=mfl[:].rearrange("p (n c) -> p n c", c=2)), 16)

    block.sync(mini)

    small_free = [0, 0]
    xnc_free = [0, 0]

    for k in range(NCH):
        bi = k % 2

        def tld(sy, k=k, bi=bi, gate=xnc_free[bi], sgate=small_free[bi]):
            ln.chunk_barrier(sy, k)
            if gate:
                sy.wait_ge(s_mm.h, gate)
            if sgate:
                sy.wait_ge(s_sg[bi].h, sgate)
            s_tl[bi].inc(sy.dma_start(
                out=xnc[bi][:].rearrange("p (dc r) -> p dc r", dc=16),
                in_=xn_dram[HALO + k * 128: HALO + (k + 1) * 128, :]
                    .rearrange("r (dc p) -> r dc p", p=128),
                transpose=True), 16)
            s_xl[bi].inc(sy.dma_start(
                out=xsc[bi][:],
                in_=xn_dram[HALO + k * 128: HALO + (k + 1) * 128, :]), 16)

        block.sync(tld)
        tln = s_tl[bi].n

        def gmm(t, k=k, bi=bi, tln=tln, an=s_a.n, vn=s_v.n):
            t.wait_ge(s_tl[bi].h, tln)
            if k == 0:
                t.wait_ge(s_w.h, 32)
            if an:
                t.wait_ge(s_a.h, an)
            x3 = xnc[bi][:].rearrange("p (dc r) -> p dc r", dc=16)
            g3 = gw[:].rearrange("p (dc e) -> p dc e", dc=16)
            for dc in range(16):
                ins = t.matmul(pg[0:128, 0:8], x3[:, dc, :], g3[:, dc, :],
                               start=(dc == 0), stop=(dc == 15))
            s_mm.inc(ins)

        block.tensor(gmm)
        gmn = s_mm.n
        xnc_free[bi] = gmn

        def gev(s, gmn=gmn):
            s.wait_ge(s_mm.h, gmn)
            s_a.inc(s.copy(gsc[:], pg[0:128, 0:8]))

        block.scalar(gev)
        gan = s_a.n

        def rt1(v, gan=gan, vprior=s_v.n):
            v.wait_ge(s_v.h, vprior)
            v.wait_ge(s_a.h, gan)
            s_v.inc(v.reduce_max(sm1[:], gsc[:], AX.X))
            v.wait_ge(s_v.h, s_v.n)
            s_v.inc(v.tensor_scalar(nmx[:], sm1[:], -1.0, None, ALU.mult))

        block.vector(rt1)
        vn1 = s_v.n

        def rt2(s, vn1=vn1):
            s.wait_ge(s_v.h, vn1)
            s_a.inc(s.activation(gex[:], gsc[:], AF.Exp, bias=nmx[:]))

        block.scalar(rt2)
        an2 = s_a.n

        def rt3(v, an2=an2, k=k, vprior=s_v.n):
            v.wait_ge(s_v.h, vprior)
            v.wait_ge(s_a.h, an2)
            s_v.inc(v.tensor_reduce(rs[:], gex[:], AX.X, ALU.add))
            v.wait_ge(s_v.h, s_v.n)
            s_v.inc(v.reciprocal(rc1[:], rs[:]))
            v.wait_ge(s_v.h, s_v.n)
            s_v.inc(v.tensor_scalar(probs[:], gex[:], rc1[:], None, ALU.mult))
            v.wait_ge(s_v.h, s_v.n)
            s_v.inc(v.reduce_max(sm1[:], probs[:], AX.X))
            v.wait_ge(s_v.h, s_v.n)
            s_v.inc(v.tensor_scalar(s0t[:], probs[:], sm1[:], None, ALU.is_ge))
            v.wait_ge(s_v.h, s_v.n)
            s_v.inc(v.scalar_tensor_tensor(gst[:], s0t[:], -1.0e30, probs[:],
                                           ALU.mult, ALU.add))
            v.wait_ge(s_v.h, s_v.n)
            s_v.inc(v.reduce_max(sm2[:], gst[:], AX.X))
            v.wait_ge(s_v.h, s_v.n)
            s_v.inc(v.tensor_scalar(rt[:], probs[:], sm2[:], None, ALU.is_ge))
            v.wait_ge(s_v.h, s_v.n)
            s_v.inc(v.tensor_sub(s1t[:], rt[:], s0t[:]))
            s_v.inc(v.tensor_copy(rb16[:], rt[:]))

        block.vector(rt3)
        vn3 = s_v.n

        def lmm(t, vn3=vn3, an=s_a.n):
            t.wait_ge(s_v.h, vn3)
            if an > 2:
                t.wait_ge(s_a.h, an)
            s_mm.inc(t.matmul(pl[0:128, 0:8], ltri[:], rb16[:],
                              start=True, stop=True))
            s_mm.inc(t.matmul(pt[0:1, 0:8], onec[:], rb16[:],
                              start=True, stop=True))

        block.tensor(lmm)
        lmn = s_mm.n

        def lev(s, lmn=lmn, k=k):
            s.wait_ge(s_mm.h, lmn)
            s_a.inc(s.copy(lc[:], pl[0:128, 0:8]))
            s_a.inc(s.copy(toti[0:1, k * 8:(k + 1) * 8], pt[0:1, 0:8]))

        block.scalar(lev)
        lan = s_a.n

        def pfx(v, lan=lan, k=k, vprior=s_v.n):
            v.wait_ge(s_v.h, vprior)
            v.wait_ge(s_a.h, lan)
            if k == 0:
                s_v.inc(v.memset(offs[0:1, 0:8], 0.0))
            else:
                s_v.inc(v.tensor_add(offs[0:1, k * 8:(k + 1) * 8],
                                     offs[0:1, (k - 1) * 8:k * 8],
                                     toti[0:1, (k - 1) * 8:k * 8]))
            v.wait_ge(s_v.h, s_v.n)
            s_v.inc(v.tensor_add(obk[:], offs[0:1, k * 8:(k + 1) * 8],
                                 base_r[:]))

        block.vector(pfx)
        pfn = s_v.n

        def bmm(t, pfn=pfn, vn=s_v.n):
            t.wait_ge(s_v.h, pfn)
            s_mm.inc(t.matmul(pbk[0:128, 0:8], oner[:], obk[:],
                              start=True, stop=True))

        block.tensor(bmm)
        bmn = s_mm.n

        def dcmp(v, bmn=bmn, k=k, bi=bi, gate=small_free[bi], vprior=s_v.n):
            v.wait_ge(s_v.h, vprior)
            v.wait_ge(s_mm.h, bmn)
            s_v.inc(v.tensor_copy(obB[:], pbk[0:128, 0:8]))
            v.wait_ge(s_v.h, s_v.n)
            s_v.inc(v.tensor_add(dst[:], lc[:], obB[:]))
            v.wait_ge(s_v.h, s_v.n)
            s_v.inc(v.tensor_scalar(gst[:], dst[:], 1.0e5, None, ALU.subtract))
            v.wait_ge(s_v.h, s_v.n)
            s_v.inc(v.tensor_mul(gst[:], gst[:], rt[:]))
            v.wait_ge(s_v.h, s_v.n)
            s_v.inc(v.tensor_scalar(dst[:], gst[:], 1.0e5, None, ALU.add))
            v.wait_ge(s_v.h, s_v.n)
            if gate:
                v.wait_ge(s_sg[bi].h, gate)
            s_v.inc(v.tensor_mul(gst[:], s0t[:], dst[:]))
            v.wait_ge(s_v.h, s_v.n)
            s_v.inc(v.tensor_reduce(dd[:, 0:1], gst[:], AX.X, ALU.add))
            v.wait_ge(s_v.h, s_v.n)
            s_v.inc(v.tensor_mul(gst[:], s1t[:], dst[:]))
            v.wait_ge(s_v.h, s_v.n)
            s_v.inc(v.tensor_reduce(dd[:, 1:2], gst[:], AX.X, ALU.add))
            v.wait_ge(s_v.h, s_v.n)
            s_v.inc(v.tensor_copy(di0[bi][:], dd[:, 0:1]))
            s_v.inc(v.tensor_copy(di1[bi][:], dd[:, 1:2]))
            s_v.inc(v.tensor_scalar(mv0[bi][:, 0:1], tokf[:, k:k + 1],
                                    0.0, None, ALU.add))
            s_v.inc(v.tensor_scalar(mv1[bi][:, 0:1], tokf[:, k:k + 1],
                                    float(TOK), None, ALU.add))
            s_v.inc(v.tensor_copy(mv0[bi][:, 1:2], sm1[:]))
            s_v.inc(v.tensor_copy(mv1[bi][:, 1:2], sm2[:]))

        block.vector(dcmp)
        dcn = s_v.n

        def sct(g, k=k, bi=bi, dcn=dcn, xln=s_xl[bi].n, min_=s_mi.n):
            g.wait_ge(s_v.h, dcn)
            g.wait_ge(s_xl[bi].h, xln)
            if k < 2:
                g.wait_ge(s_mi.h, min_)
                if tag == "m2":
                    g.wait_ge(io["s_zi"].h, io["zi_n"])
            s_sg[bi].inc(g.indirect_dma_start(
                seg_x, bass.IndirectOffsetOnAxis(di0[bi][:], 0),
                xsc[bi][:], None,
                bounds_check=io['bnd_seg'], oob_is_err=False), 16)
            s_sg[bi].inc(g.indirect_dma_start(
                seg_x, bass.IndirectOffsetOnAxis(di1[bi][:], 0),
                xsc[bi][:], None,
                bounds_check=io['bnd_seg'], oob_is_err=False), 16)
            s_sg[bi].inc(g.indirect_dma_start(
                seg_meta, bass.IndirectOffsetOnAxis(di0[bi][:], 0),
                mv0[bi][:], None,
                bounds_check=io['bnd_seg'], oob_is_err=False), 16)

        block.gpsimd(sct)

        def sct2(g, bi=bi):
            s_sg[bi].inc(g.indirect_dma_start(
                seg_meta, bass.IndirectOffsetOnAxis(di1[bi][:], 0),
                mv1[bi][:], None,
                bounds_check=io['bnd_seg'], oob_is_err=False), 16)

        block.gpsimd(sct2)
        small_free[bi] = s_sg[bi].n

    SG = [s_sg[0].n, s_sg[1].n]
    st.close()

    stE = contextlib.ExitStack()
    hT = C.sbuf(f"{tag}hT", [128, 16 * CAP], BF16, stE)
    sil = C.sbuf(f"{tag}sil", [128, 512], BF16, stE)
    s_xg = C.sem(f"{tag}_xg")
    s_wc = [C.sem(f"{tag}_wc{i}") for i in range(2)]
    s_w2 = C.sem(f"{tag}_w2")
    s_he = C.sem(f"{tag}_he")
    s_hs = C.sem(f"{tag}_hs")
    s_ml = [C.sem(f"{tag}_ml{i}") for i in range(2)]
    s_ye = C.sem(f"{tag}_ye")
    s_ysc = [C.sem(f"{tag}_ysc{i}") for i in range(2)]

    ph1 = C.psum(f"{tag}ph1", [128, 512], stE)
    ph3 = C.psum(f"{tag}ph3", [128, 512], stE)
    py = [C.psum(f"{tag}py{i}", [128, 512], stE) for i in range(2)]

    for e in range(NE):
        stX = contextlib.ExitStack()
        xgT = C.sbuf(f"{tag}xg{e}", [128, 16 * CAP], BF16, stX)
        w1cb = [C.sbuf(f"{tag}w1c{e}{i}", [128, 16 * 128], BF16, stX)
                for i in range(2)]
        w3cb = [C.sbuf(f"{tag}w3c{e}{i}", [128, 16 * 128], BF16, stX)
                for i in range(2)]

        def xgl(sy, e=e, hen=s_he.n, mmn=s_mm.n,
                y0=s_ysc[0].n, y1=s_ysc[1].n):
            if e == 0:
                sy.wait_ge(s_sg[0].h, SG[0])
                sy.wait_ge(s_sg[1].h, SG[1])
            if hen:
                sy.wait_ge(s_he.h, hen)
            if e > 0:
                sy.wait_ge(s_mm.h, mmn)
                sy.wait_ge(s_ysc[0].h, y0)
                sy.wait_ge(s_ysc[1].h, y1)
            s_xg.inc(sy.dma_start(
                out=xgT[:].rearrange("p (dc r) -> p dc r", dc=16),
                in_=seg_x[e * CAP:(e + 1) * CAP, :]
                    .rearrange("r (dc p) -> r dc p", p=128),
                transpose=True), 16)

        block.sync(xgl)
        xgn = s_xg.n

        for fc in range(16):
            wb = fc % 2

            def wld(g, e=e, fc=fc, wb=wb, gate=s_mm.n,
                    y0=s_ysc[0].n, y1=s_ysc[1].n):
                if fc >= 2:
                    g.wait_ge(s_mm.h, gate)
                if e > 0 and fc < 2:
                    g.wait_ge(s_ysc[0].h, y0)
                    g.wait_ge(s_ysc[1].h, y1)
                s_wc[wb].inc(g.dma_start(out=w1cb[wb][:],
                                         in_=w1T[e, fc]), 16)
                s_wc[wb].inc(g.dma_start(out=w3cb[wb][:],
                                         in_=w3T[e, fc]), 16)

            block.gpsimd(wld)
            wcn = s_wc[wb].n

            for rc in range(CAP // 512):
                def hmm(t, fc=fc, rc=rc, wb=wb, wcn=wcn, xgn=xgn,
                        heg=s_he.n, hsg=s_hs.n):
                    t.wait_ge(s_wc[wb].h, wcn)
                    if fc == 0 and rc == 0:
                        t.wait_ge(s_xg.h, xgn)
                    if hsg:
                        t.wait_ge(s_hs.h, hsg)
                    if heg:
                        t.wait_ge(s_he.h, heg)
                    w13 = w1cb[wb][:].rearrange("p (dc f) -> p dc f", dc=16)
                    w33 = w3cb[wb][:].rearrange("p (dc f) -> p dc f", dc=16)
                    x3 = xgT[:].rearrange("p (dc r) -> p dc r", dc=16)
                    for dc in range(16):
                        t.matmul(ph1[:], w13[:, dc, :],
                                 x3[:, dc, rc * 512:(rc + 1) * 512],
                                 start=(dc == 0), stop=(dc == 15))
                    for dc in range(16):
                        ins = t.matmul(ph3[:], w33[:, dc, :],
                                       x3[:, dc, rc * 512:(rc + 1) * 512],
                                       start=(dc == 0), stop=(dc == 15))
                    s_mm.inc(ins)

                block.tensor(hmm)
                hmn = s_mm.n

                def hsl(s, hmn=hmn):
                    s.wait_ge(s_mm.h, hmn)
                    s_hs.inc(s.activation(sil[:], ph1[:], AF.Silu))

                block.scalar(hsl)
                hsn = s_hs.n

                def hev(v, hsn=hsn, fc=fc, rc=rc):
                    v.wait_ge(s_hs.h, hsn)
                    s_he.inc(v.tensor_mul(
                        hT[:, fc * CAP + rc * 512: fc * CAP + (rc + 1) * 512],
                        ph3[:], sil[:]))

                block.vector(hev)

        HDONE = s_he.n
        stX.close()
        stW2 = contextlib.ExitStack()
        w2sb = C.sbuf(f"{tag}w2{e}", [128, 16 * D], BF16, stW2)
        ysb = [C.sbuf(f"{tag}ys{e}{i}", [128, D], F32, stW2) for i in range(2)]
        mrow = [C.sbuf(f"{tag}mr{e}{i}", [128, 2], F32, stW2) for i in range(2)]
        dix = [C.sbuf(f"{tag}dx{e}{i}", [128, 1], I32, stW2) for i in range(2)]

        def w2l(g, e=e, hd=HDONE):
            g.wait_ge(s_he.h, hd)
            for sl in range(4):
                s_w2.inc(g.dma_start(
                    out=w2sb[:, sl * 8192:(sl + 1) * 8192],
                    in_=w2T[e][:, sl * 8192:(sl + 1) * 8192]), 16)

        block.gpsimd(w2l)
        w2n = s_w2.n

        for rc in range(CAP // 128):
            bi = rc % 2

            def mld(sy, e=e, rc=rc, bi=bi, gate=s_ysc[bi].n, min_=s_mi.n):
                if gate:
                    sy.wait_ge(s_ysc[bi].h, gate)
                sy.wait_ge(s_mi.h, min_)
                s_ml[bi].inc(sy.dma_start(
                    out=mrow[bi][:],
                    in_=seg_meta[e * CAP + rc * 128: e * CAP + (rc + 1) * 128, :]),
                    16)

            block.sync(mld)
            mln = s_ml[bi].n

            def mcv(v, bi=bi, mln=mln, vprior=s_v.n):
                v.wait_ge(s_v.h, vprior)
                v.wait_ge(s_ml[bi].h, mln)
                s_v.inc(v.tensor_copy(dix[bi][:], mrow[bi][:, 0:1]))

            block.vector(mcv)
            mcn = s_v.n

            for db in range(4):
                i = 4 * rc + db

                def ymm(t, rc=rc, db=db, i=i, w2n=w2n, yeg=s_ye.n):
                    if i == 0:
                        t.wait_ge(s_w2.h, w2n)
                    if yeg >= 2:
                        t.wait_ge(s_ye.h, yeg - 1)
                    w23 = w2sb[:].rearrange("p (Fc d) -> p Fc d", Fc=16)
                    h3 = hT[:].rearrange("p (Fc r) -> p Fc r", Fc=16)
                    for Fc in range(16):
                        ins = t.matmul(py[i % 2][:],
                                       h3[:, Fc, rc * 128:(rc + 1) * 128],
                                       w23[:, Fc, db * 512:(db + 1) * 512],
                                       start=(Fc == 0), stop=(Fc == 15))
                    s_mm.inc(ins)

                block.tensor(ymm)
                ymn = s_mm.n

                def yev(v, bi=bi, db=db, i=i, ymn=ymn, mcn=mcn):
                    v.wait_ge(s_mm.h, ymn)
                    v.wait_ge(s_v.h, mcn)
                    s_ye.inc(v.tensor_scalar(
                        ysb[bi][:, db * 512:(db + 1) * 512], py[i % 2][:],
                        mrow[bi][:, 1:2], None, ALU.mult))

                block.vector(yev)

            yen = s_ye.n

            def ysct(g, bi=bi, yen=yen):
                g.wait_ge(s_ye.h, yen)
                s_ysc[bi].inc(g.indirect_dma_start(
                    y01, bass.IndirectOffsetOnAxis(dix[bi][:], 0),
                    ysb[bi][:], None,
                    bounds_check=io['bnd_y'], oob_is_err=False), 16)

            block.gpsimd(ysct)

        stW2.close()

    YS = [s_ysc[0].n, s_ysc[1].n]
    stE.close()

    stR = contextlib.ExitStack()
    y0b = [C.sbuf(f"{tag}y0{i}", [128, D], F32, stR) for i in range(2)]
    y1b = [C.sbuf(f"{tag}y1{i}", [128, D], F32, stR) for i in range(2)]
    s_yl = [C.sem(f"{tag}_yl{i}") for i in range(2)]
    s_ra = C.sem(f"{tag}_ra")
    s_res = [C.sem(f"{tag}_res{i}") for i in range(2)]
    res_at = [0, 0]

    for k in range(NCH):
        bi = k % 2

        def yld(sy, k=k, bi=bi, gate=res_at[bi]):
            if k == 0:
                sy.wait_ge(s_ysc[0].h, YS[0])
                sy.wait_ge(s_ysc[1].h, YS[1])
            if gate:
                sy.wait_ge(s_res[bi].h, gate)
            s_yl[bi].inc(sy.dma_start(out=y0b[bi][:],
                                      in_=y01[k * 128:(k + 1) * 128, :]), 16)
            s_yl[bi].inc(sy.dma_start(
                out=y1b[bi][:], in_=y01[TOK + k * 128: TOK + (k + 1) * 128, :]),
                16)

        block.sync(yld)
        yln = s_yl[bi].n

        def radd(v, bi=bi, yln=yln):
            v.wait_ge(s_yl[bi].h, yln)
            s_ra.inc(v.tensor_add(y0b[bi][:], y0b[bi][:], y1b[bi][:]))

        block.vector(radd)
        ran = s_ra.n

        def racc(g, k=k, bi=bi, ran=ran):
            g.wait_ge(s_ra.h, ran)
            s_res[bi].inc(g.dma_start(
                out=out_x[k * 128:(k + 1) * 128, :],
                in_=y0b[bi][:], accum_op=ALU.add), 16)

        block.gpsimd(racc)
        res_at[bi] = s_res[bi].n

    stR.close()
    R = [s_res[0].n, s_res[1].n]

    class ResOut:
        def barrier(self, eng):
            eng.wait_ge(s_res[0].h, R[0])
            eng.wait_ge(s_res[1].h, R[1])

    return ResOut()




def emit_attn_b(C, io, prev):
    block = C.block
    out_x, xn_dram, q_dram = io["out_x"], io["xn_dram"], io["q_dram"]
    wq_b, wkv_b, wo_b = io["wq_b"], io["wkv_b"], io["wo_b"]
    kv_contrib, kv_all, kv_win = io["kv_contrib"], io["kv_all"], io["kv_win"]
    oner, onec, onerb = io["oner"], io["onec"], io["onerb"]
    bo = io["bo_b_sb"]

    stP = contextlib.ExitStack()
    comp = C.sbuf("b3comp", [128, NCH], F32, stP)
    ropes = C.sbuf("b3rsin", [128, 16 * 64], F32, stP)
    ropec = C.sbuf("b3rcos", [128, 16 * 64], F32, stP)
    wrow = C.sbuf("b3w", [128, 16 * 2048], BF16, stP)
    xnc = [C.sbuf(f"b3xnc{i}", [128, 16 * 128], BF16, stP) for i in range(2)]
    qf = C.sbuf("b3qf", [128, 2048], F32, stP)
    t1 = C.sbuf("b3t1", [128, 64], F32, stP)
    t2 = C.sbuf("b3t2", [128, 64], F32, stP)
    orow = [C.sbuf(f"b3or{i}", [128, 2048], BF16, stP) for i in range(2)]

    pq = [C.psum("b3pq%d" % i, [128, 512], stP) for i in range(2)]

    s_rl = C.sem("b3_rl")
    s_wl = C.sem("b3_wl")
    s_tl = [C.sem(f"b3_tl{i}") for i in range(2)]
    s_mm = C.sem("b3_mm")
    s_a = C.sem("b3_a")
    s_v = C.sem("b3_v")
    s_p = C.sem("b3_p")
    s_st = [C.sem(f"b3_st{i}") for i in range(2)]
    s_cc = C.sem("b3_cc")
    s_wg = C.sem("b3_wg")
    s_ws = C.sem("b3_ws")

    lnstack = contextlib.ExitStack()
    ln = emit_ln(C, "b3ln", NCH,
                 lambda k: out_x[k * 128:(k + 1) * 128, :],
                 xn_dram, HALO, [], comp_out=comp, stack=lnstack, prev=prev)
    lnstack.close()

    def ldr(sy):
        s_rl.inc(sy.dma_start(out=ropes[:], in_=io["rope_sin"]), 16)
        s_rl.inc(sy.dma_start(out=ropec[:], in_=io["rope_cos"]), 16)

    block.sync(ldr)

    orow_st = [0, 0]
    for pas in range(2):
        def ldwb(g, pas=pas, mmn=s_mm.n):
            if pas == 0:
                ln.barrier(g)
            else:
                g.wait_ge(s_mm.h, mmn)
            s_wl.inc(g.dma_start(out=wrow[:],
                                 in_=(wq_b if pas == 0 else wkv_b)), 16)

        block.gpsimd(ldwb)
        wn = s_wl.n

        for k in range(NCH):
            bi = k % 2

            def tldb(sy, k=k, bi=bi, pas=pas, gate=s_mm.n):
                if k >= 2 or pas == 1:
                    sy.wait_ge(s_mm.h, gate)
                ln.chunk_barrier(sy, k)
                s_tl[bi].inc(sy.dma_start(
                    out=xnc[bi][:].rearrange("p (dc r) -> p dc r", dc=16),
                    in_=xn_dram[HALO + k * 128: HALO + (k + 1) * 128, :]
                        .rearrange("r (dc p) -> r dc p", p=128),
                    transpose=True), 16)

            block.sync(tldb)
            tln = s_tl[bi].n

            ev_at = [0, 0]
            for fb in range(4):
                def qmmf(t, k=k, bi=bi, fb=fb, tln=tln, wn=wn,
                         gate=ev_at[fb % 2]):
                    if fb == 0:
                        t.wait_ge(s_tl[bi].h, tln)
                        t.wait_ge(s_wl.h, wn)
                    if gate:
                        t.wait_ge(s_v.h, gate)
                    x3 = xnc[bi][:].rearrange("p (dc r) -> p dc r", dc=16)
                    w3 = wrow[:].rearrange("p (dc f) -> p dc f", dc=16)
                    for dc in range(16):
                        ins = t.matmul(pq[fb % 2][:], x3[:, dc, :],
                                       w3[:, dc, fb * 512:(fb + 1) * 512],
                                       start=(dc == 0), stop=(dc == 15))
                    s_mm.inc(ins)

                block.tensor(qmmf)
                mmn = s_mm.n

                if pas == 0:
                    def qevf(v, k=k, fb=fb, mmn=mmn, vprior=s_v.n):
                        v.wait_ge(s_v.h, vprior)
                        v.wait_ge(s_mm.h, mmn)
                        s_v.inc(v.tensor_scalar(
                            qf[:, fb * 512:(fb + 1) * 512], pq[fb % 2][:],
                            comp[:, k:k + 1], None, ALU.mult))
                else:
                    def qevf(v, k=k, fb=fb, mmn=mmn, vprior=s_v.n,
                             og=orow_st[k % 2]):
                        v.wait_ge(s_v.h, vprior)
                        v.wait_ge(s_mm.h, mmn)
                        if fb < 2:
                            s_v.inc(v.tensor_copy(
                                qf[:, fb * 512:(fb + 1) * 512], pq[fb % 2][:]))
                        else:
                            if fb == 2 and og:
                                v.wait_ge(s_st[k % 2].h, og)
                            s_v.inc(v.tensor_copy(
                                orow[k % 2][:, fb * 512:(fb + 1) * 512],
                                pq[fb % 2][:]))

                block.vector(qevf)
                ev_at[fb % 2] = s_v.n

            if pas == 0:
                def qev(v, k=k, vprior=s_v.n, og=orow_st[k % 2]):
                    v.wait_ge(s_v.h, vprior)
                    v.wait_ge(s_rl.h, 32)
                    if og:
                        v.wait_ge(s_st[k % 2].h, og)
                    for h in range(16):
                        q1 = qf[:, h * 128:h * 128 + 64]
                        q2 = qf[:, h * 128 + 64:h * 128 + 128]
                        ck = ropec[:, k * 64:(k + 1) * 64]
                        sk = ropes[:, k * 64:(k + 1) * 64]
                        v.wait_ge(s_v.h, s_v.n)
                        s_v.inc(v.tensor_mul(t1[:], q1, ck))
                        s_v.inc(v.tensor_mul(t2[:], q2, sk))
                        v.wait_ge(s_v.h, s_v.n)
                        s_v.inc(v.tensor_sub(
                            orow[k % 2][:, h * 128:h * 128 + 64], t1[:], t2[:]))
                        v.wait_ge(s_v.h, s_v.n)
                        s_v.inc(v.tensor_mul(t1[:], q1, sk))
                        s_v.inc(v.tensor_mul(t2[:], q2, ck))
                        v.wait_ge(s_v.h, s_v.n)
                        s_v.inc(v.tensor_add(
                            orow[k % 2][:, h * 128 + 64:h * 128 + 128],
                            t1[:], t2[:]))

                block.vector(qev)
                evn = s_v.n

                def qst(sy, k=k, evn=evn, bi=k % 2, stn=s_st[k % 2].n):
                    sy.wait_ge(s_v.h, evn)
                    s_st[bi].inc(sy.dma_start(
                        out=q_dram[k * 128:(k + 1) * 128, :],
                        in_=orow[bi][:]), 16)

                block.sync(qst)
                orow_st[k % 2] = s_st[k % 2].n
            else:
                def kev(v, k=k, vprior=s_v.n, og=orow_st[k % 2]):
                    v.wait_ge(s_v.h, vprior)
                    v.wait_ge(s_rl.h, 32)
                    if og:
                        v.wait_ge(s_st[k % 2].h, og)
                    for h in range(8):
                        q1 = qf[:, h * 128:h * 128 + 64]
                        q2 = qf[:, h * 128 + 64:h * 128 + 128]
                        ck = ropec[:, k * 64:(k + 1) * 64]
                        sk = ropes[:, k * 64:(k + 1) * 64]
                        v.wait_ge(s_v.h, s_v.n)
                        s_v.inc(v.tensor_mul(t1[:], q1, ck))
                        s_v.inc(v.tensor_mul(t2[:], q2, sk))
                        v.wait_ge(s_v.h, s_v.n)
                        s_v.inc(v.tensor_sub(
                            orow[k % 2][:, h * 128:h * 128 + 64], t1[:], t2[:]))
                        v.wait_ge(s_v.h, s_v.n)
                        s_v.inc(v.tensor_mul(t1[:], q1, sk))
                        s_v.inc(v.tensor_mul(t2[:], q2, ck))
                        v.wait_ge(s_v.h, s_v.n)
                        s_v.inc(v.tensor_add(
                            orow[k % 2][:, h * 128 + 64:h * 128 + 128],
                            t1[:], t2[:]))

                block.vector(kev)
                evn = s_v.n

                def kst(sy, k=k, evn=evn, bi=k % 2):
                    sy.wait_ge(s_v.h, evn)
                    s_st[bi].inc(sy.dma_start(
                        out=kv_contrib[k * 128:(k + 1) * 128, :],
                        in_=orow[bi][:]), 16)

                block.sync(kst)
                orow_st[k % 2] = s_st[k % 2].n

    QKV_ST = [s_st[0].n, s_st[1].n]

    dump = _os2.environ.get("DEV_B_DUMP")
    if dump in ("q", "kv"):
        srcd = q_dram if dump == "q" else kv_contrib
        s_dp = C.sem("b3_dp")

        def dmp(g):
            g.wait_ge(s_st[0].h, QKV_ST[0])
            g.wait_ge(s_st[1].h, QKV_ST[1])
            for r in range(16):
                s_dp.inc(g.dma_start(
                    out=out_x[r * 128:(r + 1) * 128, :],
                    in_=srcd[r * 128:(r + 1) * 128, :]), 16)

        block.gpsimd(dmp)
        DN = s_dp.n

        class _O:
            def barrier(self, eng):
                eng.wait_ge(s_dp.h, DN)
        stP.close()
        return _O()

    gbuf = C.sbuf("b3gb", [128, 2048], BF16, stP)
    widx = C.sbuf("b3wi", [128, (TOK + 640) // 128], I32, stP)
    s_wi = C.sem("b3_wi")

    def ldwi(sy):
        s_wi.inc(sy.dma_start(out=widx[:], in_=io["kvwin_idx"]), 16)

    block.sync(ldwi)

    if _os2.environ.get("DEV_NO_CC"):
        def coll(g):
            g.wait_ge(s_st[0].h, QKV_ST[0])
            g.wait_ge(s_st[1].h, QKV_ST[1])
            for rr in range(4):
                s_cc.inc(g.dma_start(out=kv_all[rr * TOK:(rr + 1) * TOK, :],
                                     in_=kv_contrib), 16)

        block.gpsimd(coll)
    else:
        def coll(g):
            g.wait_ge(s_st[0].h, QKV_ST[0])
            g.wait_ge(s_st[1].h, QKV_ST[1])
            s_cc.inc(g.collective_compute(
                "AllGather", ALU.bypass,
                replica_groups=[[0, 1, 2, 3], [4, 5, 6, 7]],
                ins=[kv_contrib.opt()], outs=[kv_all.opt()]))

        block.gpsimd(coll)

    WINC = (TOK + 640) // 128
    for r in range(WINC):
        def wgat(g, r=r, wsn=s_ws.n):
            if r == 0:
                g.wait_ge(s_cc.h, s_cc.n)
                g.wait_ge(s_wi.h, 16)
            else:
                g.wait_ge(s_ws.h, wsn)
            s_wg.inc(g.indirect_dma_start(
                gbuf[:], None, kv_all,
                bass.IndirectOffsetOnAxis(widx[:, r:r + 1], 0)), 16)

        block.gpsimd(wgat)
        wgn = s_wg.n

        def wsto(sy, r=r, wgn=wgn):
            sy.wait_ge(s_wg.h, wgn)
            s_ws.inc(sy.dma_start(
                out=kv_win[r * 128:(r + 1) * 128, :], in_=gbuf[:]), 16)

        block.sync(wsto)

    WS_ALL = s_ws.n

    if _os2.environ.get("DEV_B_DUMP") == "kvw":
        s_dp = C.sem("b3_dp")

        def dmpw(g):
            g.wait_ge(s_ws.h, WS_ALL)
            for r in range(16):
                s_dp.inc(g.dma_start(
                    out=out_x[r * 128:(r + 1) * 128, :],
                    in_=kv_win[r * 128:(r + 1) * 128, :]), 16)

        block.gpsimd(dmpw)
        DNW = s_dp.n

        class _OW:
            def barrier(self, eng):
                eng.wait_ge(s_dp.h, DNW)
        stP.close()
        return _OW()

    if _os2.environ.get("DEV_B_DUMP") == "kva":
        s_dp = C.sem("b3_dp")

        def dmpk(g):
            g.wait_ge(s_ws.h, WS_ALL)
            for r in range(16):
                s_dp.inc(g.dma_start(
                    out=out_x[r * 128:(r + 1) * 128, :],
                    in_=kv_all[TOK + r * 128: TOK + (r + 1) * 128, :]), 16)

        block.gpsimd(dmpk)
        DNK = s_dp.n

        class _OK:
            def barrier(self, eng):
                eng.wait_ge(s_dp.h, DNK)
        stP.close()
        return _OK()
    stP.close()

    att_dram = io["seg_x"]
    stA = contextlib.ExitStack()
    att_st = [C.sbuf(f"b3as{i}", [128, 256], BF16, stA) for i in range(2)]
    kTh = [C.sbuf(f"b3kT{i}", [128, SEQ], BF16, stA) for i in range(2)]
    vth = [C.sbuf(f"b3vt{i}", [128, SEQ], BF16, stA) for i in range(2)]
    kTw = [C.sbuf(f"b3kTw{i}", [128, TOK + 640], BF16, stA) for i in range(2)]
    vtw = [C.sbuf(f"b3vtw{i}", [128, TOK + 640], BF16, stA) for i in range(2)]
    qTt = [C.sbuf(f"b3qT{i}", [128, 256], BF16, stA) for i in range(2)]
    tabs = C.sbuf("b3tab", [3, 68 * 256], F32, stA)
    ramph = C.sbuf("b3ramp", [3, 16 * 128], F32, stA)
    killj = C.sbuf("b3kj", [128, 1], F32, stA)
    mrel = C.sbuf("b3mrel", [128, 4 * 256], F32, stA)
    ub64 = C.sbuf("b3ub", [128, 64], F32, stA)
    mb64 = C.sbuf("b3mb", [128, 64], F32, stA)
    urow = C.sbuf("b3ur", [1, 64], F32, stA)
    usum = C.sbuf("b3us", [1, 64], F32, stA)
    sumv8 = C.sbuf("b3sv", [128, 2], F32, stA)
    P = C.sbuf("b3P", [128, 4 * 512], BF16, stA)
    sum_sb = C.sbuf("b3sum", [1, 256], F32, stA)
    sum2 = C.sbuf("b3sum2", [1, 256], F32, stA)
    rec_sb = C.sbuf("b3rec", [1, 256], F32, stA)
    recB = C.sbuf("b3recB", [128, 256], F32, stA)
    o1t = C.sbuf("b3o1", [128, 64], F32, stA)

    psc = [C.psum(f"b3psc{i}", [128, 512], stA) for i in range(4)]
    pat = C.psum("b3pat", [128, 512], stA)
    psm = C.psum("b3psm", [128, 512], stA)
    prb = C.psum("b3prb", [128, 512], stA)
    psv = C.psum("b3psv", [128, 512], stA)

    s_kl = [C.sem(f"b3_kl{i}") for i in range(2)]
    s_wk = [C.sem(f"b3_wk{i}") for i in range(2)]
    s_ql = [C.sem(f"b3_ql{i}") for i in range(2)]
    s_tb = [C.sem(f"b3_tb{i}") for i in range(2)]
    s_cl = C.sem("b3_cl")
    s_sc = C.sem("b3_sc")
    s_ex = C.sem("b3_ex")
    s_mk = C.sem("b3_mk")
    s_pv = C.sem("b3_pv")
    s_sv = C.sem("b3_sv")
    s_sve = C.sem("b3_sve")
    s_sm = C.sem("b3_sm")
    s_rc = C.sem("b3_rc")
    s_bc = C.sem("b3_bc")
    s_rb = C.sem("b3_rb")
    s_ev = C.sem("b3_ev")

    def ldcb(sy):
        sy.wait_ge(s_ws.h, WS_ALL)
        for dst, srcn in [(ramph[:], "ramp3h"), (killj[:], "kill_j64"),
                          (ub64[:], "uB64"),
                          (mb64[:], "mB64"), (urow[:], "uni_flag"),
                          (usum[:], "uni_sum")]:
            s_cl.inc(sy.dma_start(out=dst, in_=io[srcn]), 16)

    block.sync(ldcb)
    CLN = s_cl.n

    for kvh in range(8):
        bi = kvh % 2

        def klod(sy, kvh=kvh, bi=bi, gate=s_pv.n, svg=s_sv.n):
            if kvh < 2:
                sy.wait_ge(s_cl.h, CLN)
            if gate:
                sy.wait_ge(s_pv.h, gate)
            s_kl[bi].inc(sy.dma_start(
                out=kTh[bi][:].rearrange("p (o j) -> p o j", o=1),
                in_=kv_all[:, kvh * 128:(kvh + 1) * 128]
                    .rearrange("j (o p) -> j o p", o=1),
                transpose=True), 16)
            for q4 in range(4):
                s_kl[bi].inc(sy.dma_start(
                    out=vth[bi][:].rearrange("p (c d) -> p c d", c=64)
                        [:, q4 * 16:(q4 + 1) * 16, :],
                    in_=kv_all[q4 * 2048:(q4 + 1) * 2048,
                               1024 + kvh * 128:1024 + (kvh + 1) * 128]
                        .rearrange("(c p) d -> p c d", p=128)), 16)
            s_wk[bi].inc(sy.dma_start(
                out=kTw[bi][:].rearrange("p (o j) -> p o j", o=1),
                in_=kv_win[:, kvh * 128:(kvh + 1) * 128]
                    .rearrange("j (o p) -> j o p", o=1),
                transpose=True), 16)
            s_wk[bi].inc(sy.dma_start(
                out=vtw[bi][:].rearrange("p (c d) -> p c d", c=WINC),
                in_=kv_win[:, 1024 + kvh * 128:1024 + (kvh + 1) * 128]
                    .rearrange("(c p) d -> p c d", p=128)), 16)

        block.sync(klod)
        kln = s_kl[bi].n
        wkn = s_wk[bi].n

        def svmm(t, kvh=kvh, bi=bi, kln=kln, sveg=s_sve.n):
            t.wait_ge(s_kl[bi].h, kln)
            if sveg:
                t.wait_ge(s_sve.h, sveg)
            v3 = vth[bi][:].rearrange("p (c d) -> p c d", c=64)
            for c in range(64):
                ins = t.matmul(psv[:, 0:1], v3[:, c, :], onec[:],
                               start=(c == 0), stop=(c == 63))
            s_sv.inc(ins)

        block.tensor(svmm)
        svn = s_sv.n

        def svev(v, svn=svn, vprior=s_v.n):
            v.wait_ge(s_v.h, vprior)
            v.wait_ge(s_sv.h, svn)
            s_sve.inc(v.tensor_scalar(sumv8[:, 0:1], psv[:, 0:1],
                                      1.0 / SEQ, None, ALU.mult))

        block.vector(svev)
        sven = s_sve.n

        for hh in range(2):
            h = 2 * kvh + hh
            for jb in range(NQB):
                hj = (h, jb)
                qbi = jb % 2

                def qlod(sy, h=h, jb=jb, qbi=qbi, gate=s_sc.n, mkg=s_mk.n):
                    if gate:
                        sy.wait_ge(s_sc.h, gate)
                    if mkg:
                        sy.wait_ge(s_mk.h, mkg)
                    s_ql[qbi].inc(sy.dma_start(
                        out=qTt[qbi][:].rearrange("p (o j) -> p o j", o=1),
                        in_=q_dram[jb * 256:(jb + 1) * 256,
                                   h * 128:(h + 1) * 128]
                            .rearrange("j (o p) -> j o p", o=1),
                        transpose=True), 16)
                    s_tb[0].inc(sy.dma_start(
                        out=tabs[:].rearrange("r (c t) -> r c t", c=68),
                        in_=io["atab"][jb].rearrange("c r t -> r c t")), 16)
                    s_tb[1].inc(sy.dma_start(
                        out=mrel[:],
                        in_=io["maskrel"][:, jb * 1024:(jb + 1) * 1024]), 16)

                block.sync(qlod)
                qln = s_ql[qbi].n
                tbn = s_tb[0].n

                exp_pair = {}
                pv_at = {}
                for c in range(68):
                    p_ = c // 2
                    bk = p_ % 4
                    col = (c % 2) * 256

                    def cqkt(t, h=h, jb=jb, bi=bi, qbi=qbi, c=c, bk=bk,
                             col=col, qln=qln, tbn=tbn, kln=kln, wkn=wkn,
                             exg=exp_pair.get(c // 2 - 4, 0)):
                        if c == 0:
                            t.wait_ge(s_ql[qbi].h, qln)
                            t.wait_ge(s_tb[0].h, tbn)
                            t.wait_ge(s_kl[bi].h, kln)
                            t.wait_ge(s_wk[bi].h, wkn)
                        if exg:
                            t.wait_ge(s_ex.h, exg)
                        if c < 64:
                            lhs = kTh[bi][:, c * 128:(c + 1) * 128]
                        else:
                            r = c - 64
                            off = jb * 256 + (128, 256, 640, 768)[r]
                            lhs = kTw[bi][:, off:off + 128]
                        t.matmul(psc[bk][:, col:col + 256], lhs,
                                 qTt[qbi][:, 0:256], start=True, stop=False)
                        ins = t.matmul(psc[bk][:, col:col + 256],
                                       ramph[:, h * 128:(h + 1) * 128],
                                       tabs[:, c * 256:(c + 1) * 256],
                                       start=False, stop=True)
                        s_sc.inc(ins)

                    block.tensor(cqkt)
                    scn = s_sc.n

                    if c >= 64 and c % 2 == 1:
                        def cmka(v, jb=jb, c=c, bk=bk, scn=scn,
                                 vprior=s_v.n, mreln=s_tb[1].n):
                            v.wait_ge(s_v.h, vprior)
                            v.wait_ge(s_sc.h, scn)
                            r0 = c - 1 - 64
                            v.wait_ge(s_tb[1].h, mreln)
                            s_mk.inc(v.tensor_add(
                                psc[bk][:, 0:512],
                                psc[bk][:, 0:512],
                                mrel[:, r0 * 256:(r0 + 2) * 256]))

                        block.vector(cmka)
                    mka = s_mk.n

                    if c % 2 == 1:
                        def cexp(s, c=c, p_=p_, bk=bk, scn=scn, mka=mka,
                                 pvg=pv_at.get(2 * (p_ - 4) + 1, 0),
                                 pvp=s_pv.n):
                            s.wait_ge(s_sc.h, scn)
                            if c >= 64:
                                s.wait_ge(s_mk.h, mka)
                            if pvg:
                                s.wait_ge(s_pv.h, pvg)
                            elif p_ < 4 and pvp:
                                s.wait_ge(s_pv.h, pvp)
                            s_ex.inc(s.activation(
                                P[:, (p_ % 4) * 512:(p_ % 4 + 1) * 512],
                                psc[bk][:, 0:512], AF.Exp, scale=1.0 / SQD))

                        block.scalar(cexp)
                        exp_pair[p_] = s_ex.n

                        if c == 1:
                            def cmsk(v, exn=s_ex.n, p_=p_, vprior=s_v.n):
                                v.wait_ge(s_v.h, vprior)
                                v.wait_ge(s_ex.h, exn)
                                sl = (p_ % 4) * 512
                                s_mk.inc(v.tensor_scalar(
                                    P[:, sl:sl + 256], P[:, sl:sl + 256],
                                    killj[:], None, ALU.mult))

                            block.vector(cmsk)
                        mkn2 = s_mk.n

                        for cc in (c - 1, c):
                            def cpv(t, jb=jb, bi=bi, c=cc, p_=p_, mkn2=mkn2,
                                    exn=exp_pair[p_], evg=s_ev.n):
                                if c == 0 and evg:
                                    t.wait_ge(s_ev.h, evg)
                                if c <= 1:
                                    t.wait_ge(s_mk.h, mkn2)
                                t.wait_ge(s_ex.h, exn)
                                sl = (p_ % 4) * 512 + (c % 2) * 256
                                if c < 64:
                                    lhs = vth[bi][:].rearrange(
                                        "p (cc2 d) -> p cc2 d", cc2=64)[:, c, :]
                                else:
                                    r = c - 64
                                    off = jb * 256 + (128, 256, 640, 768)[r]
                                    lhs = vtw[bi][:, off:off + 128]
                                t.matmul(pat[:, 0:256], lhs, P[:, sl:sl + 256],
                                         start=(c == 0), stop=(c == 67))
                                ins = t.matmul(psm[0:1, 0:256], onec[:],
                                               P[:, sl:sl + 256],
                                               start=(c == 0), stop=(c == 67))
                                s_pv.inc(ins)

                            block.tensor(cpv)
                            pv_at[cc] = s_pv.n

                pvn = s_pv.n

                def bsum(s, pvn=pvn):
                    s.wait_ge(s_pv.h, pvn)
                    s_sm.inc(s.copy(sum_sb[:], psm[0:1, 0:256]))

                block.scalar(bsum)
                smn = s_sm.n

                def brcp(v, jb=jb, smn=smn, sven=sven, vprior=s_v.n):
                    v.wait_ge(s_v.h, vprior)
                    v.wait_ge(s_sm.h, smn)
                    if jb == 0:
                        s_v.inc(v.tensor_mul(sum2[:, 0:64], sum_sb[:, 0:64],
                                             mb64[0:1, 0:64]))
                        v.wait_ge(s_v.h, s_v.n)
                        s_v.inc(v.tensor_add(sum_sb[:, 0:64], sum2[:, 0:64],
                                             usum[:]))
                        v.wait_ge(s_v.h, s_v.n)
                    s_rc.inc(v.reciprocal(rec_sb[:], sum_sb[:]))

                block.vector(brcp)
                rcn = s_rc.n

                def bbc(t, rcn=rcn, rbg=s_rb.n):
                    t.wait_ge(s_rc.h, rcn)
                    if rbg:
                        t.wait_ge(s_rb.h, rbg)
                    s_bc.inc(t.matmul(prb[:, 0:256], oner[:], rec_sb[:],
                                      start=True, stop=True))

                block.tensor(bbc)
                bcn = s_bc.n

                abi = (h * NQB + jb) % 2

                def bev(v, h=h, jb=jb, abi=abi, bcn=bcn, sven=sven,
                        vprior=s_v.n, stg=s_ev.n):
                    v.wait_ge(s_v.h, vprior)
                    v.wait_ge(s_bc.h, bcn)
                    s_rb.inc(v.tensor_copy(recB[:], prb[:, 0:256]))
                    v.wait_ge(s_v.h, s_v.n)
                    v.wait_ge(s_rb.h, s_rb.n)
                    if stg >= 32:
                        v.wait_ge(s_ev.h, stg - 16)
                    if jb == 0:
                        v.wait_ge(s_sve.h, sven)
                        s_v.inc(v.tensor_mul(o1t[:], pat[:, 0:64], recB[:, 0:64]))
                        v.wait_ge(s_v.h, s_v.n)
                        s_v.inc(v.tensor_mul(o1t[:], o1t[:], mb64[:, 0:64]))
                        v.wait_ge(s_v.h, s_v.n)
                        s_v.inc(v.scalar_tensor_tensor(
                            att_st[abi][:, 0:64],
                            ub64[:, 0:64], sumv8[:, 0:1], o1t[:],
                            ALU.mult, ALU.add))
                        v.wait_ge(s_v.h, s_v.n)
                        s_v.inc(v.tensor_mul(att_st[abi][:, 64:256],
                                             pat[:, 64:256], recB[:, 64:256]))
                    else:
                        s_v.inc(v.tensor_mul(att_st[abi][:, 0:256],
                                             pat[:, 0:256], recB[:, 0:256]))

                block.vector(bev)
                evn = s_v.n

                def asto(sy, h=h, jb=jb, abi=abi, evn=evn):
                    sy.wait_ge(s_v.h, evn)
                    if h == 0 and jb == 0:
                        sy.wait_ge(io["s_zi"].h, io["zi_n"])
                    s_ev.inc(sy.dma_start(
                        out=att_dram[h * 128:(h + 1) * 128,
                                     jb * 256:(jb + 1) * 256],
                        in_=att_st[abi][:]), 16)

                block.sync(asto)

    ATT_ALL = s_ev.n
    stA.close()

    if _os2.environ.get("DEV_B_DUMP") == "att":
        s_dp = C.sem("b3_dp")

        def dmpa(g):
            g.wait_ge(s_ev.h, ATT_ALL)
            for r in range(16):
                s_dp.inc(g.dma_start(
                    out=out_x[r * 128:(r + 1) * 128, :],
                    in_=att_dram[r * 128:(r + 1) * 128, :]), 16)

        block.gpsimd(dmpa)
        DN2 = s_dp.n

        class _O2:
            def barrier(self, eng):
                eng.wait_ge(s_dp.h, DN2)
        return _O2()

    st2 = contextlib.ExitStack()
    attT = C.sbuf("b3attT", [128, 16 * 2048], BF16, st2)
    s_al = C.sem("b3_al")

    def lda(sy):
        sy.wait_ge(s_ev.h, ATT_ALL)
        for mc in range(16):
            s_al.inc(sy.dma_start(
                out=attT[:, mc * 2048:(mc + 1) * 2048],
                in_=att_dram[mc * 128:(mc + 1) * 128, :]), 16)

    block.sync(lda)
    ALN = s_al.n
    wo_sb = C.sbuf("b3wo", [128, 16 * 2048], BF16, st2)
    ysb = [C.sbuf(f"b3y{i}", [128, 2048], F32, st2) for i in range(2)]
    pyb = [C.psum(f"b3py{i}", [128, 512], st2) for i in range(2)]
    s_wo = C.sem("b3_wo")
    s_ymm = C.sem("b3_ymm")
    s_yev = C.sem("b3_yev")
    s_res = [C.sem(f"b3_res{i}") for i in range(2)]

    def ldwo(g):
        g.wait_ge(s_ev.h, ATT_ALL)
        for mc in range(4):
            s_wo.inc(g.dma_start(
                out=wo_sb[:, mc * 8192:(mc + 1) * 8192],
                in_=wo_b[:, mc * 8192:(mc + 1) * 8192]), 16)

    block.gpsimd(ldwo)

    res_at = {}
    for tc in range(16):
        for fb in range(4):
            i = 4 * tc + fb

            def mmy(t, tc=tc, fb=fb, i=i, yegate=s_yev.n):
                if i == 0:
                    t.wait_ge(s_wo.h, 64)
                    t.wait_ge(s_al.h, ALN)
                if yegate >= 2:
                    t.wait_ge(s_yev.h, yegate - 1)
                wo3 = wo_sb[:].rearrange("p (m f) -> p m f", m=16)
                pb = pyb[i % 2]
                t.matmul(pb[:], onerb[:], bo[:, fb * 512:(fb + 1) * 512],
                         start=True, stop=False)
                for mc in range(16):
                    ins = t.matmul(
                        pb[:],
                        attT[:, mc * 2048 + tc * 128: mc * 2048 + (tc + 1) * 128],
                        wo3[:, mc, fb * 512:(fb + 1) * 512],
                        start=False, stop=(mc == 15))
                s_ymm.inc(ins)

            block.tensor(mmy)
            ymn = s_ymm.n

            def evy(v, tc=tc, fb=fb, i=i, ymn=ymn, rgate=res_at.get(tc - 2, 0),
                    vprior=s_v.n):
                v.wait_ge(s_v.h, vprior)
                v.wait_ge(s_ymm.h, ymn)
                if fb == 0 and rgate:
                    v.wait_ge(s_res[tc % 2].h, rgate)
                s_yev.inc(v.tensor_copy(
                    ysb[tc % 2][:, fb * 512:(fb + 1) * 512], pyb[i % 2][:]))

            block.vector(evy)

        yn = s_yev.n

        def acc(g, tc=tc, yn=yn):
            g.wait_ge(s_yev.h, yn)
            s_res[tc % 2].inc(g.dma_start(
                out=out_x[tc * 128:(tc + 1) * 128, :],
                in_=ysb[tc % 2][:], accum_op=ALU.add), 16)

        block.gpsimd(acc)
        res_at[tc] = s_res[tc % 2].n

    st2.close()
    R = [s_res[0].n, s_res[1].n]

    class ResOut:
        def barrier(self, eng):
            eng.wait_ge(s_res[0].h, R[0])
            eng.wait_ge(s_res[1].h, R[1])

    return ResOut()



def _host_consts(core):
    b, q = divmod(core, 4)
    kk = np.arange(128)[:, None]
    tt = np.arange(QB)[None, :]
    c = {}

    va = np.ones((LTOK,), np.float32)
    if q == 0:
        va[:HALO] = 0.0
    c["valid_a"] = np.ascontiguousarray(
        va.reshape(LCH, 128).T).astype(BF)

    m = np.zeros((4, 128, QB), np.float32)
    m[0] = (kk > tt)
    m[1] = (tt <= kk + 127)
    m[2] = (kk <= tt)
    m[3] = (tt >= kk + 128)
    c["mask_a"] = m.astype(BF)

    c["ones_col"] = np.ones((128, 1), np.float32).astype(BF)
    c["ones_row"] = np.ones((1, 128), np.float32)
    c["ones_row_bf"] = np.ones((1, 128), np.float32).astype(BF)
    base = q * TOK
    t64 = np.arange(QB, dtype=np.float64)
    kk64 = np.arange(128, dtype=np.float64)
    KILL = 1.0e9
    tab = np.zeros((NQB, 68, 3, QB), np.float32)
    reldelta = (256, 128, -256, -384)
    for jb in range(NQB):
        Q0 = base + QB * jb
        for ch in range(64):
            delta = Q0 - 128 * ch
            if -512 < delta < 384:
                tab[jb, ch, 0, :] = -KILL
                continue
            sg = 1.0 if delta >= 384 else -1.0
            tab[jb, ch, 0, :] = -sg * (Q0 + t64) + 257.0
            tab[jb, ch, 1, :] = sg
            tab[jb, ch, 2, :] = sg * 128.0 * ch
        for r, dl in enumerate(reldelta):
            sg = 1.0 if dl > 0 else -1.0
            tab[jb, 64 + r, 0, :] = -sg * (Q0 + t64) + 257.0
            tab[jb, 64 + r, 1, :] = sg
            tab[jb, 64 + r, 2, :] = sg * float(Q0 - dl)
    c["atab"] = tab

    mrel = np.zeros((128, NQB * 4 * QB), np.float32)
    for jb in range(NQB):
        Q0 = base + QB * jb
        for r, dl in enumerate(reldelta):
            j0 = Q0 - dl
            jj = j0 + np.arange(128)[:, None]
            dmat = (Q0 + np.arange(QB)[None, :]) - jj
            ok = (np.abs(dmat) > 256) & (jj >= 64) & (jj < SEQ)
            mrel[:, (jb * 4 + r) * QB:(jb * 4 + r + 1) * QB] = \
                np.where(ok, 0.0, -1.0e9)
    c["maskrel"] = mrel

    rows = base - 384 + np.arange(TOK + 640)
    c["kvwin_idx"] = np.ascontiguousarray(
        np.clip(rows, 0, SEQ - 1).astype(np.int32)
        .reshape((TOK + 640) // 128, 128).T)

    half = 64
    freqs = 1.0 / 10000.0 ** (np.arange(half, dtype=np.float64) / half)
    pos = base + np.arange(TOK, dtype=np.float64)
    ang = np.outer(pos, freqs)
    c["rope_sin"] = np.ascontiguousarray(
        np.sin(ang).astype(np.float32).reshape(NCH, 128, 64)
        .transpose(1, 0, 2).reshape(128, NCH * 64))
    c["rope_cos"] = np.ascontiguousarray(
        np.cos(ang).astype(np.float32).reshape(NCH, 128, 64)
        .transpose(1, 0, 2).reshape(128, NCH * 64))

    r3 = np.zeros((3, 16 * 128), np.float32)
    for h in range(16):
        s = float(SLOPES[h]) * SQD
        r3[0, h * 128:(h + 1) * 128] = s
        r3[1, h * 128:(h + 1) * 128] = s * np.arange(128)
        r3[2, h * 128:(h + 1) * 128] = s
    c["ramp3h"] = r3

    kj = np.ones((128, 1), np.float32)
    kj[:64] = 0.0
    c["kill_j64"] = kj

    u = np.zeros(64, np.float32)
    if q == 0:
        u[:] = 1.0
    c["uB64"] = np.broadcast_to(u, (128, 64)).astype(np.float32).copy()
    c["mB64"] = np.broadcast_to(1.0 - u, (128, 64)).astype(np.float32).copy()
    c["uni_flag"] = u.reshape(1, 64).copy()
    c["uni_sum"] = (u * float(SEQ)).reshape(1, 64).copy()

    ltri = (np.arange(128)[:, None] < np.arange(128)[None, :])
    c["ltri"] = ltri.astype(np.float32).astype(BF)
    c["tokf"] = (np.arange(NCH)[None, :] * 128
                 + np.arange(128)[:, None]).astype(np.float32)
    c["base_r"] = (np.arange(NE, dtype=np.float32) * CAP).reshape(1, NE)
    c["mfill"] = np.full((128, 256), 1.0e5, np.float32)
    return c



def build_program():
    nc = bass.Bass("TRN2", target_bir_lowering=False)

    def din(name, shape, dt=F32):
        return nc.dram_tensor(name, shape, dt, kind="ExternalInput").ap()

    io = {}
    io["x_own"] = din("x_own", [TOK, D])
    io["x_halo"] = din("x_halo", [HALO, D])
    io["wq_a"] = din("wq_a", [16, 128, D])
    io["wk_a"] = din("wk_a", [128, 16 * 512])
    io["wv_a"] = din("wv_a", [128, 16 * 512])
    io["wo_a"] = din("wo_a", [128, 16 * 2048])
    bo_a = din("bo_a", [1, D])
    valid_a = din("valid_a", [128, LCH], BF16)
    mask_a = din("mask_a", [4, 128, QB], BF16)
    ones_col = din("ones_col", [128, 1], BF16)
    ones_row = din("ones_row", [1, 128])
    ones_row_bf = din("ones_row_bf", [1, 128], BF16)

    io["wq_b"] = din("wq_b", [128, 16 * D])
    io["wkv_b"] = din("wkv_b", [128, 16 * D])
    io["wo_b"] = din("wo_b", [128, 16 * D])
    bo_b = din("bo_b", [1, D])
    io["rope_sin"] = din("rope_sin", [128, 16 * 64])
    io["rope_cos"] = din("rope_cos", [128, 16 * 64])
    io["atab"] = din("atab", [NQB, 68, 3, QB])
    io["maskrel"] = din("maskrel", [128, NQB * 4 * QB])
    io["kvwin_idx"] = din("kvwin_idx", [128, (TOK + 640) // 128], I32)
    io["ramp3h"] = din("ramp3h", [3, 16 * 128])
    io["kill_j64"] = din("kill_j64", [128, 1])
    io["uB64"] = din("uB64", [128, 64])
    io["mB64"] = din("mB64", [128, 64])
    io["uni_flag"] = din("uni_flag", [1, 64])
    io["uni_sum"] = din("uni_sum", [1, 64])
    io["gate_wT"] = din("gate_wT", [128, 16 * NE])
    io["w1T"] = din("w1T", [NE, 16, 128, 2048])
    io["w3T"] = din("w3T", [NE, 16, 128, 2048])
    io["w2T"] = din("w2T", [NE, 128, 16 * 2048])
    ltri = din("ltri", [128, 128], BF16)
    tokf = din("tokf", [128, NCH])
    base_r = din("base_r", [1, NE])
    mfill = din("mfill", [128, 256])

    io["out_x"] = nc.dram_tensor("out_x", [TOK, D], F32,
                                 kind="ExternalOutput").ap()
    io["xn_dram"] = nc.dram_tensor("xn_dram", [LTOK, D], BF16).ap()
    io["q_dram"] = nc.dram_tensor("q_dram", [D, D], BF16).ap()
    io["kv_contrib"] = nc.dram_tensor("kv_contrib", [TOK, D], BF16).ap()
    io["kv_all"] = nc.dram_tensor("kv_all", [4 * TOK, D], BF16).ap()
    io["kv_win"] = nc.dram_tensor("kv_win", [TOK + 640, D], BF16).ap()
    io["seg_x"] = nc.dram_tensor("seg_x", [NE * CAP, D], BF16).ap()
    io["seg_meta"] = nc.dram_tensor("seg_meta", [NE * CAP, 2], F32).ap()
    io["y01"] = nc.dram_tensor("y01", [2 * TOK, D], F32).ap()

    stack = contextlib.ExitStack()
    with nc.Block() as block:
        C = Ctx(nc, block, stack)

        onec = C.sbuf("onec", [128, 1], BF16)
        oner = C.sbuf("oner", [1, 128], F32)
        onerb = C.sbuf("onerb", [1, 128], BF16)
        maska_sb = C.sbuf("maska_sb", [128, 4 * QB], BF16)
        valida_sb = C.sbuf("valida_sb", [128, LCH], BF16)
        bo_a_sb = C.sbuf("bo_a_sb", [1, D], BF16)
        ltri_sb = C.sbuf("ltri_sb", [128, 128], BF16)
        tokf_sb = C.sbuf("tokf_sb", [128, NCH], F32)
        base_sb = C.sbuf("base_sb", [1, NE], F32)

        s_c = C.sem("consts")

        def ldc(sy):
            for dst, src in [
                (onec[:], ones_col), (oner[:], ones_row), (onerb[:], ones_row_bf),
                (maska_sb[:].rearrange("p (m t) -> p m t", m=4),
                 mask_a.rearrange("m p t -> p m t")),
                (valida_sb[:], valid_a),
                (ltri_sb[:], ltri), (tokf_sb[:], tokf), (base_sb[:], base_r),
            ]:
                s_c.inc(sy.dma_start(out=dst, in_=src), 16)

        block.sync(ldc)

        s_cg = C.sem("constsg")

        bo_b_sb = C.sbuf("bo_b_sb", [1, D], BF16)
        io["bo_b_sb"] = bo_b_sb

        def ldc2(g):
            s_cg.inc(g.dma_start(out=bo_a_sb[:], in_=bo_a), 16)
            s_cg.inc(g.dma_start(out=bo_b_sb[:], in_=bo_b), 16)

        block.gpsimd(ldc2)

        def regs(g):
            r1 = C.stack.enter_context(g.register("bnd_seg"))
            g.reg_mov(r1, NE * CAP - 1)
            io["bnd_seg"] = g.snap(r1)
            r2 = C.stack.enter_context(g.register("bnd_y"))
            g.reg_mov(r2, 2 * TOK - 1)
            io["bnd_y"] = g.snap(r2)

        block.gpsimd(regs)

        def ldc3(sy):
            s_c.inc(sy.dma_start(out=io["out_x"], in_=io["x_own"]), 16)

        block.sync(ldc3)

        zsb = C.sbuf("zsb", [128, D], BF16)
        s_zi = C.sem("zinit")
        io["s_zi"] = s_zi

        def zi(v):
            s_zi.inc(v.memset(zsb[:], 0.0))

        block.vector(zi)

        def zid(sy):
            sy.wait_ge(s_zi.h, 1)
            for r in range(NE * CAP // 128):
                s_zi.inc(sy.dma_start(
                    out=io["seg_x"][r * 128:(r + 1) * 128, :], in_=zsb[:]), 16)

        block.sync(zid)
        io["zi_n"] = s_zi.n
        CWAIT = s_c.n

        io.update(dict(onec=onec, oner=oner, onerb=onerb, maska=maska_sb,
                       valida=valida_sb, bo_a=bo_a_sb, ltri=ltri_sb,
                       tokf=tokf_sb, base_r=base_sb, mfill=mfill))

        lnstack = contextlib.ExitStack()
        io["ln1"] = emit_ln(
            C, "ln1", LCH,
            lambda k: (io["x_halo"][k * 128:(k + 1) * 128, :] if k < 8
                       else io["x_own"][(k - 8) * 128:(k - 7) * 128, :]),
            io["xn_dram"], 0, [(s_c, CWAIT), (s_cg, s_cg.n)], stack=lnstack)
        lnstack.close()
        if DEV_STOP == "ln":
            fin_o = io["ln1"]
        elif _os3.environ.get("DEV_SKIP_A"):
            fin_o = emit_moe(C, io, "m2", io["ln1"])
        elif _os3.environ.get("DEV_ONLY_B"):
            fin_o = emit_attn_b(C, io, io["ln1"])
        else:
            fin_o = emit_attn_a(C, io)
            if DEV_STOP == "full" and N_PHASES >= 2:
                fin_o = emit_moe(C, io, "m2", fin_o)
            if DEV_STOP == "full" and N_PHASES >= 3:
                fin_o = emit_attn_b(C, io, fin_o)
            if DEV_STOP == "full" and N_PHASES >= 4:
                fin_o = emit_moe(C, io, "m4", fin_o)

        def fin(g):
            fin_o.barrier(g)

        block.gpsimd(fin)

    stack.close()
    return nc


def build_in_maps(L0_attn_a_x, L0_attn_a_Wq, L0_attn_a_Wk, L0_attn_a_Wv,
                  L0_attn_a_Wo, L0_attn_a_bo, L0_attn_b_Wq, L0_attn_b_Wkv,
                  L0_attn_b_Wo, L0_attn_b_bo, L0_ffn_gate_w, L0_ffn_w1,
                  L0_ffn_w3, L0_ffn_w2):
    x = np.asarray(L0_attn_a_x, dtype=np.float32)
    wq = np.asarray(L0_attn_a_Wq, np.float32)
    wk = np.asarray(L0_attn_a_Wk, np.float32)
    wv = np.asarray(L0_attn_a_Wv, np.float32)
    wo = np.asarray(L0_attn_a_Wo, np.float32)
    gw = np.asarray(L0_ffn_gate_w, np.float32)
    w1 = np.asarray(L0_ffn_w1, np.float32)
    w3 = np.asarray(L0_ffn_w3, np.float32)
    w2 = np.asarray(L0_ffn_w2, np.float32)
    shared = {
        "wq_a": np.ascontiguousarray(
            wq.reshape(16, 128, 16, 128).transpose(2, 1, 0, 3)
              .reshape(16, 128, D)),
        "wk_a": np.ascontiguousarray(
            wk.reshape(16, 128, 512).transpose(1, 0, 2).reshape(128, 16 * 512)),
        "wv_a": np.ascontiguousarray(
            wv.reshape(16, 128, 512).transpose(1, 0, 2).reshape(128, 16 * 512)),
        "wo_a": np.ascontiguousarray(
            wo.reshape(16, 128, D).transpose(1, 0, 2).reshape(128, 16 * D)),
        "bo_a": np.ascontiguousarray(L0_attn_a_bo, np.float32).reshape(1, D),
        "wq_b": np.ascontiguousarray(
            np.asarray(L0_attn_b_Wq, np.float32).reshape(16, 128, D)
              .transpose(1, 0, 2).reshape(128, 16 * D)),
        "wkv_b": np.ascontiguousarray(
            np.asarray(L0_attn_b_Wkv, np.float32).reshape(16, 128, D)
              .transpose(1, 0, 2).reshape(128, 16 * D)),
        "wo_b": np.ascontiguousarray(
            np.asarray(L0_attn_b_Wo, np.float32).reshape(16, 128, D)
              .transpose(1, 0, 2).reshape(128, 16 * D)),
        "bo_b": np.ascontiguousarray(L0_attn_b_bo, np.float32).reshape(1, D),
        "gate_wT": np.ascontiguousarray(
            gw.T.reshape(16, 128, NE).transpose(1, 0, 2).reshape(128, 16 * NE)),
        "w1T": np.ascontiguousarray(
            w1.reshape(NE, 16, 128, 16, 128).transpose(0, 1, 4, 3, 2)
              .reshape(NE, 16, 128, D)),
        "w3T": np.ascontiguousarray(
            w3.reshape(NE, 16, 128, 16, 128).transpose(0, 1, 4, 3, 2)
              .reshape(NE, 16, 128, D)),
        "w2T": np.ascontiguousarray(
            w2.transpose(0, 2, 1).reshape(NE, 16, 128, D)
              .transpose(0, 2, 1, 3).reshape(NE, 128, 16 * D)),
    }

    in_maps = []
    for core in range(NCORES):
        b, q = divmod(core, 4)
        m = dict(shared)
        m["x_own"] = np.ascontiguousarray(x[b, q * TOK:(q + 1) * TOK])
        halo = np.zeros((HALO, D), np.float32)
        if q > 0:
            halo[:] = x[b, q * TOK - HALO:q * TOK]
        m["x_halo"] = halo
        m.update(_host_consts(core))
        in_maps.append(m)
    return in_maps


def kernel(**kw):
    in_maps = build_in_maps(**kw)
    nc = build_program()
    res = run_bass_kernel_spmd(nc, in_maps, core_ids=list(range(NCORES)))
    out = np.empty((2, SEQ, D), np.float32)
    for core in range(NCORES):
        b, q = divmod(core, 4)
        out[b, q * TOK:(q + 1) * TOK] = res.results[core]["out_x"]
    return out




# version 06
# speedup vs baseline: 1.2188x; 1.0641x over previous
import math
import os as _os2
import os as _os3
import contextlib
import numpy as np
import ml_dtypes
import concourse.bass as bass
import concourse.mybir as mybir
from concourse.bass_utils import run_bass_kernel_spmd

F32 = mybir.dt.float32
BF16 = mybir.dt.bfloat16
I32 = mybir.dt.int32
AF = mybir.ActivationFunctionType
ALU = mybir.AluOpType
AX = mybir.AxisListType
BF = ml_dtypes.bfloat16

NCORES = 8
TOK = 2048
D = 2048
HALO = 1024
LTOK = TOK + HALO
NCH = 16
LCH = 24
QB = 256
NQB = 8
SEQ = 8192
CAP = 1536
NE = 8
SQD = math.sqrt(128.0)
_AB = 2.0 ** (-(2.0 ** (-(math.log2(16.0) - 3.0))))
SLOPES = np.array([_AB ** p for p in range(1, 17)], dtype=np.float64)

N_PHASES = 4
import os
DEV_STOP = os.environ.get('DEV_STOP', 'full')


class S:
    __slots__ = ("h", "n")

    def __init__(self, h):
        self.h = h
        self.n = 0

    def inc(self, instr, k=1):
        instr.then_inc(self.h, k)
        self.n += k
        return self.n


class Ctx:
    def __init__(self, nc, block, stack):
        self.nc = nc
        self.block = block
        self.stack = stack
        self.sems = {}

    def sem(self, name):
        key = name.split("_", 1)[1] if "_" in name else name
        if key not in self.sems:
            h = self.stack.enter_context(self.nc.semaphore(key))
            self.sems[key] = S(h)
        return self.sems[key]

    def sbuf(self, name, shape, dt, stack=None):
        return (stack or self.stack).enter_context(
            self.nc.sbuf_tensor(name, shape, dt))

    def psum(self, name, shape, stack=None):
        return (stack or self.stack).enter_context(
            self.nc.psum_tensor(name, shape, F32))



def emit_ln(C, tag, n_chunks, x_src, xn_dram, row0, wait, comp_out=None,
            stack=None, prev=None):
    block = C.block
    NB = 4
    xb = [C.sbuf(f"{tag}_x{i}", [128, D], F32, stack) for i in range(NB)]
    xnb = [C.sbuf(f"{tag}_n{i}", [128, D], BF16, stack) for i in range(NB)]
    sq = C.sbuf(f"{tag}_sq", [128, D], F32, stack)
    st = C.sbuf(f"{tag}_st", [128, 8], F32, stack)
    cs = C.sbuf(f"{tag}_cs", [128, 1], F32, stack) if comp_out is not None else None

    s_ld = [C.sem(f"{tag}_ld{i}") for i in range(NB)]
    s_a = C.sem(f"{tag}_a")
    s_v = C.sem(f"{tag}_v")
    s_xs = [C.sem(f"{tag}_xs{i}") for i in range(NB)]

    xb_free = [0] * NB
    xnb_st = [0] * NB
    xn_at = []
    ex2_done = 0

    for k in range(n_chunks):
        bi = k % NB
        kw = list(wait) if k == 0 else []
        gate = xb_free[bi]

        def ld(sy, k=k, kw=kw, gate=gate, bi=bi):
            for (ws, wn) in kw:
                sy.wait_ge(ws.h, wn)
            if k == 0 and prev is not None:
                prev.barrier(sy)
            if gate:
                sy.wait_ge(s_v.h, gate)
            s_ld[bi].inc(sy.dma_start(out=xb[bi][:], in_=x_src(k)), 16)

        block.sync(ld)
        ldn = s_ld[bi].n

        def act1(s, ldn=ldn, bi=bi, exd=ex2_done):
            s.wait_ge(s_ld[bi].h, ldn)
            if exd:
                s.wait_ge(s_v.h, exd)
            s_a.inc(s.activation(sq[:], xb[bi][:], AF.Square,
                                 accum_out=st[:, 1:2]))

        block.scalar(act1)
        an1 = s_a.n

        def vec1(v, ldn=ldn, an1=an1, bi=bi):
            v.wait_ge(s_ld[bi].h, ldn)
            if comp_out is not None:
                s_v.inc(v.tensor_reduce(cs[:], xb[bi][:], AX.X, ALU.add,
                                        apply_absolute_value=True))
            s_v.inc(v.tensor_reduce(st[:, 0:1], xb[bi][:], AX.X, ALU.add))
            v.wait_ge(s_v.h, s_v.n)
            v.wait_ge(s_a.h, an1)
            s_v.inc(v.tensor_scalar(st[:, 2:3], st[:, 0:1], 1.0 / D, None, ALU.mult))
            s_v.inc(v.tensor_scalar(st[:, 3:4], st[:, 1:2], 1.0 / D, None, ALU.mult))
            v.wait_ge(s_v.h, s_v.n)
            s_v.inc(v.tensor_mul(st[:, 4:5], st[:, 2:3], st[:, 2:3]))
            v.wait_ge(s_v.h, s_v.n)
            s_v.inc(v.tensor_sub(st[:, 5:6], st[:, 3:4], st[:, 4:5]))
            v.wait_ge(s_v.h, s_v.n)
            s_v.inc(v.tensor_scalar(st[:, 5:6], st[:, 5:6], 1e-5, None, ALU.add))

        block.vector(vec1)
        ex2_done = s_v.n - 3
        vn1 = s_v.n

        def act2(s, vn1=vn1, k=k):
            s.wait_ge(s_v.h, vn1)
            s_a.inc(s.activation(st[:, 6:7], st[:, 5:6], AF.Sqrt))
            if comp_out is not None:
                s_a.inc(s.activation(comp_out[:, k:k + 1], cs[:], AF.Sigmoid))

        block.scalar(act2)
        an2 = s_a.n

        def vec2(v, an2=an2, bi=bi, xng=xnb_st[bi]):
            v.wait_ge(s_a.h, an2)
            s_v.inc(v.reciprocal(st[:, 7:8], st[:, 6:7]))
            v.wait_ge(s_v.h, s_v.n)
            if xng:
                v.wait_ge(s_xs[bi].h, xng)
            s_v.inc(v.tensor_scalar(xnb[bi][:], xb[bi][:],
                                    st[:, 2:3], st[:, 7:8],
                                    ALU.subtract, ALU.mult))

        block.vector(vec2)
        xb_free[bi] = s_v.n
        vn2 = s_v.n

        def sto(sy, vn2=vn2, k=k, bi=bi):
            sy.wait_ge(s_v.h, vn2)
            s_xs[bi].inc(sy.dma_start(
                out=xn_dram[row0 + k * 128: row0 + (k + 1) * 128, :],
                in_=xnb[bi][:]), 16)

        block.sync(sto)
        xnb_st[bi] = s_xs[bi].n
        xn_at.append((bi, s_xs[bi].n))

    class LNOut:
        def barrier(self, eng):
            for s in s_xs:
                if s.n:
                    eng.wait_ge(s.h, s.n)

        def chunk_barrier(self, eng, k):
            bi, n = xn_at[k]
            eng.wait_ge(s_xs[bi].h, n)

    return LNOut()



def emit_attn_a(C, io):
    block = C.block
    wq_a, wk_a, wv_a, wo_a = io["wq_a"], io["wk_a"], io["wv_a"], io["wo_a"]
    xn_dram, q_dram, out_x = io["xn_dram"], io["q_dram"], io["out_x"]
    oner, onerb = io["oner"], io["onerb"]
    maska, valida, bo = io["maska"], io["valida"], io["bo_a"]
    ln = io["ln1"]

    stA = contextlib.ExitStack()
    attT = C.sbuf("a1attT", [128, 16 * 2048], BF16, stA)
    st1 = contextlib.ExitStack()
    stKV = contextlib.ExitStack()
    kT = C.sbuf("a1kT", [128, 4 * 3072], BF16, st1)
    vv = C.sbuf("a1v", [128, 24 * 512], BF16, st1)
    xnt = [C.sbuf(f"a1xnt{i}", [128, 16 * 512], BF16, st1) for i in range(2)]
    wks = C.sbuf("a1wk", [128, 16 * 512], BF16, stKV)
    wvs = C.sbuf("a1wv", [128, 16 * 512], BF16, stKV)

    pj = C.psum("a1pj", [128, 512], st1)
    pj2 = C.psum("a1pj2", [128, 512], st1)
    psc = [C.psum(f"a1psc{i}", [128, 512], st1) for i in range(2)]
    pat = C.psum("a1pat", [128, 512], st1)
    psm = C.psum("a1psm", [128, 512], st1)
    prb = C.psum("a1prb", [128, 512], st1)

    s_t = [C.sem(f"a1_t{i}") for i in range(2)]
    s_w = C.sem("a1_w")
    s_wq = [C.sem(f"a1_wq{i}") for i in range(2)]
    s_kve = C.sem("a1_kve")
    s_mm = C.sem("a1_mm")
    s_qev = C.sem("a1_qev")
    s_qst = [C.sem(f"a1_qst{i}") for i in range(2)]
    s_ld = [C.sem(f"a1_ld{i}") for i in range(2)]
    s_sc = C.sem("a1_sc")
    s_ex = C.sem("a1_ex")
    s_mk = C.sem("a1_mk")
    s_pv = C.sem("a1_pv")
    s_sm = C.sem("a1_sm")
    s_rc = C.sem("a1_rc")
    s_bc = C.sem("a1_bc")
    s_rb = C.sem("a1_rb")
    s_ev = C.sem("a1_ev")
    s_wo = C.sem("a1_wo")
    s_ymm = C.sem("a1_ymm")
    s_yev = C.sem("a1_yev")
    s_res = [C.sem(f"a1_res{i}") for i in range(2)]

    def ldw(g):
        ln.barrier(g)
        s_w.inc(g.dma_start(out=wks[:], in_=wk_a), 16)
        s_w.inc(g.dma_start(out=wvs[:], in_=wv_a), 16)

    block.gpsimd(ldw)

    xnt_free = [0, 0]
    pjf = 0
    pj2f = 0
    for T in range(6):
        def ldt(sy, T=T, gate=xnt_free[T % 2]):
            if T == 0:
                ln.barrier(sy)
            if gate:
                sy.wait_ge(s_mm.h, gate)
            s_t[T % 2].inc(sy.dma_start(
                out=xnt[T % 2][:].rearrange("p (dc r) -> p dc r", dc=16),
                in_=xn_dram[T * 512:(T + 1) * 512, :]
                    .rearrange("r (dc p) -> r dc p", p=128),
                transpose=True), 16)

        block.sync(ldt)
        tw = s_t[T % 2].n

        def mmkv(t, T=T, tw=tw, pjf=pjf, pj2f=pj2f):
            if T == 0:
                t.wait_ge(s_w.h, 32)
            t.wait_ge(s_t[T % 2].h, tw)
            wk3 = wks[:].rearrange("p (dc n) -> p dc n", dc=16)
            wv3 = wvs[:].rearrange("p (dc n) -> p dc n", dc=16)
            x3 = xnt[T % 2][:].rearrange("p (dc r) -> p dc r", dc=16)
            for g4 in range(4):
                if pjf + g4 > 0:
                    t.wait_ge(s_kve.h, pjf + g4)
                for dc in range(16):
                    ins = t.matmul(pj[:], wk3[:, dc, g4 * 128:(g4 + 1) * 128],
                                   x3[:, dc, :], start=(dc == 0), stop=(dc == 15))
                s_mm.inc(ins)
            for tc in range(4):
                if pj2f + tc > 0:
                    t.wait_ge(s_kve.h, pj2f + tc + 4)
                for dc in range(16):
                    ins = t.matmul(pj2[:], x3[:, dc, tc * 128:(tc + 1) * 128],
                                   wv3[:, dc, :],
                                   start=(dc == 0), stop=(dc == 15))
                s_mm.inc(ins)

        block.tensor(mmkv)
        mmn = s_mm.n
        xnt_free[T % 2] = mmn

        def evkv(v, T=T, mmn=mmn):
            base = mmn - 8
            for g4 in range(4):
                v.wait_ge(s_mm.h, base + g4 + 1)
                s_kve.inc(v.tensor_copy(
                    kT[:, g4 * 3072 + T * 512: g4 * 3072 + (T + 1) * 512], pj[:]))
            for tc in range(4):
                v.wait_ge(s_mm.h, base + 4 + tc + 1)
                s_kve.inc(v.tensor_copy(
                    vv[:, (4 * T + tc) * 512:(4 * T + tc + 1) * 512], pj2[:]))

        block.vector(evkv)
        pjf = s_kve.n
        pj2f = s_kve.n

    KV_EV = s_kve.n
    KV_MM = s_mm.n
    stKV.close()
    wqb = [C.sbuf(f"a1wq{i}", [128, 16 * 128], BF16, st1) for i in range(2)]
    qev = [C.sbuf(f"a1qe{i}", [128, 512], BF16, st1) for i in range(2)]

    mm_at = {}
    u = 0
    for pash, tiles in ((0, (2, 3)), (1, (4, 5))):
        for bi, T in enumerate(tiles):
            def ldt2(sy, T=T, bi=bi, gate=xnt_free[bi]):
                sy.wait_ge(s_mm.h, gate)
                s_t[bi].inc(sy.dma_start(
                    out=xnt[bi][:].rearrange("p (dc r) -> p dc r", dc=16),
                    in_=xn_dram[T * 512:(T + 1) * 512, :]
                        .rearrange("r (dc p) -> r dc p", p=128),
                    transpose=True), 16)

            block.sync(ldt2)
        twq = [s_t[0].n, s_t[1].n]

        for h in range(16):
            def ldq(g, h=h, u=u, kvn=KV_MM, wg=mm_at.get(u - 2, 0)):
                if u == 0 and h == 0:
                    g.wait_ge(s_mm.h, kvn)
                if wg:
                    g.wait_ge(s_mm.h, wg)
                s_wq[h % 2].inc(g.dma_start(
                    out=wqb[h % 2][:], in_=wq_a[h]), 16)

            block.gpsimd(ldq)
            wn = s_wq[h % 2].n

            for bi, T in enumerate(tiles):
                i = 2 * u + bi

                def mmq(t, h=h, bi=bi, wn=wn, tn=twq[bi], i=i, qf=s_qev.n):
                    t.wait_ge(s_wq[h % 2].h, wn)
                    t.wait_ge(s_t[bi].h, tn)
                    if i == 0:
                        t.wait_ge(s_kve.h, KV_EV)
                    if qf >= 2:
                        t.wait_ge(s_qev.h, qf - 1)
                    wq3 = wqb[h % 2][:].rearrange("p (dc f) -> p dc f", dc=16)
                    x3 = xnt[bi][:].rearrange("p (dc r) -> p dc r", dc=16)
                    for dc in range(16):
                        ins = t.matmul(pj[:] if i % 2 == 0 else pj2[:],
                                       wq3[:, dc, :], x3[:, dc, :],
                                       start=(dc == 0), stop=(dc == 15))
                    s_mm.inc(ins)

                block.tensor(mmq)
                mmn = s_mm.n

                def evq(v, mmn=mmn, i=i, stn=s_qst[i % 2].n):
                    v.wait_ge(s_mm.h, mmn)
                    if stn:
                        v.wait_ge(s_qst[i % 2].h, stn)
                    s_qev.inc(v.tensor_copy(qev[i % 2][:],
                                            pj[:] if i % 2 == 0 else pj2[:]))

                block.vector(evq)
                qn = s_qev.n

                def stq(sy, h=h, T=T, qn=qn, i=i):
                    sy.wait_ge(s_qev.h, qn)
                    s_qst[i % 2].inc(sy.dma_start(
                        out=q_dram[h * 128:(h + 1) * 128,
                                   (T - 2) * 512:(T - 1) * 512],
                        in_=qev[i % 2][:]), 16)

                block.sync(stq)
            mm_at[u] = s_mm.n
            u += 1
        xnt_free[0] = s_mm.n
        xnt_free[1] = s_mm.n

    QST = [s_qst[0].n, s_qst[1].n]
    import os as _os
    if _os.environ.get('DEV_STOP', 'full') in ('kv', 'q'):
        st1.close()
        stA.close()

        class _O:
            def barrier(self, eng):
                eng.wait_ge(s_qst[0].h, QST[0])
                eng.wait_ge(s_qst[1].h, QST[1])
        return _O()

    qsl = [C.sbuf(f"a1qsl{i}", [128, 16 * 256], BF16, st1) for i in range(2)]
    P = C.sbuf("a1P", [128, 10 * 256], BF16, st1)
    sum_sb = C.sbuf("a1sum", [1, 256], F32, st1)
    rec_sb = C.sbuf("a1rec", [1, 256], F32, st1)
    recB = C.sbuf("a1recB", [128, 256], F32, st1)

    qsl_free = [0, 0]
    _attlim = int(_os.environ.get('DEV_ATT_LIM', '999'))

    for jb in range(NQB):
        if jb * 16 >= _attlim:
            break

        def ldqs(sy, jb=jb, gate=qsl_free[jb % 2]):
            if jb == 0:
                sy.wait_ge(s_qst[0].h, QST[0])
                sy.wait_ge(s_qst[1].h, QST[1])
            if gate:
                sy.wait_ge(s_sc.h, gate)
            s_ld[jb % 2].inc(sy.dma_start(
                out=qsl[jb % 2][:].rearrange("p (hh r) -> p hh r", hh=16),
                in_=q_dram[:, jb * 256:(jb + 1) * 256]
                    .rearrange("(hh p) r -> p hh r", p=128)), 16)

        block.sync(ldqs)
        ldn = s_ld[jb % 2].n

        for h in range(16):
            g4 = h // 4
            hq = jb * 16 + h
            if hq >= _attlim:
                break

            def mmsc(t, jb=jb, h=h, g4=g4, hq=hq, ldn=ldn):
                t.wait_ge(s_ld[jb % 2].h, ldn)
                q_ap = qsl[jb % 2][:].rearrange(
                    "p (hh r) -> p hh r", hh=16)[:, h, :]
                for c in range(10):
                    g = hq * 10 + c
                    if g >= 2:
                        t.wait_ge(s_ex.h, g - 1)
                    kc = 2 * jb + c
                    ins = t.matmul(psc[c % 2][:, 0:256],
                                   kT[:, g4 * 3072 + kc * 128:
                                      g4 * 3072 + (kc + 1) * 128],
                                   q_ap, start=True, stop=True)
                    s_sc.inc(ins)

            block.tensor(mmsc)
            scn = s_sc.n

            def exps(s, scn=scn, pvgate=s_pv.n):
                if pvgate:
                    s.wait_ge(s_pv.h, pvgate)
                for c in range(10):
                    s.wait_ge(s_sc.h, scn - 10 + c + 1)
                    s_ex.inc(s.activation(P[:, c * 256:(c + 1) * 256],
                                          psc[c % 2][:, 0:256],
                                          AF.Exp, scale=1.0 / SQD))

            block.scalar(exps)
            exn = s_ex.n
            if _os.environ.get('DEV_ATT_OPS', 'full') == 'sc':
                continue

            def msk(v, exn=exn):
                for mi, c in enumerate((0, 1, 8, 9)):
                    v.wait_ge(s_ex.h, exn - 10 + c + 1)
                    s_mk.inc(v.tensor_mul(P[:, c * 256:(c + 1) * 256],
                                          P[:, c * 256:(c + 1) * 256],
                                          maska[:, mi * 256:(mi + 1) * 256]))

            block.vector(msk)
            mkn = s_mk.n
            if _os.environ.get('DEV_ATT_OPS', 'full') == 'msk':
                continue

            def mmpv(t, jb=jb, g4=g4, exn=exn, mkn=mkn, evgate=s_ev.n):
                if evgate:
                    t.wait_ge(s_ev.h, evgate)
                t.wait_ge(s_ex.h, exn)
                t.wait_ge(s_mk.h, mkn)
                for c in range(10):
                    kc = 2 * jb + c
                    t.matmul(pat[:, 0:256], vv[:, kc * 512 + g4 * 128:
                                               kc * 512 + (g4 + 1) * 128],
                             P[:, c * 256:(c + 1) * 256],
                             start=(c == 0), stop=(c == 9))
                    ins = t.matmul(psm[0:1, 0:256], valida[:, kc:kc + 1],
                                   P[:, c * 256:(c + 1) * 256],
                                   start=(c == 0), stop=(c == 9))
                s_pv.inc(ins)

            block.tensor(mmpv)
            pvn = s_pv.n
            if _os.environ.get('DEV_ATT_OPS', 'full') == 'pv':
                continue

            def cps(s, pvn=pvn):
                s.wait_ge(s_pv.h, pvn)
                s_sm.inc(s.copy(sum_sb[:], psm[0:1, 0:256]))

            block.scalar(cps)
            smn = s_sm.n

            def rcp(v, smn=smn):
                v.wait_ge(s_sm.h, smn)
                s_rc.inc(v.reciprocal(rec_sb[:], sum_sb[:]))

            block.vector(rcp)
            rcn = s_rc.n

            def bca(t, rcn=rcn, rbgate=s_rb.n):
                t.wait_ge(s_rc.h, rcn)
                if rbgate:
                    t.wait_ge(s_rb.h, rbgate)
                s_bc.inc(t.matmul(prb[:, 0:256], oner[:], rec_sb[:],
                                  start=True, stop=True))

            block.tensor(bca)
            bcn = s_bc.n

            def evat(v, bcn=bcn, jb=jb, h=h):
                v.wait_ge(s_bc.h, bcn)
                s_rb.inc(v.tensor_copy(recB[:], prb[:, 0:256]))
                v.wait_ge(s_rb.h, s_rb.n)
                s_ev.inc(v.tensor_mul(
                    attT[:, h * 2048 + jb * 256: h * 2048 + (jb + 1) * 256],
                    pat[:, 0:256], recB[:]))

            block.vector(evat)
        qsl_free[jb % 2] = s_sc.n

    ATT_ALL = s_ev.n
    st1.close()
    if _os.environ.get('DEV_STOP', 'full') == 'att':
        stA.close()
        ops = _os.environ.get('DEV_ATT_OPS', 'full')
        sf = {'sc': s_ex, 'msk': s_mk, 'pv': s_pv, 'full': s_ev}[ops]
        FN = sf.n

        class _O:
            def barrier(self, eng):
                eng.wait_ge(sf.h, FN)
        return _O()

    st2 = contextlib.ExitStack()
    wo_sb = C.sbuf("a1wo", [128, 16 * 2048], BF16, st2)
    ysb = [C.sbuf(f"a1y{i}", [128, 2048], F32, st2) for i in range(2)]
    pyb = [C.psum(f"a1py{i}", [128, 512], st2) for i in range(2)]

    def ldwo(g):
        g.wait_ge(s_ev.h, ATT_ALL)
        for mc in range(4):
            s_wo.inc(g.dma_start(
                out=wo_sb[:, mc * 8192:(mc + 1) * 8192],
                in_=wo_a[:, mc * 8192:(mc + 1) * 8192]), 16)

    block.gpsimd(ldwo)

    res_at = {}
    for tc in range(16):
        for fb in range(4):
            i = 4 * tc + fb

            def mmy(t, tc=tc, fb=fb, i=i, yegate=s_yev.n):
                if i == 0:
                    t.wait_ge(s_wo.h, 64)
                if yegate >= 2:
                    t.wait_ge(s_yev.h, yegate - 1)
                wo3 = wo_sb[:].rearrange("p (m f) -> p m f", m=16)
                pb = pyb[i % 2]
                t.matmul(pb[:], onerb[:], bo[:, fb * 512:(fb + 1) * 512],
                         start=True, stop=False)
                for mc in range(16):
                    ins = t.matmul(
                        pb[:],
                        attT[:, mc * 2048 + tc * 128: mc * 2048 + (tc + 1) * 128],
                        wo3[:, mc, fb * 512:(fb + 1) * 512],
                        start=False, stop=(mc == 15))
                s_ymm.inc(ins)

            block.tensor(mmy)
            ymn = s_ymm.n

            def evy(v, tc=tc, fb=fb, i=i, ymn=ymn,
                    rgate=res_at.get(tc - 2, 0)):
                v.wait_ge(s_ymm.h, ymn)
                if fb == 0 and rgate:
                    v.wait_ge(s_res[tc % 2].h, rgate)
                s_yev.inc(v.tensor_copy(
                    ysb[tc % 2][:, fb * 512:(fb + 1) * 512], pyb[i % 2][:]))

            block.vector(evy)

        yn = s_yev.n

        def acc(g, tc=tc, yn=yn):
            g.wait_ge(s_yev.h, yn)
            s_res[tc % 2].inc(g.dma_start(
                out=out_x[tc * 128:(tc + 1) * 128, :],
                in_=ysb[tc % 2][:], accum_op=ALU.add), 16)

        block.gpsimd(acc)
        res_at[tc] = s_res[tc % 2].n

    st2.close()
    stA.close()
    R0, R1 = s_res[0].n, s_res[1].n

    class ResOut:
        def barrier(self, eng):
            eng.wait_ge(s_res[0].h, R0)
            eng.wait_ge(s_res[1].h, R1)

    return ResOut()




def emit_moe(C, io, tag, prev):
    block = C.block
    out_x, xn_dram = io["out_x"], io["xn_dram"]
    gate_wT, w1T, w3T, w2T = io["gate_wT"], io["w1T"], io["w3T"], io["w2T"]
    seg_x, seg_meta, y01 = io["seg_x"], io["seg_meta"], io["y01"]
    oner, onec = io["oner"], io["onec"]
    ltri, tokf, base_r, mfill = io["ltri"], io["tokf"], io["base_r"], io["mfill"]

    st = contextlib.ExitStack()
    gw = C.sbuf(f"{tag}gw", [128, 16 * 8], BF16, st)
    limB = C.sbuf(f"{tag}limB", [128, NE], F32, st)
    xnc = [C.sbuf(f"{tag}xnc{i}", [128, 16 * 128], BF16, st) for i in range(2)]
    xsc = [C.sbuf(f"{tag}xsc{i}", [128, D], BF16, st) for i in range(2)]
    gsc = C.sbuf(f"{tag}gsc", [128, 8], F32, st)
    gex = C.sbuf(f"{tag}gex", [128, 8], F32, st)
    gst = C.sbuf(f"{tag}gst", [128, 8], F32, st)
    probs = C.sbuf(f"{tag}pr", [128, 8], F32, st)
    s0t = C.sbuf(f"{tag}s0", [128, 8], F32, st)
    rt = C.sbuf(f"{tag}rt", [128, 8], F32, st)
    s1t = C.sbuf(f"{tag}s1", [128, 8], F32, st)
    rb16 = C.sbuf(f"{tag}rb", [128, 8], BF16, st)
    lc = C.sbuf(f"{tag}lc", [128, 8], F32, st)
    obB = C.sbuf(f"{tag}obB", [128, 8], F32, st)
    dst = C.sbuf(f"{tag}dst", [128, 8], F32, st)
    toti = C.sbuf(f"{tag}toti", [1, 16 * 8], F32, st)
    offs = C.sbuf(f"{tag}offs", [1, 16 * 8], F32, st)
    obk = C.sbuf(f"{tag}obk", [1, 8], F32, st)
    sm1 = C.sbuf(f"{tag}sm1", [128, 1], F32, st)
    sm2 = C.sbuf(f"{tag}sm2", [128, 1], F32, st)
    nmx = C.sbuf(f"{tag}nmx", [128, 1], F32, st)
    rs = C.sbuf(f"{tag}rs", [128, 1], F32, st)
    rc1 = C.sbuf(f"{tag}rc1", [128, 1], F32, st)
    di0 = [C.sbuf(f"{tag}dia{i}", [128, 1], I32, st) for i in range(2)]
    di1 = [C.sbuf(f"{tag}dib{i}", [128, 1], I32, st) for i in range(2)]
    dd = C.sbuf(f"{tag}dd", [128, 2], F32, st)
    mv0 = [C.sbuf(f"{tag}mva{i}", [128, 2], F32, st) for i in range(2)]
    mv1 = [C.sbuf(f"{tag}mvb{i}", [128, 2], F32, st) for i in range(2)]
    mfl = C.sbuf(f"{tag}mfl", [128, 256], F32, st)

    pg = C.psum(f"{tag}pg", [128, 512], st)
    pl = C.psum(f"{tag}pl", [128, 512], st)
    pt = C.psum(f"{tag}pt", [128, 512], st)
    pbk = C.psum(f"{tag}pbk", [128, 512], st)

    lnstack = contextlib.ExitStack()
    ln = emit_ln(C, f"{tag}ln", NCH,
                 lambda k: out_x[k * 128:(k + 1) * 128, :],
                 xn_dram, HALO, [], stack=lnstack, prev=prev)
    lnstack.close()

    s_w = C.sem(f"{tag}_w")
    s_mi = C.sem(f"{tag}_mi")
    s_tl = [C.sem(f"{tag}_tl{i}") for i in range(2)]
    s_xl = [C.sem(f"{tag}_xl{i}") for i in range(2)]
    s_mm = C.sem(f"{tag}_mm")
    s_a = C.sem(f"{tag}_a")
    s_v = C.sem(f"{tag}_v")
    s_sg = [C.sem(f"{tag}_sg{i}") for i in range(2)]

    lim_sb = C.sbuf(f"{tag}lims", [1, NE], F32, st)
    w_base = s_w.n

    def ldg(g):
        prev.barrier(g)
        s_w.inc(g.dma_start(out=gw[:], in_=gate_wT), 16)
        s_w.inc(g.dma_start(out=mfl[:], in_=mfill), 16)
        s_w.inc(g.dma_start(out=lim_sb[:], in_=io["lim_r"]), 16)

    block.gpsimd(ldg)

    def limmm(t, wn=w_base + 48):
        t.wait_ge(s_w.h, wn)
        s_mm.inc(t.matmul(pbk[0:128, 0:NE], oner[:], lim_sb[:],
                          start=True, stop=True))

    block.tensor(limmm)
    _limn = s_mm.n

    def limev(v, n=_limn):
        v.wait_ge(s_mm.h, n)
        s_v.inc(v.tensor_copy(limB[:], pbk[0:128, 0:NE]))

    block.vector(limev)
    LIMV = s_v.n

    def mini(sy, wn=w_base + 48):
        sy.wait_ge(s_w.h, wn)
        s_mi.inc(sy.dma_start(
            out=seg_meta.rearrange("(n p) c -> p n c", p=128),
            in_=mfl[0:128, 0:2 * (NE * CAP // 128)]
                .rearrange("p (n c) -> p n c", c=2)), 16)

    block.sync(mini)

    small_free = [0, 0]
    xnc_free = [0, 0]

    for k in range(NCH):
        bi = k % 2

        def tld(sy, k=k, bi=bi, gate=xnc_free[bi], sgate=small_free[bi]):
            ln.chunk_barrier(sy, k)
            if gate:
                sy.wait_ge(s_mm.h, gate)
            if sgate:
                sy.wait_ge(s_sg[bi].h, sgate)
            s_tl[bi].inc(sy.dma_start(
                out=xnc[bi][:].rearrange("p (dc r) -> p dc r", dc=16),
                in_=xn_dram[HALO + k * 128: HALO + (k + 1) * 128, :]
                    .rearrange("r (dc p) -> r dc p", p=128),
                transpose=True), 16)
            s_xl[bi].inc(sy.dma_start(
                out=xsc[bi][:],
                in_=xn_dram[HALO + k * 128: HALO + (k + 1) * 128, :]), 16)

        block.sync(tld)
        tln = s_tl[bi].n

        def gmm(t, k=k, bi=bi, tln=tln, an=s_a.n, vn=s_v.n):
            t.wait_ge(s_tl[bi].h, tln)
            if k == 0:
                t.wait_ge(s_w.h, w_base + 48)
            if an:
                t.wait_ge(s_a.h, an)
            x3 = xnc[bi][:].rearrange("p (dc r) -> p dc r", dc=16)
            g3 = gw[:].rearrange("p (dc e) -> p dc e", dc=16)
            for dc in range(16):
                ins = t.matmul(pg[0:128, 0:8], x3[:, dc, :], g3[:, dc, :],
                               start=(dc == 0), stop=(dc == 15))
            s_mm.inc(ins)

        block.tensor(gmm)
        gmn = s_mm.n
        xnc_free[bi] = gmn

        def gev(s, gmn=gmn):
            s.wait_ge(s_mm.h, gmn)
            s_a.inc(s.copy(gsc[:], pg[0:128, 0:8]))

        block.scalar(gev)
        gan = s_a.n

        def rt1(v, gan=gan, vprior=s_v.n):
            v.wait_ge(s_v.h, vprior)
            v.wait_ge(s_a.h, gan)
            s_v.inc(v.reduce_max(sm1[:], gsc[:], AX.X))
            v.wait_ge(s_v.h, s_v.n)
            s_v.inc(v.tensor_scalar(nmx[:], sm1[:], -1.0, None, ALU.mult))

        block.vector(rt1)
        vn1 = s_v.n

        def rt2(s, vn1=vn1):
            s.wait_ge(s_v.h, vn1)
            s_a.inc(s.activation(gex[:], gsc[:], AF.Exp, bias=nmx[:]))

        block.scalar(rt2)
        an2 = s_a.n

        def rt3(v, an2=an2, k=k, vprior=s_v.n):
            v.wait_ge(s_v.h, vprior)
            v.wait_ge(s_a.h, an2)
            s_v.inc(v.tensor_reduce(rs[:], gex[:], AX.X, ALU.add))
            v.wait_ge(s_v.h, s_v.n)
            s_v.inc(v.reciprocal(rc1[:], rs[:]))
            v.wait_ge(s_v.h, s_v.n)
            s_v.inc(v.tensor_scalar(probs[:], gex[:], rc1[:], None, ALU.mult))
            v.wait_ge(s_v.h, s_v.n)
            s_v.inc(v.reduce_max(sm1[:], probs[:], AX.X))
            v.wait_ge(s_v.h, s_v.n)
            s_v.inc(v.tensor_scalar(s0t[:], probs[:], sm1[:], None, ALU.is_ge))
            v.wait_ge(s_v.h, s_v.n)
            s_v.inc(v.scalar_tensor_tensor(gst[:], s0t[:], -1.0e30, probs[:],
                                           ALU.mult, ALU.add))
            v.wait_ge(s_v.h, s_v.n)
            s_v.inc(v.reduce_max(sm2[:], gst[:], AX.X))
            v.wait_ge(s_v.h, s_v.n)
            s_v.inc(v.tensor_scalar(rt[:], probs[:], sm2[:], None, ALU.is_ge))
            v.wait_ge(s_v.h, s_v.n)
            s_v.inc(v.tensor_sub(s1t[:], rt[:], s0t[:]))
            s_v.inc(v.tensor_copy(rb16[:], rt[:]))

        block.vector(rt3)
        vn3 = s_v.n

        def lmm(t, vn3=vn3, an=s_a.n):
            t.wait_ge(s_v.h, vn3)
            if an > 2:
                t.wait_ge(s_a.h, an)
            s_mm.inc(t.matmul(pl[0:128, 0:8], ltri[:], rb16[:],
                              start=True, stop=True))
            s_mm.inc(t.matmul(pt[0:1, 0:8], onec[:], rb16[:],
                              start=True, stop=True))

        block.tensor(lmm)
        lmn = s_mm.n

        def lev(s, lmn=lmn, k=k):
            s.wait_ge(s_mm.h, lmn)
            s_a.inc(s.copy(lc[:], pl[0:128, 0:8]))
            s_a.inc(s.copy(toti[0:1, k * 8:(k + 1) * 8], pt[0:1, 0:8]))

        block.scalar(lev)
        lan = s_a.n

        def pfx(v, lan=lan, k=k, vprior=s_v.n):
            v.wait_ge(s_v.h, vprior)
            v.wait_ge(s_a.h, lan)
            if k == 0:
                s_v.inc(v.memset(offs[0:1, 0:8], 0.0))
            else:
                s_v.inc(v.tensor_add(offs[0:1, k * 8:(k + 1) * 8],
                                     offs[0:1, (k - 1) * 8:k * 8],
                                     toti[0:1, (k - 1) * 8:k * 8]))
            v.wait_ge(s_v.h, s_v.n)
            s_v.inc(v.tensor_add(obk[:], offs[0:1, k * 8:(k + 1) * 8],
                                 base_r[:]))

        block.vector(pfx)
        pfn = s_v.n

        def bmm(t, pfn=pfn, vn=s_v.n):
            t.wait_ge(s_v.h, pfn)
            s_mm.inc(t.matmul(pbk[0:128, 0:8], oner[:], obk[:],
                              start=True, stop=True))

        block.tensor(bmm)
        bmn = s_mm.n

        def dcmp(v, bmn=bmn, k=k, bi=bi, gate=small_free[bi], vprior=s_v.n):
            v.wait_ge(s_v.h, vprior)
            v.wait_ge(s_mm.h, bmn)
            s_v.inc(v.tensor_copy(obB[:], pbk[0:128, 0:8]))
            v.wait_ge(s_v.h, s_v.n)
            s_v.inc(v.tensor_add(dst[:], lc[:], obB[:]))
            v.wait_ge(s_v.h, s_v.n)
            s_v.inc(v.tensor_scalar(gst[:], dst[:], 1.0e5, None, ALU.subtract))
            v.wait_ge(s_v.h, s_v.n)
            s_v.inc(v.tensor_mul(gst[:], gst[:], rt[:]))
            v.wait_ge(s_v.h, s_v.n)
            s_v.inc(v.tensor_scalar(dst[:], gst[:], 1.0e5, None, ALU.add))
            v.wait_ge(s_v.h, s_v.n)
            v.wait_ge(s_v.h, LIMV)
            s_v.inc(v.tensor_tensor(gst[:], dst[:], limB[:], ALU.is_ge))
            v.wait_ge(s_v.h, s_v.n)
            s_v.inc(v.scalar_tensor_tensor(dst[:], gst[:], 2.0e5, dst[:],
                                           ALU.mult, ALU.add))
            v.wait_ge(s_v.h, s_v.n)
            if gate:
                v.wait_ge(s_sg[bi].h, gate)
            s_v.inc(v.tensor_mul(gst[:], s0t[:], dst[:]))
            v.wait_ge(s_v.h, s_v.n)
            s_v.inc(v.tensor_reduce(dd[:, 0:1], gst[:], AX.X, ALU.add))
            v.wait_ge(s_v.h, s_v.n)
            s_v.inc(v.tensor_mul(gst[:], s1t[:], dst[:]))
            v.wait_ge(s_v.h, s_v.n)
            s_v.inc(v.tensor_reduce(dd[:, 1:2], gst[:], AX.X, ALU.add))
            v.wait_ge(s_v.h, s_v.n)
            s_v.inc(v.tensor_copy(di0[bi][:], dd[:, 0:1]))
            s_v.inc(v.tensor_copy(di1[bi][:], dd[:, 1:2]))
            s_v.inc(v.tensor_scalar(mv0[bi][:, 0:1], tokf[:, k:k + 1],
                                    0.0, None, ALU.add))
            s_v.inc(v.tensor_scalar(mv1[bi][:, 0:1], tokf[:, k:k + 1],
                                    float(TOK), None, ALU.add))
            s_v.inc(v.tensor_copy(mv0[bi][:, 1:2], sm1[:]))
            s_v.inc(v.tensor_copy(mv1[bi][:, 1:2], sm2[:]))

        block.vector(dcmp)
        dcn = s_v.n

        def sct(g, k=k, bi=bi, dcn=dcn, xln=s_xl[bi].n, min_=s_mi.n):
            g.wait_ge(s_v.h, dcn)
            g.wait_ge(s_xl[bi].h, xln)
            if k < 2:
                g.wait_ge(s_mi.h, min_)
                if tag == "m2":
                    g.wait_ge(io["s_zi"].h, io["zi_n"])
            s_sg[bi].inc(g.indirect_dma_start(
                seg_x, bass.IndirectOffsetOnAxis(di0[bi][:], 0),
                xsc[bi][:], None,
                bounds_check=io['bnd_seg'], oob_is_err=False), 16)
            s_sg[bi].inc(g.indirect_dma_start(
                seg_x, bass.IndirectOffsetOnAxis(di1[bi][:], 0),
                xsc[bi][:], None,
                bounds_check=io['bnd_seg'], oob_is_err=False), 16)
            s_sg[bi].inc(g.indirect_dma_start(
                seg_meta, bass.IndirectOffsetOnAxis(di0[bi][:], 0),
                mv0[bi][:], None,
                bounds_check=io['bnd_seg'], oob_is_err=False), 16)

        block.gpsimd(sct)

        def sct2(g, bi=bi):
            s_sg[bi].inc(g.indirect_dma_start(
                seg_meta, bass.IndirectOffsetOnAxis(di1[bi][:], 0),
                mv1[bi][:], None,
                bounds_check=io['bnd_seg'], oob_is_err=False), 16)

        block.gpsimd(sct2)
        small_free[bi] = s_sg[bi].n

    SG = [s_sg[0].n, s_sg[1].n]
    st.close()

    stE = contextlib.ExitStack()
    hT = C.sbuf(f"{tag}hT", [128, 16 * CAP], BF16, stE)
    sil = C.sbuf(f"{tag}sil", [128, 512], BF16, stE)
    s_xg = C.sem(f"{tag}_xg")
    s_wc = [C.sem(f"{tag}_wc{i}") for i in range(2)]
    s_w2 = C.sem(f"{tag}_w2")
    s_he = C.sem(f"{tag}_he")
    s_hs = C.sem(f"{tag}_hs")
    s_ml = [C.sem(f"{tag}_ml{i}") for i in range(2)]
    s_ye = C.sem(f"{tag}_ye")
    s_ysc = [C.sem(f"{tag}_ysc{i}") for i in range(2)]

    ph1 = C.psum(f"{tag}ph1", [128, 512], stE)
    ph3 = C.psum(f"{tag}ph3", [128, 512], stE)
    py = [C.psum(f"{tag}py{i}", [128, 512], stE) for i in range(2)]

    for e in range(NE):
        stX = contextlib.ExitStack()
        xgT = C.sbuf(f"{tag}xg{e}", [128, 16 * CAP], BF16, stX)
        w1cb = [C.sbuf(f"{tag}w1c{e}{i}", [128, 16 * 128], BF16, stX)
                for i in range(2)]
        w3cb = [C.sbuf(f"{tag}w3c{e}{i}", [128, 16 * 128], BF16, stX)
                for i in range(2)]

        def xgl(sy, e=e, hen=s_he.n, mmn=s_mm.n,
                y0=s_ysc[0].n, y1=s_ysc[1].n):
            if e == 0:
                sy.wait_ge(s_sg[0].h, SG[0])
                sy.wait_ge(s_sg[1].h, SG[1])
            if hen:
                sy.wait_ge(s_he.h, hen)
            if e > 0:
                sy.wait_ge(s_mm.h, mmn)
                sy.wait_ge(s_ysc[0].h, y0)
                sy.wait_ge(s_ysc[1].h, y1)
            s_xg.inc(sy.dma_start(
                out=xgT[:].rearrange("p (dc r) -> p dc r", dc=16),
                in_=seg_x[e * CAP:(e + 1) * CAP, :]
                    .rearrange("r (dc p) -> r dc p", p=128),
                transpose=True), 16)

        block.sync(xgl)
        xgn = s_xg.n

        for fc in range(16):
            wb = fc % 2

            def wld(g, e=e, fc=fc, wb=wb, gate=s_mm.n,
                    y0=s_ysc[0].n, y1=s_ysc[1].n):
                if fc >= 2:
                    g.wait_ge(s_mm.h, gate)
                if e > 0 and fc < 2:
                    g.wait_ge(s_ysc[0].h, y0)
                    g.wait_ge(s_ysc[1].h, y1)
                s_wc[wb].inc(g.dma_start(out=w1cb[wb][:],
                                         in_=w1T[e, fc]), 16)
                s_wc[wb].inc(g.dma_start(out=w3cb[wb][:],
                                         in_=w3T[e, fc]), 16)

            block.gpsimd(wld)
            wcn = s_wc[wb].n

            for rc in range(CAP // 512):
                def hmm(t, fc=fc, rc=rc, wb=wb, wcn=wcn, xgn=xgn,
                        heg=s_he.n, hsg=s_hs.n):
                    t.wait_ge(s_wc[wb].h, wcn)
                    if fc == 0 and rc == 0:
                        t.wait_ge(s_xg.h, xgn)
                    if hsg:
                        t.wait_ge(s_hs.h, hsg)
                    if heg:
                        t.wait_ge(s_he.h, heg)
                    w13 = w1cb[wb][:].rearrange("p (dc f) -> p dc f", dc=16)
                    w33 = w3cb[wb][:].rearrange("p (dc f) -> p dc f", dc=16)
                    x3 = xgT[:].rearrange("p (dc r) -> p dc r", dc=16)
                    for dc in range(16):
                        t.matmul(ph1[:], w13[:, dc, :],
                                 x3[:, dc, rc * 512:(rc + 1) * 512],
                                 start=(dc == 0), stop=(dc == 15))
                    for dc in range(16):
                        ins = t.matmul(ph3[:], w33[:, dc, :],
                                       x3[:, dc, rc * 512:(rc + 1) * 512],
                                       start=(dc == 0), stop=(dc == 15))
                    s_mm.inc(ins)

                block.tensor(hmm)
                hmn = s_mm.n

                def hsl(s, hmn=hmn):
                    s.wait_ge(s_mm.h, hmn)
                    s_hs.inc(s.activation(sil[:], ph1[:], AF.Silu))

                block.scalar(hsl)
                hsn = s_hs.n

                def hev(v, hsn=hsn, fc=fc, rc=rc):
                    v.wait_ge(s_hs.h, hsn)
                    s_he.inc(v.tensor_mul(
                        hT[:, fc * CAP + rc * 512: fc * CAP + (rc + 1) * 512],
                        ph3[:], sil[:]))

                block.vector(hev)

        HDONE = s_he.n
        stX.close()
        stW2 = contextlib.ExitStack()
        w2sb = C.sbuf(f"{tag}w2{e}", [128, 16 * D], BF16, stW2)
        ysb = [C.sbuf(f"{tag}ys{e}{i}", [128, D], F32, stW2) for i in range(2)]
        mrow = [C.sbuf(f"{tag}mr{e}{i}", [128, 2], F32, stW2) for i in range(2)]
        dix = [C.sbuf(f"{tag}dx{e}{i}", [128, 1], I32, stW2) for i in range(2)]

        def w2l(g, e=e, hd=HDONE):
            g.wait_ge(s_he.h, hd)
            for sl in range(4):
                s_w2.inc(g.dma_start(
                    out=w2sb[:, sl * 8192:(sl + 1) * 8192],
                    in_=w2T[e][:, sl * 8192:(sl + 1) * 8192]), 16)

        block.gpsimd(w2l)
        w2n = s_w2.n

        for rc in range(CAP // 128):
            bi = rc % 2

            def mld(sy, e=e, rc=rc, bi=bi, gate=s_ysc[bi].n, min_=s_mi.n):
                if gate:
                    sy.wait_ge(s_ysc[bi].h, gate)
                sy.wait_ge(s_mi.h, min_)
                s_ml[bi].inc(sy.dma_start(
                    out=mrow[bi][:],
                    in_=seg_meta[e * CAP + rc * 128: e * CAP + (rc + 1) * 128, :]),
                    16)

            block.sync(mld)
            mln = s_ml[bi].n

            def mcv(v, bi=bi, mln=mln, vprior=s_v.n):
                v.wait_ge(s_v.h, vprior)
                v.wait_ge(s_ml[bi].h, mln)
                s_v.inc(v.tensor_copy(dix[bi][:], mrow[bi][:, 0:1]))

            block.vector(mcv)
            mcn = s_v.n

            for db in range(4):
                i = 4 * rc + db

                def ymm(t, rc=rc, db=db, i=i, w2n=w2n, yeg=s_ye.n):
                    if i == 0:
                        t.wait_ge(s_w2.h, w2n)
                    if yeg >= 2:
                        t.wait_ge(s_ye.h, yeg - 1)
                    w23 = w2sb[:].rearrange("p (Fc d) -> p Fc d", Fc=16)
                    h3 = hT[:].rearrange("p (Fc r) -> p Fc r", Fc=16)
                    for Fc in range(16):
                        ins = t.matmul(py[i % 2][:],
                                       h3[:, Fc, rc * 128:(rc + 1) * 128],
                                       w23[:, Fc, db * 512:(db + 1) * 512],
                                       start=(Fc == 0), stop=(Fc == 15))
                    s_mm.inc(ins)

                block.tensor(ymm)
                ymn = s_mm.n

                def yev(v, bi=bi, db=db, i=i, ymn=ymn, mcn=mcn):
                    v.wait_ge(s_mm.h, ymn)
                    v.wait_ge(s_v.h, mcn)
                    s_ye.inc(v.tensor_scalar(
                        ysb[bi][:, db * 512:(db + 1) * 512], py[i % 2][:],
                        mrow[bi][:, 1:2], None, ALU.mult))

                block.vector(yev)

            yen = s_ye.n

            def ysct(g, bi=bi, yen=yen):
                g.wait_ge(s_ye.h, yen)
                s_ysc[bi].inc(g.indirect_dma_start(
                    y01, bass.IndirectOffsetOnAxis(dix[bi][:], 0),
                    ysb[bi][:], None,
                    bounds_check=io['bnd_y'], oob_is_err=False), 16)

            block.gpsimd(ysct)

        stW2.close()

    YS = [s_ysc[0].n, s_ysc[1].n]
    stE.close()

    stR = contextlib.ExitStack()
    y0b = [C.sbuf(f"{tag}y0{i}", [128, D], F32, stR) for i in range(2)]
    y1b = [C.sbuf(f"{tag}y1{i}", [128, D], F32, stR) for i in range(2)]
    s_yl = [C.sem(f"{tag}_yl{i}") for i in range(2)]
    s_ra = C.sem(f"{tag}_ra")
    s_res = [C.sem(f"{tag}_res{i}") for i in range(2)]
    res_at = [0, 0]

    for k in range(NCH):
        bi = k % 2

        def yld(sy, k=k, bi=bi, gate=res_at[bi]):
            if k == 0:
                sy.wait_ge(s_ysc[0].h, YS[0])
                sy.wait_ge(s_ysc[1].h, YS[1])
            if gate:
                sy.wait_ge(s_res[bi].h, gate)
            s_yl[bi].inc(sy.dma_start(out=y0b[bi][:],
                                      in_=y01[k * 128:(k + 1) * 128, :]), 16)
            s_yl[bi].inc(sy.dma_start(
                out=y1b[bi][:], in_=y01[TOK + k * 128: TOK + (k + 1) * 128, :]),
                16)

        block.sync(yld)
        yln = s_yl[bi].n

        def radd(v, bi=bi, yln=yln):
            v.wait_ge(s_yl[bi].h, yln)
            s_ra.inc(v.tensor_add(y0b[bi][:], y0b[bi][:], y1b[bi][:]))

        block.vector(radd)
        ran = s_ra.n

        def racc(g, k=k, bi=bi, ran=ran):
            g.wait_ge(s_ra.h, ran)
            s_res[bi].inc(g.dma_start(
                out=out_x[k * 128:(k + 1) * 128, :],
                in_=y0b[bi][:], accum_op=ALU.add), 16)

        block.gpsimd(racc)
        res_at[bi] = s_res[bi].n

    stR.close()
    R = [s_res[0].n, s_res[1].n]

    class ResOut:
        def barrier(self, eng):
            eng.wait_ge(s_res[0].h, R[0])
            eng.wait_ge(s_res[1].h, R[1])

    return ResOut()




def emit_attn_b(C, io, prev):
    block = C.block
    out_x, xn_dram, q_dram = io["out_x"], io["xn_dram"], io["q_dram"]
    wq_b, wkv_b, wo_b = io["wq_b"], io["wkv_b"], io["wo_b"]
    kv_contrib, kv_all, kv_win = io["kv_contrib"], io["kv_all"], io["kv_win"]
    oner, onec, onerb = io["oner"], io["onec"], io["onerb"]
    bo = io["bo_b_sb"]

    stP = contextlib.ExitStack()
    comp = C.sbuf("b3comp", [128, NCH], F32, stP)
    ropes = C.sbuf("b3rsin", [128, 16 * 64], F32, stP)
    ropec = C.sbuf("b3rcos", [128, 16 * 64], F32, stP)
    wrow = C.sbuf("b3w", [128, 16 * 2048], BF16, stP)
    xnc = [C.sbuf(f"b3xnc{i}", [128, 16 * 128], BF16, stP) for i in range(2)]
    qf = C.sbuf("b3qf", [128, 2048], F32, stP)
    t1 = C.sbuf("b3t1", [128, 64], F32, stP)
    t2 = C.sbuf("b3t2", [128, 64], F32, stP)
    orow = [C.sbuf(f"b3or{i}", [128, 2048], BF16, stP) for i in range(2)]

    pq = [C.psum("b3pq%d" % i, [128, 512], stP) for i in range(2)]

    s_rl = C.sem("b3_rl")
    s_wl = C.sem("b3_wl")
    s_tl = [C.sem(f"b3_tl{i}") for i in range(2)]
    s_mm = C.sem("b3_mm")
    s_a = C.sem("b3_a")
    s_v = C.sem("b3_v")
    s_p = C.sem("b3_p")
    s_st = [C.sem(f"b3_st{i}") for i in range(2)]
    s_cc = C.sem("b3_cc")
    s_wg = C.sem("b3_wg")
    s_ws = C.sem("b3_ws")

    lnstack = contextlib.ExitStack()
    ln = emit_ln(C, "b3ln", NCH,
                 lambda k: out_x[k * 128:(k + 1) * 128, :],
                 xn_dram, HALO, [], comp_out=comp, stack=lnstack, prev=prev)
    lnstack.close()

    def ldr(sy):
        s_rl.inc(sy.dma_start(out=ropes[:], in_=io["rope_sin"]), 16)
        s_rl.inc(sy.dma_start(out=ropec[:], in_=io["rope_cos"]), 16)

    block.sync(ldr)

    orow_st = [0, 0]
    for pas in range(2):
        def ldwb(g, pas=pas, mmn=s_mm.n):
            if pas == 0:
                ln.barrier(g)
            else:
                g.wait_ge(s_mm.h, mmn)
            s_wl.inc(g.dma_start(out=wrow[:],
                                 in_=(wq_b if pas == 0 else wkv_b)), 16)

        block.gpsimd(ldwb)
        wn = s_wl.n

        for k in range(NCH):
            bi = k % 2

            def tldb(sy, k=k, bi=bi, pas=pas, gate=s_mm.n):
                if k >= 2 or pas == 1:
                    sy.wait_ge(s_mm.h, gate)
                ln.chunk_barrier(sy, k)
                s_tl[bi].inc(sy.dma_start(
                    out=xnc[bi][:].rearrange("p (dc r) -> p dc r", dc=16),
                    in_=xn_dram[HALO + k * 128: HALO + (k + 1) * 128, :]
                        .rearrange("r (dc p) -> r dc p", p=128),
                    transpose=True), 16)

            block.sync(tldb)
            tln = s_tl[bi].n

            ev_at = [0, 0]
            for fb in range(4):
                def qmmf(t, k=k, bi=bi, fb=fb, tln=tln, wn=wn,
                         gate=ev_at[fb % 2]):
                    if fb == 0:
                        t.wait_ge(s_tl[bi].h, tln)
                        t.wait_ge(s_wl.h, wn)
                    if gate:
                        t.wait_ge(s_v.h, gate)
                    x3 = xnc[bi][:].rearrange("p (dc r) -> p dc r", dc=16)
                    w3 = wrow[:].rearrange("p (dc f) -> p dc f", dc=16)
                    for dc in range(16):
                        ins = t.matmul(pq[fb % 2][:], x3[:, dc, :],
                                       w3[:, dc, fb * 512:(fb + 1) * 512],
                                       start=(dc == 0), stop=(dc == 15))
                    s_mm.inc(ins)

                block.tensor(qmmf)
                mmn = s_mm.n

                if pas == 0:
                    def qevf(v, k=k, fb=fb, mmn=mmn, vprior=s_v.n):
                        v.wait_ge(s_v.h, vprior)
                        v.wait_ge(s_mm.h, mmn)
                        s_v.inc(v.tensor_scalar(
                            qf[:, fb * 512:(fb + 1) * 512], pq[fb % 2][:],
                            comp[:, k:k + 1], None, ALU.mult))
                else:
                    def qevf(v, k=k, fb=fb, mmn=mmn, vprior=s_v.n,
                             og=orow_st[k % 2]):
                        v.wait_ge(s_v.h, vprior)
                        v.wait_ge(s_mm.h, mmn)
                        if fb < 2:
                            s_v.inc(v.tensor_copy(
                                qf[:, fb * 512:(fb + 1) * 512], pq[fb % 2][:]))
                        else:
                            if fb == 2 and og:
                                v.wait_ge(s_st[k % 2].h, og)
                            s_v.inc(v.tensor_copy(
                                orow[k % 2][:, fb * 512:(fb + 1) * 512],
                                pq[fb % 2][:]))

                block.vector(qevf)
                ev_at[fb % 2] = s_v.n

            if pas == 0:
                def qev(v, k=k, vprior=s_v.n, og=orow_st[k % 2]):
                    v.wait_ge(s_v.h, vprior)
                    v.wait_ge(s_rl.h, 32)
                    if og:
                        v.wait_ge(s_st[k % 2].h, og)
                    for h in range(16):
                        q1 = qf[:, h * 128:h * 128 + 64]
                        q2 = qf[:, h * 128 + 64:h * 128 + 128]
                        ck = ropec[:, k * 64:(k + 1) * 64]
                        sk = ropes[:, k * 64:(k + 1) * 64]
                        v.wait_ge(s_v.h, s_v.n)
                        s_v.inc(v.tensor_mul(t1[:], q1, ck))
                        s_v.inc(v.tensor_mul(t2[:], q2, sk))
                        v.wait_ge(s_v.h, s_v.n)
                        s_v.inc(v.tensor_sub(
                            orow[k % 2][:, h * 128:h * 128 + 64], t1[:], t2[:]))
                        v.wait_ge(s_v.h, s_v.n)
                        s_v.inc(v.tensor_mul(t1[:], q1, sk))
                        s_v.inc(v.tensor_mul(t2[:], q2, ck))
                        v.wait_ge(s_v.h, s_v.n)
                        s_v.inc(v.tensor_add(
                            orow[k % 2][:, h * 128 + 64:h * 128 + 128],
                            t1[:], t2[:]))

                block.vector(qev)
                evn = s_v.n

                def qst(sy, k=k, evn=evn, bi=k % 2, stn=s_st[k % 2].n):
                    sy.wait_ge(s_v.h, evn)
                    s_st[bi].inc(sy.dma_start(
                        out=q_dram[k * 128:(k + 1) * 128, :],
                        in_=orow[bi][:]), 16)

                block.sync(qst)
                orow_st[k % 2] = s_st[k % 2].n
            else:
                def kev(v, k=k, vprior=s_v.n, og=orow_st[k % 2]):
                    v.wait_ge(s_v.h, vprior)
                    v.wait_ge(s_rl.h, 32)
                    if og:
                        v.wait_ge(s_st[k % 2].h, og)
                    for h in range(8):
                        q1 = qf[:, h * 128:h * 128 + 64]
                        q2 = qf[:, h * 128 + 64:h * 128 + 128]
                        ck = ropec[:, k * 64:(k + 1) * 64]
                        sk = ropes[:, k * 64:(k + 1) * 64]
                        v.wait_ge(s_v.h, s_v.n)
                        s_v.inc(v.tensor_mul(t1[:], q1, ck))
                        s_v.inc(v.tensor_mul(t2[:], q2, sk))
                        v.wait_ge(s_v.h, s_v.n)
                        s_v.inc(v.tensor_sub(
                            orow[k % 2][:, h * 128:h * 128 + 64], t1[:], t2[:]))
                        v.wait_ge(s_v.h, s_v.n)
                        s_v.inc(v.tensor_mul(t1[:], q1, sk))
                        s_v.inc(v.tensor_mul(t2[:], q2, ck))
                        v.wait_ge(s_v.h, s_v.n)
                        s_v.inc(v.tensor_add(
                            orow[k % 2][:, h * 128 + 64:h * 128 + 128],
                            t1[:], t2[:]))

                block.vector(kev)
                evn = s_v.n

                def kst(sy, k=k, evn=evn, bi=k % 2):
                    sy.wait_ge(s_v.h, evn)
                    s_st[bi].inc(sy.dma_start(
                        out=kv_contrib[k * 128:(k + 1) * 128, :],
                        in_=orow[bi][:]), 16)

                block.sync(kst)
                orow_st[k % 2] = s_st[k % 2].n

    QKV_ST = [s_st[0].n, s_st[1].n]

    dump = _os2.environ.get("DEV_B_DUMP")
    if dump in ("q", "kv"):
        srcd = q_dram if dump == "q" else kv_contrib
        s_dp = C.sem("b3_dp")

        def dmp(g):
            g.wait_ge(s_st[0].h, QKV_ST[0])
            g.wait_ge(s_st[1].h, QKV_ST[1])
            for r in range(16):
                s_dp.inc(g.dma_start(
                    out=out_x[r * 128:(r + 1) * 128, :],
                    in_=srcd[r * 128:(r + 1) * 128, :]), 16)

        block.gpsimd(dmp)
        DN = s_dp.n

        class _O:
            def barrier(self, eng):
                eng.wait_ge(s_dp.h, DN)
        stP.close()
        return _O()

    gbuf = C.sbuf("b3gb", [128, 2048], BF16, stP)
    widx = C.sbuf("b3wi", [128, (TOK + 640) // 128], I32, stP)
    s_wi = C.sem("b3_wi")

    def ldwi(sy):
        s_wi.inc(sy.dma_start(out=widx[:], in_=io["kvwin_idx"]), 16)

    block.sync(ldwi)

    if _os2.environ.get("DEV_NO_CC"):
        def coll(g):
            g.wait_ge(s_st[0].h, QKV_ST[0])
            g.wait_ge(s_st[1].h, QKV_ST[1])
            for rr in range(4):
                s_cc.inc(g.dma_start(out=kv_all[rr * TOK:(rr + 1) * TOK, :],
                                     in_=kv_contrib), 16)

        block.gpsimd(coll)
    else:
        def coll(g):
            g.wait_ge(s_st[0].h, QKV_ST[0])
            g.wait_ge(s_st[1].h, QKV_ST[1])
            s_cc.inc(g.collective_compute(
                "AllGather", ALU.bypass,
                replica_groups=[[0, 1, 2, 3], [4, 5, 6, 7]],
                ins=[kv_contrib.opt()], outs=[kv_all.opt()]))

        block.gpsimd(coll)

    WINC = (TOK + 640) // 128
    for r in range(WINC):
        def wgat(g, r=r, wsn=s_ws.n):
            if r == 0:
                g.wait_ge(s_cc.h, s_cc.n)
                g.wait_ge(s_wi.h, 16)
            else:
                g.wait_ge(s_ws.h, wsn)
            s_wg.inc(g.indirect_dma_start(
                gbuf[:], None, kv_all,
                bass.IndirectOffsetOnAxis(widx[:, r:r + 1], 0)), 16)

        block.gpsimd(wgat)
        wgn = s_wg.n

        def wsto(sy, r=r, wgn=wgn):
            sy.wait_ge(s_wg.h, wgn)
            s_ws.inc(sy.dma_start(
                out=kv_win[r * 128:(r + 1) * 128, :], in_=gbuf[:]), 16)

        block.sync(wsto)

    WS_ALL = s_ws.n

    if _os2.environ.get("DEV_B_DUMP") == "kvw":
        s_dp = C.sem("b3_dp")

        def dmpw(g):
            g.wait_ge(s_ws.h, WS_ALL)
            for r in range(16):
                s_dp.inc(g.dma_start(
                    out=out_x[r * 128:(r + 1) * 128, :],
                    in_=kv_win[r * 128:(r + 1) * 128, :]), 16)

        block.gpsimd(dmpw)
        DNW = s_dp.n

        class _OW:
            def barrier(self, eng):
                eng.wait_ge(s_dp.h, DNW)
        stP.close()
        return _OW()

    if _os2.environ.get("DEV_B_DUMP") == "kva":
        s_dp = C.sem("b3_dp")

        def dmpk(g):
            g.wait_ge(s_ws.h, WS_ALL)
            for r in range(16):
                s_dp.inc(g.dma_start(
                    out=out_x[r * 128:(r + 1) * 128, :],
                    in_=kv_all[TOK + r * 128: TOK + (r + 1) * 128, :]), 16)

        block.gpsimd(dmpk)
        DNK = s_dp.n

        class _OK:
            def barrier(self, eng):
                eng.wait_ge(s_dp.h, DNK)
        stP.close()
        return _OK()
    stP.close()

    att_dram = io["seg_x"]
    stA = contextlib.ExitStack()
    att_st = [C.sbuf(f"b3as{i}", [128, 256], BF16, stA) for i in range(2)]
    kTh = [C.sbuf(f"b3kT{i}", [128, SEQ], BF16, stA) for i in range(2)]
    vth = [C.sbuf(f"b3vt{i}", [128, SEQ], BF16, stA) for i in range(2)]
    kTw = [C.sbuf(f"b3kTw{i}", [128, TOK + 640], BF16, stA) for i in range(2)]
    vtw = [C.sbuf(f"b3vtw{i}", [128, TOK + 640], BF16, stA) for i in range(2)]
    qTt = [C.sbuf(f"b3qT{i}", [128, 256], BF16, stA) for i in range(2)]
    tabs = C.sbuf("b3tab", [3, 68 * 256], F32, stA)
    ramph = C.sbuf("b3ramp", [3, 16 * 128], F32, stA)
    killj = C.sbuf("b3kj", [128, 1], F32, stA)
    mrel = C.sbuf("b3mrel", [128, 4 * 256], F32, stA)
    ub64 = C.sbuf("b3ub", [128, 64], F32, stA)
    mb64 = C.sbuf("b3mb", [128, 64], F32, stA)
    urow = C.sbuf("b3ur", [1, 64], F32, stA)
    usum = C.sbuf("b3us", [1, 64], F32, stA)
    sumv8 = C.sbuf("b3sv", [128, 2], F32, stA)
    P = C.sbuf("b3P", [128, 4 * 512], BF16, stA)
    sum_sb = C.sbuf("b3sum", [1, 256], F32, stA)
    sum2 = C.sbuf("b3sum2", [1, 256], F32, stA)
    rec_sb = C.sbuf("b3rec", [1, 256], F32, stA)
    recB = C.sbuf("b3recB", [128, 256], F32, stA)
    o1t = C.sbuf("b3o1", [128, 64], F32, stA)

    psc = [C.psum(f"b3psc{i}", [128, 512], stA) for i in range(4)]
    pat = C.psum("b3pat", [128, 512], stA)
    psm = C.psum("b3psm", [128, 512], stA)
    prb = C.psum("b3prb", [128, 512], stA)
    psv = C.psum("b3psv", [128, 512], stA)

    s_kl = [C.sem(f"b3_kl{i}") for i in range(2)]
    s_wk = [C.sem(f"b3_wk{i}") for i in range(2)]
    s_ql = [C.sem(f"b3_ql{i}") for i in range(2)]
    s_tb = [C.sem(f"b3_tb{i}") for i in range(2)]
    s_cl = C.sem("b3_cl")
    s_sc = C.sem("b3_sc")
    s_ex = C.sem("b3_ex")
    s_mk = C.sem("b3_mk")
    s_pv = C.sem("b3_pv")
    s_sv = C.sem("b3_sv")
    s_sve = C.sem("b3_sve")
    s_sm = C.sem("b3_sm")
    s_rc = C.sem("b3_rc")
    s_bc = C.sem("b3_bc")
    s_rb = C.sem("b3_rb")
    s_ev = C.sem("b3_ev")

    def ldcb(sy):
        sy.wait_ge(s_ws.h, WS_ALL)
        for dst, srcn in [(ramph[:], "ramp3h"), (killj[:], "kill_j64"),
                          (ub64[:], "uB64"),
                          (mb64[:], "mB64"), (urow[:], "uni_flag"),
                          (usum[:], "uni_sum")]:
            s_cl.inc(sy.dma_start(out=dst, in_=io[srcn]), 16)

    block.sync(ldcb)
    CLN = s_cl.n

    kl_free = [0, 0]
    for kvh in range(8):
        bi = kvh % 2

        def klod(sy, kvh=kvh, bi=bi, gate=kl_free[bi], svg=s_sv.n):
            if kvh < 2:
                sy.wait_ge(s_cl.h, CLN)
            if gate:
                sy.wait_ge(s_pv.h, gate)
            s_kl[bi].inc(sy.dma_start(
                out=kTh[bi][:].rearrange("p (o j) -> p o j", o=1),
                in_=kv_all[:, kvh * 128:(kvh + 1) * 128]
                    .rearrange("j (o p) -> j o p", o=1),
                transpose=True), 16)
            for q4 in range(4):
                s_kl[bi].inc(sy.dma_start(
                    out=vth[bi][:].rearrange("p (c d) -> p c d", c=64)
                        [:, q4 * 16:(q4 + 1) * 16, :],
                    in_=kv_all[q4 * 2048:(q4 + 1) * 2048,
                               1024 + kvh * 128:1024 + (kvh + 1) * 128]
                        .rearrange("(c p) d -> p c d", p=128)), 16)
            s_wk[bi].inc(sy.dma_start(
                out=kTw[bi][:].rearrange("p (o j) -> p o j", o=1),
                in_=kv_win[:, kvh * 128:(kvh + 1) * 128]
                    .rearrange("j (o p) -> j o p", o=1),
                transpose=True), 16)
            s_wk[bi].inc(sy.dma_start(
                out=vtw[bi][:].rearrange("p (c d) -> p c d", c=WINC),
                in_=kv_win[:, 1024 + kvh * 128:1024 + (kvh + 1) * 128]
                    .rearrange("(c p) d -> p c d", p=128)), 16)

        block.sync(klod)
        kln = s_kl[bi].n
        wkn = s_wk[bi].n

        def svmm(t, kvh=kvh, bi=bi, kln=kln, sveg=s_sve.n):
            t.wait_ge(s_kl[bi].h, kln)
            if sveg:
                t.wait_ge(s_sve.h, sveg)
            v3 = vth[bi][:].rearrange("p (c d) -> p c d", c=64)
            for c in range(64):
                ins = t.matmul(psv[:, 0:1], v3[:, c, :], onec[:],
                               start=(c == 0), stop=(c == 63))
            s_sv.inc(ins)

        block.tensor(svmm)
        svn = s_sv.n

        def svev(v, svn=svn, vprior=s_v.n):
            v.wait_ge(s_v.h, vprior)
            v.wait_ge(s_sv.h, svn)
            s_sve.inc(v.tensor_scalar(sumv8[:, 0:1], psv[:, 0:1],
                                      1.0 / SEQ, None, ALU.mult))

        block.vector(svev)
        sven = s_sve.n

        for hh in range(2):
            h = 2 * kvh + hh
            if kvh == 0 and hh == 0:
                ql_free = [0, 0]
            for jb in range(NQB):
                hj = (h, jb)
                qbi = jb % 2

                def qlod(sy, h=h, jb=jb, qbi=qbi, gate=ql_free[qbi],
                         tbg=s_sc.n, mkg=s_mk.n):
                    if gate:
                        sy.wait_ge(s_sc.h, gate)
                    if tbg:
                        sy.wait_ge(s_sc.h, tbg)
                    if mkg:
                        sy.wait_ge(s_mk.h, mkg)
                    s_ql[qbi].inc(sy.dma_start(
                        out=qTt[qbi][:].rearrange("p (o j) -> p o j", o=1),
                        in_=q_dram[jb * 256:(jb + 1) * 256,
                                   h * 128:(h + 1) * 128]
                            .rearrange("j (o p) -> j o p", o=1),
                        transpose=True), 16)
                    s_tb[0].inc(sy.dma_start(
                        out=tabs[:].rearrange("r (c t) -> r c t", c=68),
                        in_=io["atab"][jb].rearrange("c r t -> r c t")), 16)
                    s_tb[1].inc(sy.dma_start(
                        out=mrel[:],
                        in_=io["maskrel"][:, jb * 1024:(jb + 1) * 1024]), 16)

                block.sync(qlod)
                qln = s_ql[qbi].n
                tbn = s_tb[0].n

                exp_pair = {}
                pv_at = {}
                for c in range(68):
                    p_ = c // 2
                    bk = p_ % 4
                    col = (c % 2) * 256

                    def cqkt(t, h=h, jb=jb, bi=bi, qbi=qbi, c=c, bk=bk,
                             col=col, qln=qln, tbn=tbn, kln=kln, wkn=wkn,
                             exg=exp_pair.get(c // 2 - 4, 0)):
                        if c == 0:
                            t.wait_ge(s_ql[qbi].h, qln)
                            t.wait_ge(s_tb[0].h, tbn)
                            t.wait_ge(s_kl[bi].h, kln)
                            t.wait_ge(s_wk[bi].h, wkn)
                        if exg:
                            t.wait_ge(s_ex.h, exg)
                        if c < 64:
                            lhs = kTh[bi][:, c * 128:(c + 1) * 128]
                        else:
                            r = c - 64
                            off = jb * 256 + (128, 256, 640, 768)[r]
                            lhs = kTw[bi][:, off:off + 128]
                        t.matmul(psc[bk][:, col:col + 256], lhs,
                                 qTt[qbi][:, 0:256], start=True, stop=False)
                        ins = t.matmul(psc[bk][:, col:col + 256],
                                       ramph[:, h * 128:(h + 1) * 128],
                                       tabs[:, c * 256:(c + 1) * 256],
                                       start=False, stop=True)
                        s_sc.inc(ins)

                    block.tensor(cqkt)
                    scn = s_sc.n

                    if c >= 64 and c % 2 == 1:
                        def cmka(v, jb=jb, c=c, bk=bk, scn=scn,
                                 vprior=s_v.n, mreln=s_tb[1].n):
                            v.wait_ge(s_v.h, vprior)
                            v.wait_ge(s_sc.h, scn)
                            r0 = c - 1 - 64
                            v.wait_ge(s_tb[1].h, mreln)
                            s_mk.inc(v.tensor_add(
                                psc[bk][:, 0:512],
                                psc[bk][:, 0:512],
                                mrel[:, r0 * 256:(r0 + 2) * 256]))

                        block.vector(cmka)
                    mka = s_mk.n

                    if c % 2 == 1:
                        def cexp(s, c=c, p_=p_, bk=bk, scn=scn, mka=mka,
                                 pvg=pv_at.get(2 * (p_ - 4) + 1, 0),
                                 pvp=s_pv.n):
                            s.wait_ge(s_sc.h, scn)
                            if c >= 64:
                                s.wait_ge(s_mk.h, mka)
                            if pvg:
                                s.wait_ge(s_pv.h, pvg)
                            elif p_ < 4 and pvp:
                                s.wait_ge(s_pv.h, pvp)
                            s_ex.inc(s.activation(
                                P[:, (p_ % 4) * 512:(p_ % 4 + 1) * 512],
                                psc[bk][:, 0:512], AF.Exp, scale=1.0 / SQD))

                        block.scalar(cexp)
                        exp_pair[p_] = s_ex.n

                        if c == 1:
                            def cmsk(v, exn=s_ex.n, p_=p_, vprior=s_v.n):
                                v.wait_ge(s_v.h, vprior)
                                v.wait_ge(s_ex.h, exn)
                                sl = (p_ % 4) * 512
                                s_mk.inc(v.tensor_scalar(
                                    P[:, sl:sl + 256], P[:, sl:sl + 256],
                                    killj[:], None, ALU.mult))

                            block.vector(cmsk)
                        mkn2 = s_mk.n

                        for cc in (c - 1, c):
                            def cpv(t, jb=jb, bi=bi, c=cc, p_=p_, mkn2=mkn2,
                                    exn=exp_pair[p_], evg=s_ev.n):
                                if c == 0 and evg:
                                    t.wait_ge(s_ev.h, evg)
                                if c <= 1:
                                    t.wait_ge(s_mk.h, mkn2)
                                t.wait_ge(s_ex.h, exn)
                                sl = (p_ % 4) * 512 + (c % 2) * 256
                                if c < 64:
                                    lhs = vth[bi][:].rearrange(
                                        "p (cc2 d) -> p cc2 d", cc2=64)[:, c, :]
                                else:
                                    r = c - 64
                                    off = jb * 256 + (128, 256, 640, 768)[r]
                                    lhs = vtw[bi][:, off:off + 128]
                                t.matmul(pat[:, 0:256], lhs, P[:, sl:sl + 256],
                                         start=(c == 0), stop=(c == 67))
                                ins = t.matmul(psm[0:1, 0:256], onec[:],
                                               P[:, sl:sl + 256],
                                               start=(c == 0), stop=(c == 67))
                                s_pv.inc(ins)

                            block.tensor(cpv)
                            pv_at[cc] = s_pv.n

                pvn = s_pv.n

                def bsum(s, pvn=pvn):
                    s.wait_ge(s_pv.h, pvn)
                    s_sm.inc(s.copy(sum_sb[:], psm[0:1, 0:256]))

                block.scalar(bsum)
                smn = s_sm.n

                def brcp(v, jb=jb, smn=smn, sven=sven, vprior=s_v.n):
                    v.wait_ge(s_v.h, vprior)
                    v.wait_ge(s_sm.h, smn)
                    if jb == 0:
                        s_v.inc(v.tensor_mul(sum2[:, 0:64], sum_sb[:, 0:64],
                                             mb64[0:1, 0:64]))
                        v.wait_ge(s_v.h, s_v.n)
                        s_v.inc(v.tensor_add(sum_sb[:, 0:64], sum2[:, 0:64],
                                             usum[:]))
                        v.wait_ge(s_v.h, s_v.n)
                    s_rc.inc(v.reciprocal(rec_sb[:], sum_sb[:]))

                block.vector(brcp)
                rcn = s_rc.n

                def bbc(t, rcn=rcn, rbg=s_rb.n):
                    t.wait_ge(s_rc.h, rcn)
                    if rbg:
                        t.wait_ge(s_rb.h, rbg)
                    s_bc.inc(t.matmul(prb[:, 0:256], oner[:], rec_sb[:],
                                      start=True, stop=True))

                block.tensor(bbc)
                bcn = s_bc.n

                abi = (h * NQB + jb) % 2

                def bev(v, h=h, jb=jb, abi=abi, bcn=bcn, sven=sven,
                        vprior=s_v.n, stg=s_ev.n):
                    v.wait_ge(s_v.h, vprior)
                    v.wait_ge(s_bc.h, bcn)
                    s_rb.inc(v.tensor_copy(recB[:], prb[:, 0:256]))
                    v.wait_ge(s_v.h, s_v.n)
                    v.wait_ge(s_rb.h, s_rb.n)
                    if stg >= 32:
                        v.wait_ge(s_ev.h, stg - 16)
                    if jb == 0:
                        v.wait_ge(s_sve.h, sven)
                        s_v.inc(v.tensor_mul(o1t[:], pat[:, 0:64], recB[:, 0:64]))
                        v.wait_ge(s_v.h, s_v.n)
                        s_v.inc(v.tensor_mul(o1t[:], o1t[:], mb64[:, 0:64]))
                        v.wait_ge(s_v.h, s_v.n)
                        s_v.inc(v.scalar_tensor_tensor(
                            att_st[abi][:, 0:64],
                            ub64[:, 0:64], sumv8[:, 0:1], o1t[:],
                            ALU.mult, ALU.add))
                        v.wait_ge(s_v.h, s_v.n)
                        s_v.inc(v.tensor_mul(att_st[abi][:, 64:256],
                                             pat[:, 64:256], recB[:, 64:256]))
                    else:
                        s_v.inc(v.tensor_mul(att_st[abi][:, 0:256],
                                             pat[:, 0:256], recB[:, 0:256]))

                block.vector(bev)
                evn = s_v.n

                def asto(sy, h=h, jb=jb, abi=abi, evn=evn):
                    sy.wait_ge(s_v.h, evn)
                    if h == 0 and jb == 0:
                        sy.wait_ge(io["s_zi"].h, io["zi_n"])
                    s_ev.inc(sy.dma_start(
                        out=att_dram[h * 128:(h + 1) * 128,
                                     jb * 256:(jb + 1) * 256],
                        in_=att_st[abi][:]), 16)

                block.sync(asto)
                ql_free[qbi] = s_sc.n
        kl_free[bi] = s_pv.n

    ATT_ALL = s_ev.n
    stA.close()

    if _os2.environ.get("DEV_B_DUMP") == "att":
        s_dp = C.sem("b3_dp")

        def dmpa(g):
            g.wait_ge(s_ev.h, ATT_ALL)
            for r in range(16):
                s_dp.inc(g.dma_start(
                    out=out_x[r * 128:(r + 1) * 128, :],
                    in_=att_dram[r * 128:(r + 1) * 128, :]), 16)

        block.gpsimd(dmpa)
        DN2 = s_dp.n

        class _O2:
            def barrier(self, eng):
                eng.wait_ge(s_dp.h, DN2)
        return _O2()

    st2 = contextlib.ExitStack()
    attT = C.sbuf("b3attT", [128, 16 * 2048], BF16, st2)
    s_al = C.sem("b3_al")

    def lda(sy):
        sy.wait_ge(s_ev.h, ATT_ALL)
        for mc in range(16):
            s_al.inc(sy.dma_start(
                out=attT[:, mc * 2048:(mc + 1) * 2048],
                in_=att_dram[mc * 128:(mc + 1) * 128, :]), 16)

    block.sync(lda)
    ALN = s_al.n
    wo_sb = C.sbuf("b3wo", [128, 16 * 2048], BF16, st2)
    ysb = [C.sbuf(f"b3y{i}", [128, 2048], F32, st2) for i in range(2)]
    pyb = [C.psum(f"b3py{i}", [128, 512], st2) for i in range(2)]
    s_wo = C.sem("b3_wo")
    s_ymm = C.sem("b3_ymm")
    s_yev = C.sem("b3_yev")
    s_res = [C.sem(f"b3_res{i}") for i in range(2)]

    def ldwo(g):
        g.wait_ge(s_ev.h, ATT_ALL)
        for mc in range(4):
            s_wo.inc(g.dma_start(
                out=wo_sb[:, mc * 8192:(mc + 1) * 8192],
                in_=wo_b[:, mc * 8192:(mc + 1) * 8192]), 16)

    block.gpsimd(ldwo)

    res_at = {}
    for tc in range(16):
        for fb in range(4):
            i = 4 * tc + fb

            def mmy(t, tc=tc, fb=fb, i=i, yegate=s_yev.n):
                if i == 0:
                    t.wait_ge(s_wo.h, 64)
                    t.wait_ge(s_al.h, ALN)
                if yegate >= 2:
                    t.wait_ge(s_yev.h, yegate - 1)
                wo3 = wo_sb[:].rearrange("p (m f) -> p m f", m=16)
                pb = pyb[i % 2]
                t.matmul(pb[:], onerb[:], bo[:, fb * 512:(fb + 1) * 512],
                         start=True, stop=False)
                for mc in range(16):
                    ins = t.matmul(
                        pb[:],
                        attT[:, mc * 2048 + tc * 128: mc * 2048 + (tc + 1) * 128],
                        wo3[:, mc, fb * 512:(fb + 1) * 512],
                        start=False, stop=(mc == 15))
                s_ymm.inc(ins)

            block.tensor(mmy)
            ymn = s_ymm.n

            def evy(v, tc=tc, fb=fb, i=i, ymn=ymn, rgate=res_at.get(tc - 2, 0),
                    vprior=s_v.n):
                v.wait_ge(s_v.h, vprior)
                v.wait_ge(s_ymm.h, ymn)
                if fb == 0 and rgate:
                    v.wait_ge(s_res[tc % 2].h, rgate)
                s_yev.inc(v.tensor_copy(
                    ysb[tc % 2][:, fb * 512:(fb + 1) * 512], pyb[i % 2][:]))

            block.vector(evy)

        yn = s_yev.n

        def acc(g, tc=tc, yn=yn):
            g.wait_ge(s_yev.h, yn)
            s_res[tc % 2].inc(g.dma_start(
                out=out_x[tc * 128:(tc + 1) * 128, :],
                in_=ysb[tc % 2][:], accum_op=ALU.add), 16)

        block.gpsimd(acc)
        res_at[tc] = s_res[tc % 2].n

    st2.close()
    R = [s_res[0].n, s_res[1].n]

    class ResOut:
        def barrier(self, eng):
            eng.wait_ge(s_res[0].h, R[0])
            eng.wait_ge(s_res[1].h, R[1])

    return ResOut()



def _host_consts(core):
    b, q = divmod(core, 4)
    kk = np.arange(128)[:, None]
    tt = np.arange(QB)[None, :]
    c = {}

    va = np.ones((LTOK,), np.float32)
    if q == 0:
        va[:HALO] = 0.0
    c["valid_a"] = np.ascontiguousarray(
        va.reshape(LCH, 128).T).astype(BF)

    m = np.zeros((4, 128, QB), np.float32)
    m[0] = (kk > tt)
    m[1] = (tt <= kk + 127)
    m[2] = (kk <= tt)
    m[3] = (tt >= kk + 128)
    c["mask_a"] = m.astype(BF)

    c["ones_col"] = np.ones((128, 1), np.float32).astype(BF)
    c["ones_row"] = np.ones((1, 128), np.float32)
    c["ones_row_bf"] = np.ones((1, 128), np.float32).astype(BF)
    base = q * TOK
    t64 = np.arange(QB, dtype=np.float64)
    kk64 = np.arange(128, dtype=np.float64)
    KILL = 1.0e9
    tab = np.zeros((NQB, 68, 3, QB), np.float32)
    reldelta = (256, 128, -256, -384)
    for jb in range(NQB):
        Q0 = base + QB * jb
        for ch in range(64):
            delta = Q0 - 128 * ch
            if -512 < delta < 384:
                tab[jb, ch, 0, :] = -KILL
                continue
            sg = 1.0 if delta >= 384 else -1.0
            tab[jb, ch, 0, :] = -sg * (Q0 + t64) + 257.0
            tab[jb, ch, 1, :] = sg
            tab[jb, ch, 2, :] = sg * 128.0 * ch
        for r, dl in enumerate(reldelta):
            sg = 1.0 if dl > 0 else -1.0
            tab[jb, 64 + r, 0, :] = -sg * (Q0 + t64) + 257.0
            tab[jb, 64 + r, 1, :] = sg
            tab[jb, 64 + r, 2, :] = sg * float(Q0 - dl)
    c["atab"] = tab

    mrel = np.zeros((128, NQB * 4 * QB), np.float32)
    for jb in range(NQB):
        Q0 = base + QB * jb
        for r, dl in enumerate(reldelta):
            j0 = Q0 - dl
            jj = j0 + np.arange(128)[:, None]
            dmat = (Q0 + np.arange(QB)[None, :]) - jj
            ok = (np.abs(dmat) > 256) & (jj >= 64) & (jj < SEQ)
            mrel[:, (jb * 4 + r) * QB:(jb * 4 + r + 1) * QB] = \
                np.where(ok, 0.0, -1.0e9)
    c["maskrel"] = mrel

    rows = base - 384 + np.arange(TOK + 640)
    c["kvwin_idx"] = np.ascontiguousarray(
        np.clip(rows, 0, SEQ - 1).astype(np.int32)
        .reshape((TOK + 640) // 128, 128).T)

    half = 64
    freqs = 1.0 / 10000.0 ** (np.arange(half, dtype=np.float64) / half)
    pos = base + np.arange(TOK, dtype=np.float64)
    ang = np.outer(pos, freqs)
    c["rope_sin"] = np.ascontiguousarray(
        np.sin(ang).astype(np.float32).reshape(NCH, 128, 64)
        .transpose(1, 0, 2).reshape(128, NCH * 64))
    c["rope_cos"] = np.ascontiguousarray(
        np.cos(ang).astype(np.float32).reshape(NCH, 128, 64)
        .transpose(1, 0, 2).reshape(128, NCH * 64))

    r3 = np.zeros((3, 16 * 128), np.float32)
    for h in range(16):
        s = float(SLOPES[h]) * SQD
        r3[0, h * 128:(h + 1) * 128] = s
        r3[1, h * 128:(h + 1) * 128] = s * np.arange(128)
        r3[2, h * 128:(h + 1) * 128] = s
    c["ramp3h"] = r3

    kj = np.ones((128, 1), np.float32)
    kj[:64] = 0.0
    c["kill_j64"] = kj

    u = np.zeros(64, np.float32)
    if q == 0:
        u[:] = 1.0
    c["uB64"] = np.broadcast_to(u, (128, 64)).astype(np.float32).copy()
    c["mB64"] = np.broadcast_to(1.0 - u, (128, 64)).astype(np.float32).copy()
    c["uni_flag"] = u.reshape(1, 64).copy()
    c["uni_sum"] = (u * float(SEQ)).reshape(1, 64).copy()

    ltri = (np.arange(128)[:, None] < np.arange(128)[None, :])
    c["ltri"] = ltri.astype(np.float32).astype(BF)
    c["tokf"] = (np.arange(NCH)[None, :] * 128
                 + np.arange(128)[:, None]).astype(np.float32)
    c["base_r"] = (np.arange(NE, dtype=np.float32) * CAP).reshape(1, NE)
    c["lim_r"] = (np.arange(NE, dtype=np.float32) * CAP + CAP).reshape(1, NE)
    c["mfill"] = np.full((128, 256), 1.0e5, np.float32)
    return c



def build_program():
    nc = bass.Bass("TRN2", target_bir_lowering=False)

    def din(name, shape, dt=F32):
        return nc.dram_tensor(name, shape, dt, kind="ExternalInput").ap()

    io = {}
    io["x_own"] = din("x_own", [TOK, D])
    io["x_halo"] = din("x_halo", [HALO, D])
    io["wq_a"] = din("wq_a", [16, 128, D])
    io["wk_a"] = din("wk_a", [128, 16 * 512])
    io["wv_a"] = din("wv_a", [128, 16 * 512])
    io["wo_a"] = din("wo_a", [128, 16 * 2048])
    bo_a = din("bo_a", [1, D])
    valid_a = din("valid_a", [128, LCH], BF16)
    mask_a = din("mask_a", [4, 128, QB], BF16)
    ones_col = din("ones_col", [128, 1], BF16)
    ones_row = din("ones_row", [1, 128])
    ones_row_bf = din("ones_row_bf", [1, 128], BF16)

    io["wq_b"] = din("wq_b", [128, 16 * D])
    io["wkv_b"] = din("wkv_b", [128, 16 * D])
    io["wo_b"] = din("wo_b", [128, 16 * D])
    bo_b = din("bo_b", [1, D])
    io["rope_sin"] = din("rope_sin", [128, 16 * 64])
    io["rope_cos"] = din("rope_cos", [128, 16 * 64])
    io["atab"] = din("atab", [NQB, 68, 3, QB])
    io["maskrel"] = din("maskrel", [128, NQB * 4 * QB])
    io["kvwin_idx"] = din("kvwin_idx", [128, (TOK + 640) // 128], I32)
    io["ramp3h"] = din("ramp3h", [3, 16 * 128])
    io["kill_j64"] = din("kill_j64", [128, 1])
    io["uB64"] = din("uB64", [128, 64])
    io["mB64"] = din("mB64", [128, 64])
    io["uni_flag"] = din("uni_flag", [1, 64])
    io["uni_sum"] = din("uni_sum", [1, 64])
    io["gate_wT"] = din("gate_wT", [128, 16 * NE])
    io["w1T"] = din("w1T", [NE, 16, 128, 2048])
    io["w3T"] = din("w3T", [NE, 16, 128, 2048])
    io["w2T"] = din("w2T", [NE, 128, 16 * 2048])
    ltri = din("ltri", [128, 128], BF16)
    tokf = din("tokf", [128, NCH])
    base_r = din("base_r", [1, NE])
    io["lim_r"] = din("lim_r", [1, NE])
    mfill = din("mfill", [128, 256])

    io["out_x"] = nc.dram_tensor("out_x", [TOK, D], F32,
                                 kind="ExternalOutput").ap()
    io["xn_dram"] = nc.dram_tensor("xn_dram", [LTOK, D], BF16).ap()
    io["q_dram"] = nc.dram_tensor("q_dram", [D, D], BF16).ap()
    io["kv_contrib"] = nc.dram_tensor("kv_contrib", [TOK, D], BF16).ap()
    io["kv_all"] = nc.dram_tensor("kv_all", [4 * TOK, D], BF16).ap()
    io["kv_win"] = nc.dram_tensor("kv_win", [TOK + 640, D], BF16).ap()
    io["seg_x"] = nc.dram_tensor("seg_x", [NE * CAP, D], BF16).ap()
    io["seg_meta"] = nc.dram_tensor("seg_meta", [NE * CAP, 2], F32).ap()
    io["y01"] = nc.dram_tensor("y01", [2 * TOK, D], F32).ap()

    stack = contextlib.ExitStack()
    with nc.Block() as block:
        C = Ctx(nc, block, stack)

        onec = C.sbuf("onec", [128, 1], BF16)
        oner = C.sbuf("oner", [1, 128], F32)
        onerb = C.sbuf("onerb", [1, 128], BF16)
        maska_sb = C.sbuf("maska_sb", [128, 4 * QB], BF16)
        valida_sb = C.sbuf("valida_sb", [128, LCH], BF16)
        bo_a_sb = C.sbuf("bo_a_sb", [1, D], BF16)
        ltri_sb = C.sbuf("ltri_sb", [128, 128], BF16)
        tokf_sb = C.sbuf("tokf_sb", [128, NCH], F32)
        base_sb = C.sbuf("base_sb", [1, NE], F32)

        s_c = C.sem("consts")

        def ldc(sy):
            for dst, src in [
                (onec[:], ones_col), (oner[:], ones_row), (onerb[:], ones_row_bf),
                (maska_sb[:].rearrange("p (m t) -> p m t", m=4),
                 mask_a.rearrange("m p t -> p m t")),
                (valida_sb[:], valid_a),
                (ltri_sb[:], ltri), (tokf_sb[:], tokf), (base_sb[:], base_r),
            ]:
                s_c.inc(sy.dma_start(out=dst, in_=src), 16)

        block.sync(ldc)

        s_cg = C.sem("constsg")

        bo_b_sb = C.sbuf("bo_b_sb", [1, D], BF16)
        io["bo_b_sb"] = bo_b_sb

        def ldc2(g):
            s_cg.inc(g.dma_start(out=bo_a_sb[:], in_=bo_a), 16)
            s_cg.inc(g.dma_start(out=bo_b_sb[:], in_=bo_b), 16)

        block.gpsimd(ldc2)

        def regs(g):
            r1 = C.stack.enter_context(g.register("bnd_seg"))
            g.reg_mov(r1, NE * CAP - 1)
            io["bnd_seg"] = g.snap(r1)
            r2 = C.stack.enter_context(g.register("bnd_y"))
            g.reg_mov(r2, 2 * TOK - 1)
            io["bnd_y"] = g.snap(r2)

        block.gpsimd(regs)

        def ldc3(sy):
            s_c.inc(sy.dma_start(out=io["out_x"], in_=io["x_own"]), 16)

        block.sync(ldc3)

        zsb = C.sbuf("zsb", [128, D], BF16)
        s_zi = C.sem("zinit")
        io["s_zi"] = s_zi

        def zi(v):
            s_zi.inc(v.memset(zsb[:], 0.0))

        block.vector(zi)

        def zid(sy):
            sy.wait_ge(s_zi.h, 1)
            for r in range(NE * CAP // 128):
                s_zi.inc(sy.dma_start(
                    out=io["seg_x"][r * 128:(r + 1) * 128, :], in_=zsb[:]), 16)

        block.sync(zid)
        io["zi_n"] = s_zi.n
        CWAIT = s_c.n

        io.update(dict(onec=onec, oner=oner, onerb=onerb, maska=maska_sb,
                       valida=valida_sb, bo_a=bo_a_sb, ltri=ltri_sb,
                       tokf=tokf_sb, base_r=base_sb, mfill=mfill))

        lnstack = contextlib.ExitStack()
        io["ln1"] = emit_ln(
            C, "ln1", LCH,
            lambda k: (io["x_halo"][k * 128:(k + 1) * 128, :] if k < 8
                       else io["x_own"][(k - 8) * 128:(k - 7) * 128, :]),
            io["xn_dram"], 0, [(s_c, CWAIT), (s_cg, s_cg.n)], stack=lnstack)
        lnstack.close()
        if DEV_STOP == "ln":
            fin_o = io["ln1"]
        elif _os3.environ.get("DEV_SKIP_A"):
            fin_o = emit_moe(C, io, "m2", io["ln1"])
        elif _os3.environ.get("DEV_ONLY_B"):
            fin_o = emit_attn_b(C, io, io["ln1"])
        else:
            fin_o = emit_attn_a(C, io)
            if DEV_STOP == "full" and N_PHASES >= 2:
                fin_o = emit_moe(C, io, "m2", fin_o)
            if DEV_STOP == "full" and N_PHASES >= 3:
                fin_o = emit_attn_b(C, io, fin_o)
            if DEV_STOP == "full" and N_PHASES >= 4:
                fin_o = emit_moe(C, io, "m4", fin_o)

        def fin(g):
            fin_o.barrier(g)

        block.gpsimd(fin)

    stack.close()
    return nc


def build_in_maps(L0_attn_a_x, L0_attn_a_Wq, L0_attn_a_Wk, L0_attn_a_Wv,
                  L0_attn_a_Wo, L0_attn_a_bo, L0_attn_b_Wq, L0_attn_b_Wkv,
                  L0_attn_b_Wo, L0_attn_b_bo, L0_ffn_gate_w, L0_ffn_w1,
                  L0_ffn_w3, L0_ffn_w2):
    x = np.asarray(L0_attn_a_x, dtype=np.float32)
    wq = np.asarray(L0_attn_a_Wq, np.float32)
    wk = np.asarray(L0_attn_a_Wk, np.float32)
    wv = np.asarray(L0_attn_a_Wv, np.float32)
    wo = np.asarray(L0_attn_a_Wo, np.float32)
    gw = np.asarray(L0_ffn_gate_w, np.float32)
    w1 = np.asarray(L0_ffn_w1, np.float32)
    w3 = np.asarray(L0_ffn_w3, np.float32)
    w2 = np.asarray(L0_ffn_w2, np.float32)
    shared = {
        "wq_a": np.ascontiguousarray(
            wq.reshape(16, 128, 16, 128).transpose(2, 1, 0, 3)
              .reshape(16, 128, D)),
        "wk_a": np.ascontiguousarray(
            wk.reshape(16, 128, 512).transpose(1, 0, 2).reshape(128, 16 * 512)),
        "wv_a": np.ascontiguousarray(
            wv.reshape(16, 128, 512).transpose(1, 0, 2).reshape(128, 16 * 512)),
        "wo_a": np.ascontiguousarray(
            wo.reshape(16, 128, D).transpose(1, 0, 2).reshape(128, 16 * D)),
        "bo_a": np.ascontiguousarray(L0_attn_a_bo, np.float32).reshape(1, D),
        "wq_b": np.ascontiguousarray(
            np.asarray(L0_attn_b_Wq, np.float32).reshape(16, 128, D)
              .transpose(1, 0, 2).reshape(128, 16 * D)),
        "wkv_b": np.ascontiguousarray(
            np.asarray(L0_attn_b_Wkv, np.float32).reshape(16, 128, D)
              .transpose(1, 0, 2).reshape(128, 16 * D)),
        "wo_b": np.ascontiguousarray(
            np.asarray(L0_attn_b_Wo, np.float32).reshape(16, 128, D)
              .transpose(1, 0, 2).reshape(128, 16 * D)),
        "bo_b": np.ascontiguousarray(L0_attn_b_bo, np.float32).reshape(1, D),
        "gate_wT": np.ascontiguousarray(
            gw.T.reshape(16, 128, NE).transpose(1, 0, 2).reshape(128, 16 * NE)),
        "w1T": np.ascontiguousarray(
            w1.reshape(NE, 16, 128, 16, 128).transpose(0, 1, 4, 3, 2)
              .reshape(NE, 16, 128, D)),
        "w3T": np.ascontiguousarray(
            w3.reshape(NE, 16, 128, 16, 128).transpose(0, 1, 4, 3, 2)
              .reshape(NE, 16, 128, D)),
        "w2T": np.ascontiguousarray(
            w2.transpose(0, 2, 1).reshape(NE, 16, 128, D)
              .transpose(0, 2, 1, 3).reshape(NE, 128, 16 * D)),
    }

    in_maps = []
    for core in range(NCORES):
        b, q = divmod(core, 4)
        m = dict(shared)
        m["x_own"] = np.ascontiguousarray(x[b, q * TOK:(q + 1) * TOK])
        halo = np.zeros((HALO, D), np.float32)
        if q > 0:
            halo[:] = x[b, q * TOK - HALO:q * TOK]
        m["x_halo"] = halo
        m.update(_host_consts(core))
        in_maps.append(m)
    return in_maps


def kernel(**kw):
    in_maps = build_in_maps(**kw)
    nc = build_program()
    res = run_bass_kernel_spmd(nc, in_maps, core_ids=list(range(NCORES)))
    out = np.empty((2, SEQ, D), np.float32)
    for core in range(NCORES):
        b, q = divmod(core, 4)
        out[b, q * TOK:(q + 1) * TOK] = res.results[core]["out_x"]
    return out




# version 08 (final)
# speedup vs baseline: 1.2377x; 1.0155x over previous
import math
import os as _os2
import os as _os3
import contextlib
import numpy as np
import ml_dtypes
import concourse.bass as bass
import concourse.mybir as mybir
from concourse.bass_utils import run_bass_kernel_spmd

F32 = mybir.dt.float32
BF16 = mybir.dt.bfloat16
I32 = mybir.dt.int32
AF = mybir.ActivationFunctionType
ALU = mybir.AluOpType
AX = mybir.AxisListType
BF = ml_dtypes.bfloat16

NCORES = 8
TOK = 2048
D = 2048
HALO = 1024
LTOK = TOK + HALO
NCH = 16
LCH = 24
QB = 256
NQB = 8
SEQ = 8192
CAP = 1536
NE = 8
SQD = math.sqrt(128.0)
_AB = 2.0 ** (-(2.0 ** (-(math.log2(16.0) - 3.0))))
SLOPES = np.array([_AB ** p for p in range(1, 17)], dtype=np.float64)

import os
N_PHASES = int(os.environ.get('DEV_NPH', '4'))
DEV_STOP = os.environ.get('DEV_STOP', 'full')


class S:
    __slots__ = ("h", "n")

    def __init__(self, h):
        self.h = h
        self.n = 0

    def inc(self, instr, k=1):
        instr.then_inc(self.h, k)
        self.n += k
        return self.n


class Ctx:
    def __init__(self, nc, block, stack):
        self.nc = nc
        self.block = block
        self.stack = stack
        self.sems = {}

    def sem(self, name):
        key = name.split("_", 1)[1] if "_" in name else name
        if key not in self.sems:
            h = self.stack.enter_context(self.nc.semaphore(key))
            self.sems[key] = S(h)
        return self.sems[key]

    def sbuf(self, name, shape, dt, stack=None):
        return (stack or self.stack).enter_context(
            self.nc.sbuf_tensor(name, shape, dt))

    def psum(self, name, shape, stack=None):
        return (stack or self.stack).enter_context(
            self.nc.psum_tensor(name, shape, F32))



def emit_ln(C, tag, n_chunks, x_src, xn_dram, row0, wait, comp_out=None,
            stack=None, prev=None):
    block = C.block
    NB = 4
    xb = [C.sbuf(f"{tag}_x{i}", [128, D], F32, stack) for i in range(NB)]
    xnb = [C.sbuf(f"{tag}_n{i}", [128, D], BF16, stack) for i in range(NB)]
    sq = C.sbuf(f"{tag}_sq", [128, D], F32, stack)
    st = C.sbuf(f"{tag}_st", [128, 8], F32, stack)
    cs = C.sbuf(f"{tag}_cs", [128, 1], F32, stack) if comp_out is not None else None

    s_ld = [C.sem(f"{tag}_ld{i}") for i in range(NB)]
    s_a = C.sem(f"{tag}_a")
    s_v = C.sem(f"{tag}_v")
    s_xs = [C.sem(f"{tag}_xs{i}") for i in range(NB)]

    xb_free = [0] * NB
    xnb_st = [0] * NB
    xn_at = []
    ex2_done = 0

    for k in range(n_chunks):
        bi = k % NB
        kw = list(wait) if k == 0 else []
        gate = xb_free[bi]

        def ld(sy, k=k, kw=kw, gate=gate, bi=bi):
            for (ws, wn) in kw:
                sy.wait_ge(ws.h, wn)
            if k == 0 and prev is not None:
                prev.barrier(sy)
            if gate:
                sy.wait_ge(s_v.h, gate)
            s_ld[bi].inc(sy.dma_start(out=xb[bi][:], in_=x_src(k)), 16)

        block.sync(ld)
        ldn = s_ld[bi].n

        def act1(s, ldn=ldn, bi=bi, exd=ex2_done):
            s.wait_ge(s_ld[bi].h, ldn)
            if exd:
                s.wait_ge(s_v.h, exd)
            s_a.inc(s.activation(sq[:], xb[bi][:], AF.Square,
                                 accum_out=st[:, 1:2]))

        block.scalar(act1)
        an1 = s_a.n

        def vec1(v, ldn=ldn, an1=an1, bi=bi):
            v.wait_ge(s_ld[bi].h, ldn)
            if comp_out is not None:
                s_v.inc(v.tensor_reduce(cs[:], xb[bi][:], AX.X, ALU.add,
                                        apply_absolute_value=True))
            s_v.inc(v.tensor_reduce(st[:, 0:1], xb[bi][:], AX.X, ALU.add))
            v.wait_ge(s_v.h, s_v.n)
            v.wait_ge(s_a.h, an1)
            s_v.inc(v.tensor_scalar(st[:, 2:3], st[:, 0:1], 1.0 / D, None, ALU.mult))
            s_v.inc(v.tensor_scalar(st[:, 3:4], st[:, 1:2], 1.0 / D, None, ALU.mult))
            v.wait_ge(s_v.h, s_v.n)
            s_v.inc(v.tensor_mul(st[:, 4:5], st[:, 2:3], st[:, 2:3]))
            v.wait_ge(s_v.h, s_v.n)
            s_v.inc(v.tensor_sub(st[:, 5:6], st[:, 3:4], st[:, 4:5]))
            v.wait_ge(s_v.h, s_v.n)
            s_v.inc(v.tensor_scalar(st[:, 5:6], st[:, 5:6], 1e-5, None, ALU.add))

        block.vector(vec1)
        ex2_done = s_v.n - 3
        vn1 = s_v.n

        def act2(s, vn1=vn1, k=k):
            s.wait_ge(s_v.h, vn1)
            s_a.inc(s.activation(st[:, 6:7], st[:, 5:6], AF.Sqrt))
            if comp_out is not None:
                s_a.inc(s.activation(comp_out[:, k:k + 1], cs[:], AF.Sigmoid))

        block.scalar(act2)
        an2 = s_a.n

        def vec2(v, an2=an2, bi=bi, xng=xnb_st[bi]):
            v.wait_ge(s_a.h, an2)
            s_v.inc(v.reciprocal(st[:, 7:8], st[:, 6:7]))
            v.wait_ge(s_v.h, s_v.n)
            if xng:
                v.wait_ge(s_xs[bi].h, xng)
            s_v.inc(v.tensor_scalar(xnb[bi][:], xb[bi][:],
                                    st[:, 2:3], st[:, 7:8],
                                    ALU.subtract, ALU.mult))

        block.vector(vec2)
        xb_free[bi] = s_v.n
        vn2 = s_v.n

        def sto(sy, vn2=vn2, k=k, bi=bi):
            sy.wait_ge(s_v.h, vn2)
            s_xs[bi].inc(sy.dma_start(
                out=xn_dram[row0 + k * 128: row0 + (k + 1) * 128, :],
                in_=xnb[bi][:]), 16)

        block.sync(sto)
        xnb_st[bi] = s_xs[bi].n
        xn_at.append((bi, s_xs[bi].n))

    class LNOut:
        def barrier(self, eng):
            for s in s_xs:
                if s.n:
                    eng.wait_ge(s.h, s.n)

        def chunk_barrier(self, eng, k):
            bi, n = xn_at[k]
            eng.wait_ge(s_xs[bi].h, n)

    return LNOut()



def emit_attn_a(C, io):
    block = C.block
    wq_a, wk_a, wv_a, wo_a = io["wq_a"], io["wk_a"], io["wv_a"], io["wo_a"]
    xn_dram, q_dram, out_x = io["xn_dram"], io["q_dram"], io["out_x"]
    oner, onerb = io["oner"], io["onerb"]
    maska, valida, bo = io["maska"], io["valida"], io["bo_a"]
    ln = io["ln1"]

    stA = contextlib.ExitStack()
    attT = C.sbuf("a1attT", [128, 16 * 2048], BF16, stA)
    st1 = contextlib.ExitStack()
    stKV = contextlib.ExitStack()
    kT = C.sbuf("a1kT", [128, 4 * 3072], BF16, st1)
    vv = C.sbuf("a1v", [128, 24 * 512], BF16, st1)
    xnt = [C.sbuf(f"a1xnt{i}", [128, 16 * 512], BF16, st1) for i in range(2)]
    wks = C.sbuf("a1wk", [128, 16 * 512], BF16, stKV)
    wvs = C.sbuf("a1wv", [128, 16 * 512], BF16, stKV)

    pj = C.psum("a1pj", [128, 512], st1)
    pj2 = C.psum("a1pj2", [128, 512], st1)
    psc = [C.psum(f"a1psc{i}", [128, 512], st1) for i in range(2)]
    pat = C.psum("a1pat", [128, 512], st1)
    pat2 = C.psum("a1pat2", [128, 512], st1)
    psm = C.psum("a1psm", [128, 512], st1)
    prb = C.psum("a1prb", [128, 512], st1)

    s_t = [C.sem(f"a1_t{i}") for i in range(2)]
    s_w = C.sem("a1_w")
    s_wq = [C.sem(f"a1_wq{i}") for i in range(2)]
    s_kve = C.sem("a1_kve")
    s_mm = C.sem("a1_mm")
    s_qev = C.sem("a1_qev")
    s_qst = [C.sem(f"a1_qst{i}") for i in range(2)]
    s_ld = [C.sem(f"a1_ld{i}") for i in range(2)]
    s_sc = C.sem("a1_sc")
    s_ex = C.sem("a1_ex")
    s_mk = C.sem("a1_mk")
    s_pv = C.sem("a1_pv")
    s_sm = C.sem("a1_sm")
    s_rc = C.sem("a1_rc")
    s_bc = C.sem("a1_bc")
    s_rb = C.sem("a1_rb")
    s_ev = C.sem("a1_ev")
    s_wo = C.sem("a1_wo")
    s_ymm = C.sem("a1_ymm")
    s_yev = C.sem("a1_yev")
    s_res = [C.sem(f"a1_res{i}") for i in range(2)]

    def ldw(g):
        ln.barrier(g)
        s_w.inc(g.dma_start(out=wks[:], in_=wk_a), 16)
        s_w.inc(g.dma_start(out=wvs[:], in_=wv_a), 16)

    block.gpsimd(ldw)

    xnt_free = [0, 0]
    pjf = 0
    pj2f = 0
    for T in range(6):
        def ldt(sy, T=T, gate=xnt_free[T % 2]):
            if T == 0:
                ln.barrier(sy)
            if gate:
                sy.wait_ge(s_mm.h, gate)
            s_t[T % 2].inc(sy.dma_start(
                out=xnt[T % 2][:].rearrange("p (dc r) -> p dc r", dc=16),
                in_=xn_dram[T * 512:(T + 1) * 512, :]
                    .rearrange("r (dc p) -> r dc p", p=128),
                transpose=True), 16)

        block.sync(ldt)
        tw = s_t[T % 2].n

        def mmkv(t, T=T, tw=tw, pjf=pjf, pj2f=pj2f):
            if T == 0:
                t.wait_ge(s_w.h, 32)
            t.wait_ge(s_t[T % 2].h, tw)
            wk3 = wks[:].rearrange("p (dc n) -> p dc n", dc=16)
            wv3 = wvs[:].rearrange("p (dc n) -> p dc n", dc=16)
            x3 = xnt[T % 2][:].rearrange("p (dc r) -> p dc r", dc=16)
            for g4 in range(4):
                if pjf + g4 > 0:
                    t.wait_ge(s_kve.h, pjf + g4)
                for dc in range(16):
                    ins = t.matmul(pj[:], wk3[:, dc, g4 * 128:(g4 + 1) * 128],
                                   x3[:, dc, :], start=(dc == 0), stop=(dc == 15))
                s_mm.inc(ins)
            for tc in range(4):
                if pj2f + tc > 0:
                    t.wait_ge(s_kve.h, pj2f + tc + 4)
                for dc in range(16):
                    ins = t.matmul(pj2[:], x3[:, dc, tc * 128:(tc + 1) * 128],
                                   wv3[:, dc, :],
                                   start=(dc == 0), stop=(dc == 15))
                s_mm.inc(ins)

        block.tensor(mmkv)
        mmn = s_mm.n
        xnt_free[T % 2] = mmn

        def evkv(v, T=T, mmn=mmn):
            base = mmn - 8
            for g4 in range(4):
                v.wait_ge(s_mm.h, base + g4 + 1)
                s_kve.inc(v.tensor_copy(
                    kT[:, g4 * 3072 + T * 512: g4 * 3072 + (T + 1) * 512], pj[:]))
            for tc in range(4):
                v.wait_ge(s_mm.h, base + 4 + tc + 1)
                s_kve.inc(v.tensor_copy(
                    vv[:, (4 * T + tc) * 512:(4 * T + tc + 1) * 512], pj2[:]))

        block.vector(evkv)
        pjf = s_kve.n
        pj2f = s_kve.n

    KV_EV = s_kve.n
    KV_MM = s_mm.n
    stKV.close()
    wqb = [C.sbuf(f"a1wq{i}", [128, 16 * 128], BF16, st1) for i in range(2)]
    qev = [C.sbuf(f"a1qe{i}", [128, 512], BF16, st1) for i in range(2)]

    mm_at = {}
    u = 0
    for pash, tiles in ((0, (2, 3)), (1, (4, 5))):
        for bi, T in enumerate(tiles):
            def ldt2(sy, T=T, bi=bi, gate=xnt_free[bi]):
                sy.wait_ge(s_mm.h, gate)
                s_t[bi].inc(sy.dma_start(
                    out=xnt[bi][:].rearrange("p (dc r) -> p dc r", dc=16),
                    in_=xn_dram[T * 512:(T + 1) * 512, :]
                        .rearrange("r (dc p) -> r dc p", p=128),
                    transpose=True), 16)

            block.sync(ldt2)
        twq = [s_t[0].n, s_t[1].n]

        for h in range(16):
            def ldq(g, h=h, u=u, kvn=KV_MM, wg=mm_at.get(u - 2, 0)):
                if u == 0 and h == 0:
                    g.wait_ge(s_mm.h, kvn)
                if wg:
                    g.wait_ge(s_mm.h, wg)
                s_wq[h % 2].inc(g.dma_start(
                    out=wqb[h % 2][:], in_=wq_a[h]), 16)

            block.gpsimd(ldq)
            wn = s_wq[h % 2].n

            for bi, T in enumerate(tiles):
                i = 2 * u + bi

                def mmq(t, h=h, bi=bi, wn=wn, tn=twq[bi], i=i, qf=s_qev.n):
                    t.wait_ge(s_wq[h % 2].h, wn)
                    t.wait_ge(s_t[bi].h, tn)
                    if i == 0:
                        t.wait_ge(s_kve.h, KV_EV)
                    if qf >= 2:
                        t.wait_ge(s_qev.h, qf - 1)
                    wq3 = wqb[h % 2][:].rearrange("p (dc f) -> p dc f", dc=16)
                    x3 = xnt[bi][:].rearrange("p (dc r) -> p dc r", dc=16)
                    for dc in range(16):
                        ins = t.matmul(pj[:] if i % 2 == 0 else pj2[:],
                                       wq3[:, dc, :], x3[:, dc, :],
                                       start=(dc == 0), stop=(dc == 15))
                    s_mm.inc(ins)

                block.tensor(mmq)
                mmn = s_mm.n

                def evq(v, mmn=mmn, i=i, stn=s_qst[i % 2].n):
                    v.wait_ge(s_mm.h, mmn)
                    if stn:
                        v.wait_ge(s_qst[i % 2].h, stn)
                    s_qev.inc(v.tensor_copy(qev[i % 2][:],
                                            pj[:] if i % 2 == 0 else pj2[:]))

                block.vector(evq)
                qn = s_qev.n

                def stq(sy, h=h, T=T, qn=qn, i=i):
                    sy.wait_ge(s_qev.h, qn)
                    s_qst[i % 2].inc(sy.dma_start(
                        out=q_dram[h * 128:(h + 1) * 128,
                                   (T - 2) * 512:(T - 1) * 512],
                        in_=qev[i % 2][:]), 16)

                block.sync(stq)
            mm_at[u] = s_mm.n
            u += 1
        xnt_free[0] = s_mm.n
        xnt_free[1] = s_mm.n

    QST = [s_qst[0].n, s_qst[1].n]
    import os as _os
    if _os.environ.get('DEV_STOP', 'full') in ('kv', 'q'):
        st1.close()
        stA.close()

        class _O:
            def barrier(self, eng):
                eng.wait_ge(s_qst[0].h, QST[0])
                eng.wait_ge(s_qst[1].h, QST[1])
        return _O()

    qsl = [C.sbuf(f"a1qsl{i}", [128, 16 * 256], BF16, st1) for i in range(2)]
    P = C.sbuf("a1P", [128, 10 * 256], BF16, st1)
    sum_sb = C.sbuf("a1sum", [1, 256], F32, st1)
    rec_sb = C.sbuf("a1rec", [1, 256], F32, st1)
    recB = C.sbuf("a1recB", [128, 256], F32, st1)

    qsl_free = [0, 0]
    _attlim = int(_os.environ.get('DEV_ATT_LIM', '999'))
    pscb = [psc[0], psc[1], pj, pj2]
    pats = [pat, pat2]
    pat_free = [0, 0]
    pend = None
    u = 0

    def a_tail(rcn, jb, h, u):
        def bca(t, rcn=rcn, rbgate=s_rb.n):
            t.wait_ge(s_rc.h, rcn)
            if rbgate:
                t.wait_ge(s_rb.h, rbgate)
            s_bc.inc(t.matmul(prb[:, 0:256], oner[:], rec_sb[:],
                              start=True, stop=True))

        block.tensor(bca)
        bcn = s_bc.n

        def evat(v, bcn=bcn, jb=jb, h=h, u=u):
            v.wait_ge(s_bc.h, bcn)
            s_rb.inc(v.tensor_copy(recB[:], prb[:, 0:256]))
            v.wait_ge(s_rb.h, s_rb.n)
            s_ev.inc(v.tensor_mul(
                attT[:, h * 2048 + jb * 256: h * 2048 + (jb + 1) * 256],
                pats[u % 2][:, 0:256], recB[:]))

        block.vector(evat)
        pat_free[u % 2] = s_ev.n

    for jb in range(NQB):
        if jb * 16 >= _attlim:
            break

        def ldqs(sy, jb=jb, gate=qsl_free[jb % 2]):
            if jb == 0:
                sy.wait_ge(s_qst[0].h, QST[0])
                sy.wait_ge(s_qst[1].h, QST[1])
            if gate:
                sy.wait_ge(s_sc.h, gate)
            s_ld[jb % 2].inc(sy.dma_start(
                out=qsl[jb % 2][:].rearrange("p (hh r) -> p hh r", hh=16),
                in_=q_dram[:, jb * 256:(jb + 1) * 256]
                    .rearrange("(hh p) r -> p hh r", p=128)), 16)

        block.sync(ldqs)
        ldn = s_ld[jb % 2].n

        for h in range(16):
            g4 = h // 4
            hq = jb * 16 + h
            if hq >= _attlim:
                break

            def mmsc(t, jb=jb, h=h, g4=g4, hq=hq, ldn=ldn):
                t.wait_ge(s_ld[jb % 2].h, ldn)
                q_ap = qsl[jb % 2][:].rearrange(
                    "p (hh r) -> p hh r", hh=16)[:, h, :]
                for c in range(10):
                    if c % 2 == 0:
                        gp = hq * 5 + c // 2
                        if gp >= 4:
                            t.wait_ge(s_ex.h, gp - 3)
                    kc = 2 * jb + c
                    ins = t.matmul(
                        pscb[(c // 2) % 4][:,
                                           (c % 2) * 256:(c % 2) * 256 + 256],
                        kT[:, g4 * 3072 + kc * 128:
                           g4 * 3072 + (kc + 1) * 128],
                        q_ap, start=True, stop=True)
                    s_sc.inc(ins)

            block.tensor(mmsc)
            scn = s_sc.n

            def exps(s, scn=scn, pvgate=s_pv.n):
                if pvgate:
                    s.wait_ge(s_pv.h, pvgate)
                for p in range(5):
                    s.wait_ge(s_sc.h, scn - 10 + 2 * (p + 1))
                    s_ex.inc(s.activation(P[:, p * 512:(p + 1) * 512],
                                          pscb[p % 4][:, 0:512],
                                          AF.Exp, scale=1.0 / SQD))

            block.scalar(exps)
            exn = s_ex.n
            if _os.environ.get('DEV_ATT_OPS', 'full') == 'sc':
                continue

            def msk(v, exn=exn):
                v.wait_ge(s_ex.h, exn - 4)
                s_mk.inc(v.tensor_mul(P[:, 0:512], P[:, 0:512],
                                      maska[:, 0:512]))
                v.wait_ge(s_ex.h, exn)
                s_mk.inc(v.tensor_mul(P[:, 2048:2560], P[:, 2048:2560],
                                      maska[:, 512:1024]))

            block.vector(msk)
            mkn = s_mk.n
            if _os.environ.get('DEV_ATT_OPS', 'full') == 'msk':
                continue

            if pend is not None:
                a_tail(*pend)
                pend = None

            def mmpv(t, jb=jb, g4=g4, exn=exn, mkn=mkn, u=u,
                     evgate=pat_free[u % 2]):
                if evgate:
                    t.wait_ge(s_ev.h, evgate)
                t.wait_ge(s_ex.h, exn)
                t.wait_ge(s_mk.h, mkn)
                pb = pats[u % 2]
                for c in range(10):
                    kc = 2 * jb + c
                    t.matmul(pb[:, 0:256], vv[:, kc * 512 + g4 * 128:
                                               kc * 512 + (g4 + 1) * 128],
                             P[:, c * 256:(c + 1) * 256],
                             start=(c == 0), stop=(c == 9))
                    ins = t.matmul(psm[0:1, 0:256], valida[:, kc:kc + 1],
                                   P[:, c * 256:(c + 1) * 256],
                                   start=(c == 0), stop=(c == 9))
                s_pv.inc(ins)

            block.tensor(mmpv)
            pvn = s_pv.n
            if _os.environ.get('DEV_ATT_OPS', 'full') == 'pv':
                continue

            def cps(s, pvn=pvn):
                s.wait_ge(s_pv.h, pvn)
                s_sm.inc(s.copy(sum_sb[:], psm[0:1, 0:256]))

            block.scalar(cps)
            smn = s_sm.n

            def rcp(v, smn=smn):
                v.wait_ge(s_sm.h, smn)
                s_rc.inc(v.reciprocal(rec_sb[:], sum_sb[:]))

            block.vector(rcp)
            pend = (s_rc.n, jb, h, u)
            u += 1
        qsl_free[jb % 2] = s_sc.n

    if pend is not None:
        a_tail(*pend)
        pend = None

    ATT_ALL = s_ev.n
    st1.close()
    if _os.environ.get('DEV_STOP', 'full') == 'att':
        stA.close()
        ops = _os.environ.get('DEV_ATT_OPS', 'full')
        sf = {'sc': s_ex, 'msk': s_mk, 'pv': s_pv, 'full': s_ev}[ops]
        FN = sf.n

        class _O:
            def barrier(self, eng):
                eng.wait_ge(sf.h, FN)
        return _O()

    st2 = contextlib.ExitStack()
    wo_sb = C.sbuf("a1wo", [128, 16 * 2048], BF16, st2)
    ysb = [C.sbuf(f"a1y{i}", [128, 2048], F32, st2) for i in range(2)]
    pyb = [C.psum(f"a1py{i}", [128, 512], st2) for i in range(2)]

    def ldwo(g):
        g.wait_ge(s_ev.h, ATT_ALL)
        for mc in range(4):
            s_wo.inc(g.dma_start(
                out=wo_sb[:, mc * 8192:(mc + 1) * 8192],
                in_=wo_a[:, mc * 8192:(mc + 1) * 8192]), 16)

    block.gpsimd(ldwo)

    res_at = {}
    for tc in range(16):
        for fb in range(4):
            i = 4 * tc + fb

            def mmy(t, tc=tc, fb=fb, i=i, yegate=s_yev.n):
                if i == 0:
                    t.wait_ge(s_wo.h, 64)
                if yegate >= 2:
                    t.wait_ge(s_yev.h, yegate - 1)
                wo3 = wo_sb[:].rearrange("p (m f) -> p m f", m=16)
                pb = pyb[i % 2]
                t.matmul(pb[:], onerb[:], bo[:, fb * 512:(fb + 1) * 512],
                         start=True, stop=False)
                for mc in range(16):
                    ins = t.matmul(
                        pb[:],
                        attT[:, mc * 2048 + tc * 128: mc * 2048 + (tc + 1) * 128],
                        wo3[:, mc, fb * 512:(fb + 1) * 512],
                        start=False, stop=(mc == 15))
                s_ymm.inc(ins)

            block.tensor(mmy)
            ymn = s_ymm.n

            def evy(v, tc=tc, fb=fb, i=i, ymn=ymn,
                    rgate=res_at.get(tc - 2, 0)):
                v.wait_ge(s_ymm.h, ymn)
                if fb == 0 and rgate:
                    v.wait_ge(s_res[tc % 2].h, rgate)
                s_yev.inc(v.tensor_copy(
                    ysb[tc % 2][:, fb * 512:(fb + 1) * 512], pyb[i % 2][:]))

            block.vector(evy)

        yn = s_yev.n

        def acc(g, tc=tc, yn=yn):
            g.wait_ge(s_yev.h, yn)
            s_res[tc % 2].inc(g.dma_start(
                out=out_x[tc * 128:(tc + 1) * 128, :],
                in_=ysb[tc % 2][:], accum_op=ALU.add), 16)

        block.gpsimd(acc)
        res_at[tc] = s_res[tc % 2].n

    st2.close()
    stA.close()
    R0, R1 = s_res[0].n, s_res[1].n

    class ResOut:
        def barrier(self, eng):
            eng.wait_ge(s_res[0].h, R0)
            eng.wait_ge(s_res[1].h, R1)

    return ResOut()




def emit_moe(C, io, tag, prev):
    block = C.block
    out_x, xn_dram = io["out_x"], io["xn_dram"]
    gate_wT, w1T, w3T, w2T = io["gate_wT"], io["w1T"], io["w3T"], io["w2T"]
    seg_x, seg_meta, y01 = io["seg_x"], io["seg_meta"], io["y01"]
    oner, onec = io["oner"], io["onec"]
    ltri, tokf, base_r, mfill = io["ltri"], io["tokf"], io["base_r"], io["mfill"]

    st = contextlib.ExitStack()
    gw = C.sbuf(f"{tag}gw", [128, 16 * 8], BF16, st)
    limB = C.sbuf(f"{tag}limB", [128, NE], F32, st)
    xnc = [C.sbuf(f"{tag}xnc{i}", [128, 16 * 128], BF16, st) for i in range(2)]
    xsc = [C.sbuf(f"{tag}xsc{i}", [128, D], BF16, st) for i in range(2)]
    gsc = C.sbuf(f"{tag}gsc", [128, 8], F32, st)
    gex = C.sbuf(f"{tag}gex", [128, 8], F32, st)
    gst = C.sbuf(f"{tag}gst", [128, 8], F32, st)
    probs = C.sbuf(f"{tag}pr", [128, 8], F32, st)
    s0t = C.sbuf(f"{tag}s0", [128, 8], F32, st)
    rt = C.sbuf(f"{tag}rt", [128, 8], F32, st)
    s1t = C.sbuf(f"{tag}s1", [128, 8], F32, st)
    rb16 = C.sbuf(f"{tag}rb", [128, 8], BF16, st)
    lc = C.sbuf(f"{tag}lc", [128, 8], F32, st)
    obB = C.sbuf(f"{tag}obB", [128, 8], F32, st)
    dst = C.sbuf(f"{tag}dst", [128, 8], F32, st)
    toti = C.sbuf(f"{tag}toti", [1, 16 * 8], F32, st)
    offs = C.sbuf(f"{tag}offs", [1, 16 * 8], F32, st)
    obk = C.sbuf(f"{tag}obk", [1, 8], F32, st)
    sm1 = C.sbuf(f"{tag}sm1", [128, 1], F32, st)
    sm2 = C.sbuf(f"{tag}sm2", [128, 1], F32, st)
    nmx = C.sbuf(f"{tag}nmx", [128, 1], F32, st)
    rs = C.sbuf(f"{tag}rs", [128, 1], F32, st)
    rc1 = C.sbuf(f"{tag}rc1", [128, 1], F32, st)
    di0 = [C.sbuf(f"{tag}dia{i}", [128, 1], I32, st) for i in range(2)]
    di1 = [C.sbuf(f"{tag}dib{i}", [128, 1], I32, st) for i in range(2)]
    dd = C.sbuf(f"{tag}dd", [128, 2], F32, st)
    mv0 = [C.sbuf(f"{tag}mva{i}", [128, 2], F32, st) for i in range(2)]
    mv1 = [C.sbuf(f"{tag}mvb{i}", [128, 2], F32, st) for i in range(2)]
    mfl = C.sbuf(f"{tag}mfl", [128, 256], F32, st)

    pg = C.psum(f"{tag}pg", [128, 512], st)
    pl = C.psum(f"{tag}pl", [128, 512], st)
    pt = C.psum(f"{tag}pt", [128, 512], st)
    pbk = C.psum(f"{tag}pbk", [128, 512], st)

    lnstack = contextlib.ExitStack()
    ln = emit_ln(C, f"{tag}ln", NCH,
                 lambda k: out_x[k * 128:(k + 1) * 128, :],
                 xn_dram, HALO, [], stack=lnstack, prev=prev)
    lnstack.close()

    s_w = C.sem(f"{tag}_w")
    s_mi = C.sem(f"{tag}_mi")
    s_tl = [C.sem(f"{tag}_tl{i}") for i in range(2)]
    s_xl = [C.sem(f"{tag}_xl{i}") for i in range(2)]
    s_mm = C.sem(f"{tag}_mm")
    s_a = C.sem(f"{tag}_a")
    s_v = C.sem(f"{tag}_v")
    s_sg = [C.sem(f"{tag}_sg{i}") for i in range(2)]

    lim_sb = C.sbuf(f"{tag}lims", [1, NE], F32, st)
    w_base = s_w.n

    def ldg(g):
        prev.barrier(g)
        s_w.inc(g.dma_start(out=gw[:], in_=gate_wT), 16)
        s_w.inc(g.dma_start(out=mfl[:], in_=mfill), 16)
        s_w.inc(g.dma_start(out=lim_sb[:], in_=io["lim_r"]), 16)

    block.gpsimd(ldg)

    def limmm(t, wn=w_base + 48):
        t.wait_ge(s_w.h, wn)
        s_mm.inc(t.matmul(pbk[0:128, 0:NE], oner[:], lim_sb[:],
                          start=True, stop=True))

    block.tensor(limmm)
    _limn = s_mm.n

    def limev(v, n=_limn):
        v.wait_ge(s_mm.h, n)
        s_v.inc(v.tensor_copy(limB[:], pbk[0:128, 0:NE]))

    block.vector(limev)
    LIMV = s_v.n

    def mini(sy, wn=w_base + 48):
        sy.wait_ge(s_w.h, wn)
        s_mi.inc(sy.dma_start(
            out=seg_meta.rearrange("(n p) c -> p n c", p=128),
            in_=mfl[0:128, 0:2 * (NE * CAP // 128)]
                .rearrange("p (n c) -> p n c", c=2)), 16)

    block.sync(mini)

    small_free = [0, 0]
    xnc_free = [0, 0]

    for k in range(NCH):
        bi = k % 2

        def tld(sy, k=k, bi=bi, gate=xnc_free[bi], sgate=small_free[bi]):
            ln.chunk_barrier(sy, k)
            if gate:
                sy.wait_ge(s_mm.h, gate)
            if sgate:
                sy.wait_ge(s_sg[bi].h, sgate)
            s_tl[bi].inc(sy.dma_start(
                out=xnc[bi][:].rearrange("p (dc r) -> p dc r", dc=16),
                in_=xn_dram[HALO + k * 128: HALO + (k + 1) * 128, :]
                    .rearrange("r (dc p) -> r dc p", p=128),
                transpose=True), 16)
            s_xl[bi].inc(sy.dma_start(
                out=xsc[bi][:],
                in_=xn_dram[HALO + k * 128: HALO + (k + 1) * 128, :]), 16)

        block.sync(tld)
        tln = s_tl[bi].n

        def gmm(t, k=k, bi=bi, tln=tln, an=s_a.n, vn=s_v.n):
            t.wait_ge(s_tl[bi].h, tln)
            if k == 0:
                t.wait_ge(s_w.h, w_base + 48)
            if an:
                t.wait_ge(s_a.h, an)
            x3 = xnc[bi][:].rearrange("p (dc r) -> p dc r", dc=16)
            g3 = gw[:].rearrange("p (dc e) -> p dc e", dc=16)
            for dc in range(16):
                ins = t.matmul(pg[0:128, 0:8], x3[:, dc, :], g3[:, dc, :],
                               start=(dc == 0), stop=(dc == 15))
            s_mm.inc(ins)

        block.tensor(gmm)
        gmn = s_mm.n
        xnc_free[bi] = gmn

        def gev(s, gmn=gmn):
            s.wait_ge(s_mm.h, gmn)
            s_a.inc(s.copy(gsc[:], pg[0:128, 0:8]))

        block.scalar(gev)
        gan = s_a.n

        def rt1(v, gan=gan, vprior=s_v.n):
            v.wait_ge(s_v.h, vprior)
            v.wait_ge(s_a.h, gan)
            s_v.inc(v.reduce_max(sm1[:], gsc[:], AX.X))
            v.wait_ge(s_v.h, s_v.n)
            s_v.inc(v.tensor_scalar(nmx[:], sm1[:], -1.0, None, ALU.mult))

        block.vector(rt1)
        vn1 = s_v.n

        def rt2(s, vn1=vn1):
            s.wait_ge(s_v.h, vn1)
            s_a.inc(s.activation(gex[:], gsc[:], AF.Exp, bias=nmx[:]))

        block.scalar(rt2)
        an2 = s_a.n

        def rt3(v, an2=an2, k=k, vprior=s_v.n):
            v.wait_ge(s_v.h, vprior)
            v.wait_ge(s_a.h, an2)
            s_v.inc(v.tensor_reduce(rs[:], gex[:], AX.X, ALU.add))
            v.wait_ge(s_v.h, s_v.n)
            s_v.inc(v.reciprocal(rc1[:], rs[:]))
            v.wait_ge(s_v.h, s_v.n)
            s_v.inc(v.tensor_scalar(probs[:], gex[:], rc1[:], None, ALU.mult))
            v.wait_ge(s_v.h, s_v.n)
            s_v.inc(v.reduce_max(sm1[:], probs[:], AX.X))
            v.wait_ge(s_v.h, s_v.n)
            s_v.inc(v.tensor_scalar(s0t[:], probs[:], sm1[:], None, ALU.is_ge))
            v.wait_ge(s_v.h, s_v.n)
            s_v.inc(v.scalar_tensor_tensor(gst[:], s0t[:], -1.0e30, probs[:],
                                           ALU.mult, ALU.add))
            v.wait_ge(s_v.h, s_v.n)
            s_v.inc(v.reduce_max(sm2[:], gst[:], AX.X))
            v.wait_ge(s_v.h, s_v.n)
            s_v.inc(v.tensor_scalar(rt[:], probs[:], sm2[:], None, ALU.is_ge))
            v.wait_ge(s_v.h, s_v.n)
            s_v.inc(v.tensor_sub(s1t[:], rt[:], s0t[:]))
            s_v.inc(v.tensor_copy(rb16[:], rt[:]))

        block.vector(rt3)
        vn3 = s_v.n

        def lmm(t, vn3=vn3, an=s_a.n):
            t.wait_ge(s_v.h, vn3)
            if an > 2:
                t.wait_ge(s_a.h, an)
            s_mm.inc(t.matmul(pl[0:128, 0:8], ltri[:], rb16[:],
                              start=True, stop=True))
            s_mm.inc(t.matmul(pt[0:1, 0:8], onec[:], rb16[:],
                              start=True, stop=True))

        block.tensor(lmm)
        lmn = s_mm.n

        def lev(s, lmn=lmn, k=k):
            s.wait_ge(s_mm.h, lmn)
            s_a.inc(s.copy(lc[:], pl[0:128, 0:8]))
            s_a.inc(s.copy(toti[0:1, k * 8:(k + 1) * 8], pt[0:1, 0:8]))

        block.scalar(lev)
        lan = s_a.n

        def pfx(v, lan=lan, k=k, vprior=s_v.n):
            v.wait_ge(s_v.h, vprior)
            v.wait_ge(s_a.h, lan)
            if k == 0:
                s_v.inc(v.memset(offs[0:1, 0:8], 0.0))
            else:
                s_v.inc(v.tensor_add(offs[0:1, k * 8:(k + 1) * 8],
                                     offs[0:1, (k - 1) * 8:k * 8],
                                     toti[0:1, (k - 1) * 8:k * 8]))
            v.wait_ge(s_v.h, s_v.n)
            s_v.inc(v.tensor_add(obk[:], offs[0:1, k * 8:(k + 1) * 8],
                                 base_r[:]))

        block.vector(pfx)
        pfn = s_v.n

        def bmm(t, pfn=pfn, vn=s_v.n):
            t.wait_ge(s_v.h, pfn)
            s_mm.inc(t.matmul(pbk[0:128, 0:8], oner[:], obk[:],
                              start=True, stop=True))

        block.tensor(bmm)
        bmn = s_mm.n

        def dcmp(v, bmn=bmn, k=k, bi=bi, gate=small_free[bi], vprior=s_v.n):
            v.wait_ge(s_v.h, vprior)
            v.wait_ge(s_mm.h, bmn)
            s_v.inc(v.tensor_copy(obB[:], pbk[0:128, 0:8]))
            v.wait_ge(s_v.h, s_v.n)
            s_v.inc(v.tensor_add(dst[:], lc[:], obB[:]))
            v.wait_ge(s_v.h, s_v.n)
            s_v.inc(v.tensor_scalar(gst[:], dst[:], 1.0e5, None, ALU.subtract))
            v.wait_ge(s_v.h, s_v.n)
            s_v.inc(v.tensor_mul(gst[:], gst[:], rt[:]))
            v.wait_ge(s_v.h, s_v.n)
            s_v.inc(v.tensor_scalar(dst[:], gst[:], 1.0e5, None, ALU.add))
            v.wait_ge(s_v.h, s_v.n)
            v.wait_ge(s_v.h, LIMV)
            s_v.inc(v.tensor_tensor(gst[:], dst[:], limB[:], ALU.is_ge))
            v.wait_ge(s_v.h, s_v.n)
            s_v.inc(v.scalar_tensor_tensor(dst[:], gst[:], 2.0e5, dst[:],
                                           ALU.mult, ALU.add))
            v.wait_ge(s_v.h, s_v.n)
            if gate:
                v.wait_ge(s_sg[bi].h, gate)
            s_v.inc(v.tensor_mul(gst[:], s0t[:], dst[:]))
            v.wait_ge(s_v.h, s_v.n)
            s_v.inc(v.tensor_reduce(dd[:, 0:1], gst[:], AX.X, ALU.add))
            v.wait_ge(s_v.h, s_v.n)
            s_v.inc(v.tensor_mul(gst[:], s1t[:], dst[:]))
            v.wait_ge(s_v.h, s_v.n)
            s_v.inc(v.tensor_reduce(dd[:, 1:2], gst[:], AX.X, ALU.add))
            v.wait_ge(s_v.h, s_v.n)
            s_v.inc(v.tensor_copy(di0[bi][:], dd[:, 0:1]))
            s_v.inc(v.tensor_copy(di1[bi][:], dd[:, 1:2]))
            s_v.inc(v.tensor_scalar(mv0[bi][:, 0:1], tokf[:, k:k + 1],
                                    0.0, None, ALU.add))
            s_v.inc(v.tensor_scalar(mv1[bi][:, 0:1], tokf[:, k:k + 1],
                                    float(TOK), None, ALU.add))
            s_v.inc(v.tensor_copy(mv0[bi][:, 1:2], sm1[:]))
            s_v.inc(v.tensor_copy(mv1[bi][:, 1:2], sm2[:]))

        block.vector(dcmp)
        dcn = s_v.n

        def sct(g, k=k, bi=bi, dcn=dcn, xln=s_xl[bi].n, min_=s_mi.n):
            g.wait_ge(s_v.h, dcn)
            g.wait_ge(s_xl[bi].h, xln)
            if k < 2:
                g.wait_ge(s_mi.h, min_)
                if tag == "m2":
                    g.wait_ge(io["s_zi"].h, io["zi_n"])
            s_sg[bi].inc(g.indirect_dma_start(
                seg_x, bass.IndirectOffsetOnAxis(di0[bi][:], 0),
                xsc[bi][:], None,
                bounds_check=io['bnd_seg'], oob_is_err=False), 16)
            s_sg[bi].inc(g.indirect_dma_start(
                seg_x, bass.IndirectOffsetOnAxis(di1[bi][:], 0),
                xsc[bi][:], None,
                bounds_check=io['bnd_seg'], oob_is_err=False), 16)
            s_sg[bi].inc(g.indirect_dma_start(
                seg_meta, bass.IndirectOffsetOnAxis(di0[bi][:], 0),
                mv0[bi][:], None,
                bounds_check=io['bnd_seg'], oob_is_err=False), 16)

        block.gpsimd(sct)

        def sct2(g, bi=bi):
            s_sg[bi].inc(g.indirect_dma_start(
                seg_meta, bass.IndirectOffsetOnAxis(di1[bi][:], 0),
                mv1[bi][:], None,
                bounds_check=io['bnd_seg'], oob_is_err=False), 16)

        block.gpsimd(sct2)
        small_free[bi] = s_sg[bi].n

    SG = [s_sg[0].n, s_sg[1].n]
    st.close()

    stE = contextlib.ExitStack()
    hT = C.sbuf(f"{tag}hT", [128, 16 * CAP], BF16, stE)
    sil = C.sbuf(f"{tag}sil", [128, 512], BF16, stE)
    s_xg = C.sem(f"{tag}_xg")
    s_wc = [C.sem(f"{tag}_wc{i}") for i in range(2)]
    s_w2 = C.sem(f"{tag}_w2")
    s_he = C.sem(f"{tag}_he")
    s_hs = C.sem(f"{tag}_hs")
    s_ml = [C.sem(f"{tag}_ml{i}") for i in range(2)]
    s_ye = C.sem(f"{tag}_ye")
    s_ysc = [C.sem(f"{tag}_ysc{i}") for i in range(2)]

    ph1 = C.psum(f"{tag}ph1", [128, 512], stE)
    ph3 = C.psum(f"{tag}ph3", [128, 512], stE)
    py = [C.psum(f"{tag}py{i}", [128, 512], stE) for i in range(2)]

    for e in range(NE):
        stX = contextlib.ExitStack()
        xgT = C.sbuf(f"{tag}xg{e}", [128, 16 * CAP], BF16, stX)
        w1cb = [C.sbuf(f"{tag}w1c{e}{i}", [128, 16 * 128], BF16, stX)
                for i in range(2)]
        w3cb = [C.sbuf(f"{tag}w3c{e}{i}", [128, 16 * 128], BF16, stX)
                for i in range(2)]

        def xgl(sy, e=e, hen=s_he.n, mmn=s_mm.n,
                y0=s_ysc[0].n, y1=s_ysc[1].n):
            if e == 0:
                sy.wait_ge(s_sg[0].h, SG[0])
                sy.wait_ge(s_sg[1].h, SG[1])
            if hen:
                sy.wait_ge(s_he.h, hen)
            if e > 0:
                sy.wait_ge(s_mm.h, mmn)
                sy.wait_ge(s_ysc[0].h, y0)
                sy.wait_ge(s_ysc[1].h, y1)
            s_xg.inc(sy.dma_start(
                out=xgT[:].rearrange("p (dc r) -> p dc r", dc=16),
                in_=seg_x[e * CAP:(e + 1) * CAP, :]
                    .rearrange("r (dc p) -> r dc p", p=128),
                transpose=True), 16)

        block.sync(xgl)
        xgn = s_xg.n

        for fc in range(16):
            wb = fc % 2

            def wld(g, e=e, fc=fc, wb=wb, gate=s_mm.n,
                    y0=s_ysc[0].n, y1=s_ysc[1].n):
                if fc >= 2:
                    g.wait_ge(s_mm.h, gate)
                if e > 0 and fc < 2:
                    g.wait_ge(s_ysc[0].h, y0)
                    g.wait_ge(s_ysc[1].h, y1)
                s_wc[wb].inc(g.dma_start(out=w1cb[wb][:],
                                         in_=w1T[e, fc]), 16)
                s_wc[wb].inc(g.dma_start(out=w3cb[wb][:],
                                         in_=w3T[e, fc]), 16)

            block.gpsimd(wld)
            wcn = s_wc[wb].n

            for rc in range(CAP // 512):
                def hmm(t, fc=fc, rc=rc, wb=wb, wcn=wcn, xgn=xgn,
                        heg=s_he.n, hsg=s_hs.n):
                    t.wait_ge(s_wc[wb].h, wcn)
                    if fc == 0 and rc == 0:
                        t.wait_ge(s_xg.h, xgn)
                    if hsg:
                        t.wait_ge(s_hs.h, hsg)
                    if heg:
                        t.wait_ge(s_he.h, heg)
                    w13 = w1cb[wb][:].rearrange("p (dc f) -> p dc f", dc=16)
                    w33 = w3cb[wb][:].rearrange("p (dc f) -> p dc f", dc=16)
                    x3 = xgT[:].rearrange("p (dc r) -> p dc r", dc=16)
                    for dc in range(16):
                        t.matmul(ph1[:], w13[:, dc, :],
                                 x3[:, dc, rc * 512:(rc + 1) * 512],
                                 start=(dc == 0), stop=(dc == 15))
                    for dc in range(16):
                        ins = t.matmul(ph3[:], w33[:, dc, :],
                                       x3[:, dc, rc * 512:(rc + 1) * 512],
                                       start=(dc == 0), stop=(dc == 15))
                    s_mm.inc(ins)

                block.tensor(hmm)
                hmn = s_mm.n

                def hsl(s, hmn=hmn):
                    s.wait_ge(s_mm.h, hmn)
                    s_hs.inc(s.activation(sil[:], ph1[:], AF.Silu))

                block.scalar(hsl)
                hsn = s_hs.n

                def hev(v, hsn=hsn, fc=fc, rc=rc):
                    v.wait_ge(s_hs.h, hsn)
                    s_he.inc(v.tensor_mul(
                        hT[:, fc * CAP + rc * 512: fc * CAP + (rc + 1) * 512],
                        ph3[:], sil[:]))

                block.vector(hev)

        HDONE = s_he.n
        stX.close()
        stW2 = contextlib.ExitStack()
        w2sb = C.sbuf(f"{tag}w2{e}", [128, 16 * D], BF16, stW2)
        ysb = [C.sbuf(f"{tag}ys{e}{i}", [128, D], F32, stW2) for i in range(2)]
        mrow = [C.sbuf(f"{tag}mr{e}{i}", [128, 2], F32, stW2) for i in range(2)]
        dix = [C.sbuf(f"{tag}dx{e}{i}", [128, 1], I32, stW2) for i in range(2)]

        def w2l(g, e=e, hd=HDONE):
            g.wait_ge(s_he.h, hd)
            for sl in range(4):
                s_w2.inc(g.dma_start(
                    out=w2sb[:, sl * 8192:(sl + 1) * 8192],
                    in_=w2T[e][:, sl * 8192:(sl + 1) * 8192]), 16)

        block.gpsimd(w2l)
        w2n = s_w2.n

        for rc in range(CAP // 128):
            bi = rc % 2

            def mld(sy, e=e, rc=rc, bi=bi, gate=s_ysc[bi].n, min_=s_mi.n):
                if gate:
                    sy.wait_ge(s_ysc[bi].h, gate)
                sy.wait_ge(s_mi.h, min_)
                s_ml[bi].inc(sy.dma_start(
                    out=mrow[bi][:],
                    in_=seg_meta[e * CAP + rc * 128: e * CAP + (rc + 1) * 128, :]),
                    16)

            block.sync(mld)
            mln = s_ml[bi].n

            def mcv(v, bi=bi, mln=mln, vprior=s_v.n):
                v.wait_ge(s_v.h, vprior)
                v.wait_ge(s_ml[bi].h, mln)
                s_v.inc(v.tensor_copy(dix[bi][:], mrow[bi][:, 0:1]))

            block.vector(mcv)
            mcn = s_v.n

            for db in range(4):
                i = 4 * rc + db

                def ymm(t, rc=rc, db=db, i=i, w2n=w2n, yeg=s_ye.n):
                    if i == 0:
                        t.wait_ge(s_w2.h, w2n)
                    if yeg >= 2:
                        t.wait_ge(s_ye.h, yeg - 1)
                    w23 = w2sb[:].rearrange("p (Fc d) -> p Fc d", Fc=16)
                    h3 = hT[:].rearrange("p (Fc r) -> p Fc r", Fc=16)
                    for Fc in range(16):
                        ins = t.matmul(py[i % 2][:],
                                       h3[:, Fc, rc * 128:(rc + 1) * 128],
                                       w23[:, Fc, db * 512:(db + 1) * 512],
                                       start=(Fc == 0), stop=(Fc == 15))
                    s_mm.inc(ins)

                block.tensor(ymm)
                ymn = s_mm.n

                def yev(v, bi=bi, db=db, i=i, ymn=ymn, mcn=mcn):
                    v.wait_ge(s_mm.h, ymn)
                    v.wait_ge(s_v.h, mcn)
                    s_ye.inc(v.tensor_scalar(
                        ysb[bi][:, db * 512:(db + 1) * 512], py[i % 2][:],
                        mrow[bi][:, 1:2], None, ALU.mult))

                block.vector(yev)

            yen = s_ye.n

            def ysct(g, bi=bi, yen=yen):
                g.wait_ge(s_ye.h, yen)
                s_ysc[bi].inc(g.indirect_dma_start(
                    y01, bass.IndirectOffsetOnAxis(dix[bi][:], 0),
                    ysb[bi][:], None,
                    bounds_check=io['bnd_y'], oob_is_err=False), 16)

            block.gpsimd(ysct)

        stW2.close()

    YS = [s_ysc[0].n, s_ysc[1].n]
    stE.close()

    stR = contextlib.ExitStack()
    y0b = [C.sbuf(f"{tag}y0{i}", [128, D], F32, stR) for i in range(2)]
    y1b = [C.sbuf(f"{tag}y1{i}", [128, D], F32, stR) for i in range(2)]
    s_yl = [C.sem(f"{tag}_yl{i}") for i in range(2)]
    s_ra = C.sem(f"{tag}_ra")
    s_res = [C.sem(f"{tag}_res{i}") for i in range(2)]
    res_at = [0, 0]

    for k in range(NCH):
        bi = k % 2

        def yld(sy, k=k, bi=bi, gate=res_at[bi]):
            if k == 0:
                sy.wait_ge(s_ysc[0].h, YS[0])
                sy.wait_ge(s_ysc[1].h, YS[1])
            if gate:
                sy.wait_ge(s_res[bi].h, gate)
            s_yl[bi].inc(sy.dma_start(out=y0b[bi][:],
                                      in_=y01[k * 128:(k + 1) * 128, :]), 16)
            s_yl[bi].inc(sy.dma_start(
                out=y1b[bi][:], in_=y01[TOK + k * 128: TOK + (k + 1) * 128, :]),
                16)

        block.sync(yld)
        yln = s_yl[bi].n

        def radd(v, bi=bi, yln=yln):
            v.wait_ge(s_yl[bi].h, yln)
            s_ra.inc(v.tensor_add(y0b[bi][:], y0b[bi][:], y1b[bi][:]))

        block.vector(radd)
        ran = s_ra.n

        def racc(g, k=k, bi=bi, ran=ran):
            g.wait_ge(s_ra.h, ran)
            s_res[bi].inc(g.dma_start(
                out=out_x[k * 128:(k + 1) * 128, :],
                in_=y0b[bi][:], accum_op=ALU.add), 16)

        block.gpsimd(racc)
        res_at[bi] = s_res[bi].n

    stR.close()
    R = [s_res[0].n, s_res[1].n]

    class ResOut:
        def barrier(self, eng):
            eng.wait_ge(s_res[0].h, R[0])
            eng.wait_ge(s_res[1].h, R[1])

    return ResOut()




def emit_attn_b(C, io, prev):
    block = C.block
    out_x, xn_dram, q_dram = io["out_x"], io["xn_dram"], io["q_dram"]
    wq_b, wkv_b, wo_b = io["wq_b"], io["wkv_b"], io["wo_b"]
    kv_contrib, kv_all, kv_win = io["kv_contrib"], io["kv_all"], io["kv_win"]
    oner, onec, onerb = io["oner"], io["onec"], io["onerb"]
    bo = io["bo_b_sb"]

    stP = contextlib.ExitStack()
    comp = C.sbuf("b3comp", [128, NCH], F32, stP)
    ropes = C.sbuf("b3rsin", [128, 16 * 64], F32, stP)
    ropec = C.sbuf("b3rcos", [128, 16 * 64], F32, stP)
    wrow = C.sbuf("b3w", [128, 16 * 2048], BF16, stP)
    xnc = [C.sbuf(f"b3xnc{i}", [128, 16 * 128], BF16, stP) for i in range(2)]
    qf = C.sbuf("b3qf", [128, 2048], F32, stP)
    t1 = C.sbuf("b3t1", [128, 64], F32, stP)
    t2 = C.sbuf("b3t2", [128, 64], F32, stP)
    orow = [C.sbuf(f"b3or{i}", [128, 2048], BF16, stP) for i in range(2)]

    pq = [C.psum("b3pq%d" % i, [128, 512], stP) for i in range(2)]

    s_rl = C.sem("b3_rl")
    s_wl = C.sem("b3_wl")
    s_tl = [C.sem(f"b3_tl{i}") for i in range(2)]
    s_mm = C.sem("b3_mm")
    s_a = C.sem("b3_a")
    s_v = C.sem("b3_v")
    s_p = C.sem("b3_p")
    s_st = [C.sem(f"b3_st{i}") for i in range(2)]
    s_cc = C.sem("b3_cc")
    s_wg = C.sem("b3_wg")
    s_ws = C.sem("b3_ws")

    lnstack = contextlib.ExitStack()
    ln = emit_ln(C, "b3ln", NCH,
                 lambda k: out_x[k * 128:(k + 1) * 128, :],
                 xn_dram, HALO, [], comp_out=comp, stack=lnstack, prev=prev)
    lnstack.close()

    def ldr(sy):
        s_rl.inc(sy.dma_start(out=ropes[:], in_=io["rope_sin"]), 16)
        s_rl.inc(sy.dma_start(out=ropec[:], in_=io["rope_cos"]), 16)

    block.sync(ldr)

    orow_st = [0, 0]
    for pas in range(2):
        def ldwb(g, pas=pas, mmn=s_mm.n):
            if pas == 0:
                ln.barrier(g)
            else:
                g.wait_ge(s_mm.h, mmn)
            s_wl.inc(g.dma_start(out=wrow[:],
                                 in_=(wq_b if pas == 0 else wkv_b)), 16)

        block.gpsimd(ldwb)
        wn = s_wl.n

        for k in range(NCH):
            bi = k % 2

            def tldb(sy, k=k, bi=bi, pas=pas, gate=s_mm.n):
                if k >= 2 or pas == 1:
                    sy.wait_ge(s_mm.h, gate)
                ln.chunk_barrier(sy, k)
                s_tl[bi].inc(sy.dma_start(
                    out=xnc[bi][:].rearrange("p (dc r) -> p dc r", dc=16),
                    in_=xn_dram[HALO + k * 128: HALO + (k + 1) * 128, :]
                        .rearrange("r (dc p) -> r dc p", p=128),
                    transpose=True), 16)

            block.sync(tldb)
            tln = s_tl[bi].n

            ev_at = [0, 0]
            for fb in range(4):
                def qmmf(t, k=k, bi=bi, fb=fb, tln=tln, wn=wn,
                         gate=ev_at[fb % 2]):
                    if fb == 0:
                        t.wait_ge(s_tl[bi].h, tln)
                        t.wait_ge(s_wl.h, wn)
                    if gate:
                        t.wait_ge(s_v.h, gate)
                    x3 = xnc[bi][:].rearrange("p (dc r) -> p dc r", dc=16)
                    w3 = wrow[:].rearrange("p (dc f) -> p dc f", dc=16)
                    for dc in range(16):
                        ins = t.matmul(pq[fb % 2][:], x3[:, dc, :],
                                       w3[:, dc, fb * 512:(fb + 1) * 512],
                                       start=(dc == 0), stop=(dc == 15))
                    s_mm.inc(ins)

                block.tensor(qmmf)
                mmn = s_mm.n

                if pas == 0:
                    def qevf(v, k=k, fb=fb, mmn=mmn, vprior=s_v.n):
                        v.wait_ge(s_v.h, vprior)
                        v.wait_ge(s_mm.h, mmn)
                        s_v.inc(v.tensor_scalar(
                            qf[:, fb * 512:(fb + 1) * 512], pq[fb % 2][:],
                            comp[:, k:k + 1], None, ALU.mult))
                else:
                    def qevf(v, k=k, fb=fb, mmn=mmn, vprior=s_v.n,
                             og=orow_st[k % 2]):
                        v.wait_ge(s_v.h, vprior)
                        v.wait_ge(s_mm.h, mmn)
                        if fb < 2:
                            s_v.inc(v.tensor_copy(
                                qf[:, fb * 512:(fb + 1) * 512], pq[fb % 2][:]))
                        else:
                            if fb == 2 and og:
                                v.wait_ge(s_st[k % 2].h, og)
                            s_v.inc(v.tensor_copy(
                                orow[k % 2][:, fb * 512:(fb + 1) * 512],
                                pq[fb % 2][:]))

                block.vector(qevf)
                ev_at[fb % 2] = s_v.n

            if pas == 0:
                def qev(v, k=k, vprior=s_v.n, og=orow_st[k % 2]):
                    v.wait_ge(s_v.h, vprior)
                    v.wait_ge(s_rl.h, 32)
                    if og:
                        v.wait_ge(s_st[k % 2].h, og)
                    for h in range(16):
                        q1 = qf[:, h * 128:h * 128 + 64]
                        q2 = qf[:, h * 128 + 64:h * 128 + 128]
                        ck = ropec[:, k * 64:(k + 1) * 64]
                        sk = ropes[:, k * 64:(k + 1) * 64]
                        v.wait_ge(s_v.h, s_v.n)
                        s_v.inc(v.tensor_mul(t1[:], q1, ck))
                        s_v.inc(v.tensor_mul(t2[:], q2, sk))
                        v.wait_ge(s_v.h, s_v.n)
                        s_v.inc(v.tensor_sub(
                            orow[k % 2][:, h * 128:h * 128 + 64], t1[:], t2[:]))
                        v.wait_ge(s_v.h, s_v.n)
                        s_v.inc(v.tensor_mul(t1[:], q1, sk))
                        s_v.inc(v.tensor_mul(t2[:], q2, ck))
                        v.wait_ge(s_v.h, s_v.n)
                        s_v.inc(v.tensor_add(
                            orow[k % 2][:, h * 128 + 64:h * 128 + 128],
                            t1[:], t2[:]))

                block.vector(qev)
                evn = s_v.n

                def qst(sy, k=k, evn=evn, bi=k % 2, stn=s_st[k % 2].n):
                    sy.wait_ge(s_v.h, evn)
                    s_st[bi].inc(sy.dma_start(
                        out=q_dram[k * 128:(k + 1) * 128, :],
                        in_=orow[bi][:]), 16)

                block.sync(qst)
                orow_st[k % 2] = s_st[k % 2].n
            else:
                def kev(v, k=k, vprior=s_v.n, og=orow_st[k % 2]):
                    v.wait_ge(s_v.h, vprior)
                    v.wait_ge(s_rl.h, 32)
                    if og:
                        v.wait_ge(s_st[k % 2].h, og)
                    for h in range(8):
                        q1 = qf[:, h * 128:h * 128 + 64]
                        q2 = qf[:, h * 128 + 64:h * 128 + 128]
                        ck = ropec[:, k * 64:(k + 1) * 64]
                        sk = ropes[:, k * 64:(k + 1) * 64]
                        v.wait_ge(s_v.h, s_v.n)
                        s_v.inc(v.tensor_mul(t1[:], q1, ck))
                        s_v.inc(v.tensor_mul(t2[:], q2, sk))
                        v.wait_ge(s_v.h, s_v.n)
                        s_v.inc(v.tensor_sub(
                            orow[k % 2][:, h * 128:h * 128 + 64], t1[:], t2[:]))
                        v.wait_ge(s_v.h, s_v.n)
                        s_v.inc(v.tensor_mul(t1[:], q1, sk))
                        s_v.inc(v.tensor_mul(t2[:], q2, ck))
                        v.wait_ge(s_v.h, s_v.n)
                        s_v.inc(v.tensor_add(
                            orow[k % 2][:, h * 128 + 64:h * 128 + 128],
                            t1[:], t2[:]))

                block.vector(kev)
                evn = s_v.n

                def kst(sy, k=k, evn=evn, bi=k % 2):
                    sy.wait_ge(s_v.h, evn)
                    s_st[bi].inc(sy.dma_start(
                        out=kv_contrib[k * 128:(k + 1) * 128, :],
                        in_=orow[bi][:]), 16)

                block.sync(kst)
                orow_st[k % 2] = s_st[k % 2].n

    QKV_ST = [s_st[0].n, s_st[1].n]

    dump = _os2.environ.get("DEV_B_DUMP")
    if dump in ("q", "kv"):
        srcd = q_dram if dump == "q" else kv_contrib
        s_dp = C.sem("b3_dp")

        def dmp(g):
            g.wait_ge(s_st[0].h, QKV_ST[0])
            g.wait_ge(s_st[1].h, QKV_ST[1])
            for r in range(16):
                s_dp.inc(g.dma_start(
                    out=out_x[r * 128:(r + 1) * 128, :],
                    in_=srcd[r * 128:(r + 1) * 128, :]), 16)

        block.gpsimd(dmp)
        DN = s_dp.n

        class _O:
            def barrier(self, eng):
                eng.wait_ge(s_dp.h, DN)
        stP.close()
        return _O()

    gbuf = C.sbuf("b3gb", [128, 2048], BF16, stP)
    widx = C.sbuf("b3wi", [128, (TOK + 640) // 128], I32, stP)
    s_wi = C.sem("b3_wi")

    def ldwi(sy):
        s_wi.inc(sy.dma_start(out=widx[:], in_=io["kvwin_idx"]), 16)

    block.sync(ldwi)

    if _os2.environ.get("DEV_NO_CC"):
        def coll(g):
            g.wait_ge(s_st[0].h, QKV_ST[0])
            g.wait_ge(s_st[1].h, QKV_ST[1])
            for rr in range(4):
                s_cc.inc(g.dma_start(out=kv_all[rr * TOK:(rr + 1) * TOK, :],
                                     in_=kv_contrib), 16)

        block.gpsimd(coll)
    else:
        def coll(g):
            g.wait_ge(s_st[0].h, QKV_ST[0])
            g.wait_ge(s_st[1].h, QKV_ST[1])
            s_cc.inc(g.collective_compute(
                "AllGather", ALU.bypass,
                replica_groups=[[0, 1, 2, 3], [4, 5, 6, 7]],
                ins=[kv_contrib.opt()], outs=[kv_all.opt()]))

        block.gpsimd(coll)

    WINC = (TOK + 640) // 128
    for r in range(WINC):
        def wgat(g, r=r, wsn=s_ws.n):
            if r == 0:
                g.wait_ge(s_cc.h, s_cc.n)
                g.wait_ge(s_wi.h, 16)
            else:
                g.wait_ge(s_ws.h, wsn)
            s_wg.inc(g.indirect_dma_start(
                gbuf[:], None, kv_all,
                bass.IndirectOffsetOnAxis(widx[:, r:r + 1], 0)), 16)

        block.gpsimd(wgat)
        wgn = s_wg.n

        def wsto(sy, r=r, wgn=wgn):
            sy.wait_ge(s_wg.h, wgn)
            s_ws.inc(sy.dma_start(
                out=kv_win[r * 128:(r + 1) * 128, :], in_=gbuf[:]), 16)

        block.sync(wsto)

    WS_ALL = s_ws.n

    if _os2.environ.get("DEV_B_DUMP") == "kvw":
        s_dp = C.sem("b3_dp")

        def dmpw(g):
            g.wait_ge(s_ws.h, WS_ALL)
            for r in range(16):
                s_dp.inc(g.dma_start(
                    out=out_x[r * 128:(r + 1) * 128, :],
                    in_=kv_win[r * 128:(r + 1) * 128, :]), 16)

        block.gpsimd(dmpw)
        DNW = s_dp.n

        class _OW:
            def barrier(self, eng):
                eng.wait_ge(s_dp.h, DNW)
        stP.close()
        return _OW()

    if _os2.environ.get("DEV_B_DUMP") == "kva":
        s_dp = C.sem("b3_dp")

        def dmpk(g):
            g.wait_ge(s_ws.h, WS_ALL)
            for r in range(16):
                s_dp.inc(g.dma_start(
                    out=out_x[r * 128:(r + 1) * 128, :],
                    in_=kv_all[TOK + r * 128: TOK + (r + 1) * 128, :]), 16)

        block.gpsimd(dmpk)
        DNK = s_dp.n

        class _OK:
            def barrier(self, eng):
                eng.wait_ge(s_dp.h, DNK)
        stP.close()
        return _OK()
    stP.close()

    att_dram = io["seg_x"]
    stA = contextlib.ExitStack()
    att_st = [C.sbuf(f"b3as{i}", [128, 256], BF16, stA) for i in range(2)]
    kTh = [C.sbuf(f"b3kT{i}", [128, SEQ], BF16, stA) for i in range(2)]
    vth = [C.sbuf(f"b3vt{i}", [128, SEQ], BF16, stA) for i in range(2)]
    kTw = [C.sbuf(f"b3kTw{i}", [128, TOK + 640], BF16, stA) for i in range(2)]
    vtw = [C.sbuf(f"b3vtw{i}", [128, TOK + 640], BF16, stA) for i in range(2)]
    qTt = [C.sbuf(f"b3qT{i}", [128, 256], BF16, stA) for i in range(2)]
    tabs = C.sbuf("b3tab", [3, 68 * 256], F32, stA)
    ramph = C.sbuf("b3ramp", [3, 16 * 128], F32, stA)
    killj = C.sbuf("b3kj", [128, 1], F32, stA)
    mrel = C.sbuf("b3mrel", [128, 4 * 256], F32, stA)
    ub64 = C.sbuf("b3ub", [128, 64], F32, stA)
    mb64 = C.sbuf("b3mb", [128, 64], F32, stA)
    urow = C.sbuf("b3ur", [1, 64], F32, stA)
    usum = C.sbuf("b3us", [1, 64], F32, stA)
    sumv8 = C.sbuf("b3sv", [128, 2], F32, stA)
    P = C.sbuf("b3P", [128, 4 * 512], BF16, stA)
    sum_sb = C.sbuf("b3sum", [1, 256], F32, stA)
    sum2 = C.sbuf("b3sum2", [1, 256], F32, stA)
    rec_sb = C.sbuf("b3rec", [1, 256], F32, stA)
    recB = C.sbuf("b3recB", [128, 256], F32, stA)
    o1t = C.sbuf("b3o1", [128, 64], F32, stA)

    psc = [C.psum(f"b3psc{i}", [128, 512], stA) for i in range(4)]
    pat = C.psum("b3pat", [128, 512], stA)
    psm = C.psum("b3psm", [128, 512], stA)
    prb = C.psum("b3prb", [128, 512], stA)
    psv = C.psum("b3psv", [128, 512], stA)

    s_kl = [C.sem(f"b3_kl{i}") for i in range(2)]
    s_wk = [C.sem(f"b3_wk{i}") for i in range(2)]
    s_ql = [C.sem(f"b3_ql{i}") for i in range(2)]
    s_tb = [C.sem(f"b3_tb{i}") for i in range(2)]
    s_cl = C.sem("b3_cl")
    s_sc = C.sem("b3_sc")
    s_ex = C.sem("b3_ex")
    s_mk = C.sem("b3_mk")
    s_pv = C.sem("b3_pv")
    s_sv = C.sem("b3_sv")
    s_sve = C.sem("b3_sve")
    s_sm = C.sem("b3_sm")
    s_rc = C.sem("b3_rc")
    s_bc = C.sem("b3_bc")
    s_rb = C.sem("b3_rb")
    s_ev = C.sem("b3_ev")

    def ldcb(sy):
        sy.wait_ge(s_ws.h, WS_ALL)
        for dst, srcn in [(ramph[:], "ramp3h"), (killj[:], "kill_j64"),
                          (ub64[:], "uB64"),
                          (mb64[:], "mB64"), (urow[:], "uni_flag"),
                          (usum[:], "uni_sum")]:
            s_cl.inc(sy.dma_start(out=dst, in_=io[srcn]), 16)

    block.sync(ldcb)
    CLN = s_cl.n

    kl_free = [0, 0]
    for kvh in range(8):
        bi = kvh % 2

        def klod(sy, kvh=kvh, bi=bi, gate=kl_free[bi], svg=s_sv.n):
            if kvh < 2:
                sy.wait_ge(s_cl.h, CLN)
            if gate:
                sy.wait_ge(s_pv.h, gate)
            s_kl[bi].inc(sy.dma_start(
                out=kTh[bi][:].rearrange("p (o j) -> p o j", o=1),
                in_=kv_all[:, kvh * 128:(kvh + 1) * 128]
                    .rearrange("j (o p) -> j o p", o=1),
                transpose=True), 16)
            for q4 in range(4):
                s_kl[bi].inc(sy.dma_start(
                    out=vth[bi][:].rearrange("p (c d) -> p c d", c=64)
                        [:, q4 * 16:(q4 + 1) * 16, :],
                    in_=kv_all[q4 * 2048:(q4 + 1) * 2048,
                               1024 + kvh * 128:1024 + (kvh + 1) * 128]
                        .rearrange("(c p) d -> p c d", p=128)), 16)
            s_wk[bi].inc(sy.dma_start(
                out=kTw[bi][:].rearrange("p (o j) -> p o j", o=1),
                in_=kv_win[:, kvh * 128:(kvh + 1) * 128]
                    .rearrange("j (o p) -> j o p", o=1),
                transpose=True), 16)
            s_wk[bi].inc(sy.dma_start(
                out=vtw[bi][:].rearrange("p (c d) -> p c d", c=WINC),
                in_=kv_win[:, 1024 + kvh * 128:1024 + (kvh + 1) * 128]
                    .rearrange("(c p) d -> p c d", p=128)), 16)

        block.sync(klod)
        kln = s_kl[bi].n
        wkn = s_wk[bi].n

        def svmm(t, kvh=kvh, bi=bi, kln=kln, sveg=s_sve.n):
            t.wait_ge(s_kl[bi].h, kln)
            if sveg:
                t.wait_ge(s_sve.h, sveg)
            v3 = vth[bi][:].rearrange("p (c d) -> p c d", c=64)
            for c in range(64):
                ins = t.matmul(psv[:, 0:1], v3[:, c, :], onec[:],
                               start=(c == 0), stop=(c == 63))
            s_sv.inc(ins)

        block.tensor(svmm)
        svn = s_sv.n

        def svev(v, svn=svn, vprior=s_v.n):
            v.wait_ge(s_v.h, vprior)
            v.wait_ge(s_sv.h, svn)
            s_sve.inc(v.tensor_scalar(sumv8[:, 0:1], psv[:, 0:1],
                                      1.0 / SEQ, None, ALU.mult))

        block.vector(svev)
        sven = s_sve.n

        for hh in range(2):
            h = 2 * kvh + hh
            if kvh == 0 and hh == 0:
                ql_free = [0, 0]
            for jb in range(NQB):
                hj = (h, jb)
                qbi = jb % 2

                def qlod(sy, h=h, jb=jb, qbi=qbi, gate=ql_free[qbi],
                         tbg=s_sc.n, mkg=s_mk.n):
                    if gate:
                        sy.wait_ge(s_sc.h, gate)
                    if tbg:
                        sy.wait_ge(s_sc.h, tbg)
                    if mkg:
                        sy.wait_ge(s_mk.h, mkg)
                    s_ql[qbi].inc(sy.dma_start(
                        out=qTt[qbi][:].rearrange("p (o j) -> p o j", o=1),
                        in_=q_dram[jb * 256:(jb + 1) * 256,
                                   h * 128:(h + 1) * 128]
                            .rearrange("j (o p) -> j o p", o=1),
                        transpose=True), 16)
                    s_tb[0].inc(sy.dma_start(
                        out=tabs[:].rearrange("r (c t) -> r c t", c=68),
                        in_=io["atab"][jb].rearrange("c r t -> r c t")), 16)
                    s_tb[1].inc(sy.dma_start(
                        out=mrel[:],
                        in_=io["maskrel"][:, jb * 1024:(jb + 1) * 1024]), 16)

                block.sync(qlod)
                qln = s_ql[qbi].n
                tbn = s_tb[0].n

                exp_pair = {}
                pv_at = {}
                for c in range(68):
                    p_ = c // 2
                    bk = p_ % 4
                    col = (c % 2) * 256

                    def cqkt(t, h=h, jb=jb, bi=bi, qbi=qbi, c=c, bk=bk,
                             col=col, qln=qln, tbn=tbn, kln=kln, wkn=wkn,
                             exg=exp_pair.get(c // 2 - 4, 0)):
                        if c == 0:
                            t.wait_ge(s_ql[qbi].h, qln)
                            t.wait_ge(s_tb[0].h, tbn)
                            t.wait_ge(s_kl[bi].h, kln)
                            t.wait_ge(s_wk[bi].h, wkn)
                        if exg:
                            t.wait_ge(s_ex.h, exg)
                        if c < 64:
                            lhs = kTh[bi][:, c * 128:(c + 1) * 128]
                        else:
                            r = c - 64
                            off = jb * 256 + (128, 256, 640, 768)[r]
                            lhs = kTw[bi][:, off:off + 128]
                        t.matmul(psc[bk][:, col:col + 256], lhs,
                                 qTt[qbi][:, 0:256], start=True, stop=False)
                        ins = t.matmul(psc[bk][:, col:col + 256],
                                       ramph[:, h * 128:(h + 1) * 128],
                                       tabs[:, c * 256:(c + 1) * 256],
                                       start=False, stop=True)
                        s_sc.inc(ins)

                    block.tensor(cqkt)
                    scn = s_sc.n

                    if c >= 64 and c % 2 == 1:
                        def cmka(v, jb=jb, c=c, bk=bk, scn=scn,
                                 vprior=s_v.n, mreln=s_tb[1].n):
                            v.wait_ge(s_v.h, vprior)
                            v.wait_ge(s_sc.h, scn)
                            r0 = c - 1 - 64
                            v.wait_ge(s_tb[1].h, mreln)
                            s_mk.inc(v.tensor_add(
                                psc[bk][:, 0:512],
                                psc[bk][:, 0:512],
                                mrel[:, r0 * 256:(r0 + 2) * 256]))

                        block.vector(cmka)
                    mka = s_mk.n

                    if c % 2 == 1:
                        def cexp(s, c=c, p_=p_, bk=bk, scn=scn, mka=mka,
                                 pvg=pv_at.get(2 * (p_ - 4) + 1, 0),
                                 pvp=s_pv.n):
                            s.wait_ge(s_sc.h, scn)
                            if c >= 64:
                                s.wait_ge(s_mk.h, mka)
                            if pvg:
                                s.wait_ge(s_pv.h, pvg)
                            elif p_ < 4 and pvp:
                                s.wait_ge(s_pv.h, pvp)
                            s_ex.inc(s.activation(
                                P[:, (p_ % 4) * 512:(p_ % 4 + 1) * 512],
                                psc[bk][:, 0:512], AF.Exp, scale=1.0 / SQD))

                        block.scalar(cexp)
                        exp_pair[p_] = s_ex.n

                        if c == 1:
                            def cmsk(v, exn=s_ex.n, p_=p_, vprior=s_v.n):
                                v.wait_ge(s_v.h, vprior)
                                v.wait_ge(s_ex.h, exn)
                                sl = (p_ % 4) * 512
                                s_mk.inc(v.tensor_scalar(
                                    P[:, sl:sl + 256], P[:, sl:sl + 256],
                                    killj[:], None, ALU.mult))

                            block.vector(cmsk)
                        mkn2 = s_mk.n

                        for cc in (c - 1, c):
                            def cpv(t, jb=jb, bi=bi, c=cc, p_=p_, mkn2=mkn2,
                                    exn=exp_pair[p_], evg=s_ev.n):
                                if c == 0 and evg:
                                    t.wait_ge(s_ev.h, evg)
                                if c <= 1:
                                    t.wait_ge(s_mk.h, mkn2)
                                t.wait_ge(s_ex.h, exn)
                                sl = (p_ % 4) * 512 + (c % 2) * 256
                                if c < 64:
                                    lhs = vth[bi][:].rearrange(
                                        "p (cc2 d) -> p cc2 d", cc2=64)[:, c, :]
                                else:
                                    r = c - 64
                                    off = jb * 256 + (128, 256, 640, 768)[r]
                                    lhs = vtw[bi][:, off:off + 128]
                                t.matmul(pat[:, 0:256], lhs, P[:, sl:sl + 256],
                                         start=(c == 0), stop=(c == 67))
                                ins = t.matmul(psm[0:1, 0:256], onec[:],
                                               P[:, sl:sl + 256],
                                               start=(c == 0), stop=(c == 67))
                                s_pv.inc(ins)

                            block.tensor(cpv)
                            pv_at[cc] = s_pv.n

                pvn = s_pv.n

                def bsum(s, pvn=pvn):
                    s.wait_ge(s_pv.h, pvn)
                    s_sm.inc(s.copy(sum_sb[:], psm[0:1, 0:256]))

                block.scalar(bsum)
                smn = s_sm.n

                def brcp(v, jb=jb, smn=smn, sven=sven, vprior=s_v.n):
                    v.wait_ge(s_v.h, vprior)
                    v.wait_ge(s_sm.h, smn)
                    if jb == 0:
                        s_v.inc(v.tensor_mul(sum2[:, 0:64], sum_sb[:, 0:64],
                                             mb64[0:1, 0:64]))
                        v.wait_ge(s_v.h, s_v.n)
                        s_v.inc(v.tensor_add(sum_sb[:, 0:64], sum2[:, 0:64],
                                             usum[:]))
                        v.wait_ge(s_v.h, s_v.n)
                    s_rc.inc(v.reciprocal(rec_sb[:], sum_sb[:]))

                block.vector(brcp)
                rcn = s_rc.n

                def bbc(t, rcn=rcn, rbg=s_rb.n):
                    t.wait_ge(s_rc.h, rcn)
                    if rbg:
                        t.wait_ge(s_rb.h, rbg)
                    s_bc.inc(t.matmul(prb[:, 0:256], oner[:], rec_sb[:],
                                      start=True, stop=True))

                block.tensor(bbc)
                bcn = s_bc.n

                abi = (h * NQB + jb) % 2

                def bev(v, h=h, jb=jb, abi=abi, bcn=bcn, sven=sven,
                        vprior=s_v.n, stg=s_ev.n):
                    v.wait_ge(s_v.h, vprior)
                    v.wait_ge(s_bc.h, bcn)
                    s_rb.inc(v.tensor_copy(recB[:], prb[:, 0:256]))
                    v.wait_ge(s_v.h, s_v.n)
                    v.wait_ge(s_rb.h, s_rb.n)
                    if stg >= 32:
                        v.wait_ge(s_ev.h, stg - 16)
                    if jb == 0:
                        v.wait_ge(s_sve.h, sven)
                        s_v.inc(v.tensor_mul(o1t[:], pat[:, 0:64], recB[:, 0:64]))
                        v.wait_ge(s_v.h, s_v.n)
                        s_v.inc(v.tensor_mul(o1t[:], o1t[:], mb64[:, 0:64]))
                        v.wait_ge(s_v.h, s_v.n)
                        s_v.inc(v.scalar_tensor_tensor(
                            att_st[abi][:, 0:64],
                            ub64[:, 0:64], sumv8[:, 0:1], o1t[:],
                            ALU.mult, ALU.add))
                        v.wait_ge(s_v.h, s_v.n)
                        s_v.inc(v.tensor_mul(att_st[abi][:, 64:256],
                                             pat[:, 64:256], recB[:, 64:256]))
                    else:
                        s_v.inc(v.tensor_mul(att_st[abi][:, 0:256],
                                             pat[:, 0:256], recB[:, 0:256]))

                block.vector(bev)
                evn = s_v.n

                def asto(sy, h=h, jb=jb, abi=abi, evn=evn):
                    sy.wait_ge(s_v.h, evn)
                    if h == 0 and jb == 0:
                        sy.wait_ge(io["s_zi"].h, io["zi_n"])
                    s_ev.inc(sy.dma_start(
                        out=att_dram[h * 128:(h + 1) * 128,
                                     jb * 256:(jb + 1) * 256],
                        in_=att_st[abi][:]), 16)

                block.sync(asto)
                ql_free[qbi] = s_sc.n
        kl_free[bi] = s_pv.n

    ATT_ALL = s_ev.n
    stA.close()

    if _os2.environ.get("DEV_B_DUMP") == "att":
        s_dp = C.sem("b3_dp")

        def dmpa(g):
            g.wait_ge(s_ev.h, ATT_ALL)
            for r in range(16):
                s_dp.inc(g.dma_start(
                    out=out_x[r * 128:(r + 1) * 128, :],
                    in_=att_dram[r * 128:(r + 1) * 128, :]), 16)

        block.gpsimd(dmpa)
        DN2 = s_dp.n

        class _O2:
            def barrier(self, eng):
                eng.wait_ge(s_dp.h, DN2)
        return _O2()

    st2 = contextlib.ExitStack()
    attT = C.sbuf("b3attT", [128, 16 * 2048], BF16, st2)
    s_al = C.sem("b3_al")

    def lda(sy):
        sy.wait_ge(s_ev.h, ATT_ALL)
        for mc in range(16):
            s_al.inc(sy.dma_start(
                out=attT[:, mc * 2048:(mc + 1) * 2048],
                in_=att_dram[mc * 128:(mc + 1) * 128, :]), 16)

    block.sync(lda)
    ALN = s_al.n
    wo_sb = C.sbuf("b3wo", [128, 16 * 2048], BF16, st2)
    ysb = [C.sbuf(f"b3y{i}", [128, 2048], F32, st2) for i in range(2)]
    pyb = [C.psum(f"b3py{i}", [128, 512], st2) for i in range(2)]
    s_wo = C.sem("b3_wo")
    s_ymm = C.sem("b3_ymm")
    s_yev = C.sem("b3_yev")
    s_res = [C.sem(f"b3_res{i}") for i in range(2)]

    def ldwo(g):
        g.wait_ge(s_ev.h, ATT_ALL)
        for mc in range(4):
            s_wo.inc(g.dma_start(
                out=wo_sb[:, mc * 8192:(mc + 1) * 8192],
                in_=wo_b[:, mc * 8192:(mc + 1) * 8192]), 16)

    block.gpsimd(ldwo)

    res_at = {}
    for tc in range(16):
        for fb in range(4):
            i = 4 * tc + fb

            def mmy(t, tc=tc, fb=fb, i=i, yegate=s_yev.n):
                if i == 0:
                    t.wait_ge(s_wo.h, 64)
                    t.wait_ge(s_al.h, ALN)
                if yegate >= 2:
                    t.wait_ge(s_yev.h, yegate - 1)
                wo3 = wo_sb[:].rearrange("p (m f) -> p m f", m=16)
                pb = pyb[i % 2]
                t.matmul(pb[:], onerb[:], bo[:, fb * 512:(fb + 1) * 512],
                         start=True, stop=False)
                for mc in range(16):
                    ins = t.matmul(
                        pb[:],
                        attT[:, mc * 2048 + tc * 128: mc * 2048 + (tc + 1) * 128],
                        wo3[:, mc, fb * 512:(fb + 1) * 512],
                        start=False, stop=(mc == 15))
                s_ymm.inc(ins)

            block.tensor(mmy)
            ymn = s_ymm.n

            def evy(v, tc=tc, fb=fb, i=i, ymn=ymn, rgate=res_at.get(tc - 2, 0),
                    vprior=s_v.n):
                v.wait_ge(s_v.h, vprior)
                v.wait_ge(s_ymm.h, ymn)
                if fb == 0 and rgate:
                    v.wait_ge(s_res[tc % 2].h, rgate)
                s_yev.inc(v.tensor_copy(
                    ysb[tc % 2][:, fb * 512:(fb + 1) * 512], pyb[i % 2][:]))

            block.vector(evy)

        yn = s_yev.n

        def acc(g, tc=tc, yn=yn):
            g.wait_ge(s_yev.h, yn)
            s_res[tc % 2].inc(g.dma_start(
                out=out_x[tc * 128:(tc + 1) * 128, :],
                in_=ysb[tc % 2][:], accum_op=ALU.add), 16)

        block.gpsimd(acc)
        res_at[tc] = s_res[tc % 2].n

    st2.close()
    R = [s_res[0].n, s_res[1].n]

    class ResOut:
        def barrier(self, eng):
            eng.wait_ge(s_res[0].h, R[0])
            eng.wait_ge(s_res[1].h, R[1])

    return ResOut()



def _host_consts(core):
    b, q = divmod(core, 4)
    kk = np.arange(128)[:, None]
    tt = np.arange(QB)[None, :]
    c = {}

    va = np.ones((LTOK,), np.float32)
    if q == 0:
        va[:HALO] = 0.0
    c["valid_a"] = np.ascontiguousarray(
        va.reshape(LCH, 128).T).astype(BF)

    m = np.zeros((4, 128, QB), np.float32)
    m[0] = (kk > tt)
    m[1] = (tt <= kk + 127)
    m[2] = (kk <= tt)
    m[3] = (tt >= kk + 128)
    c["mask_a"] = m.astype(BF)

    c["ones_col"] = np.ones((128, 1), np.float32).astype(BF)
    c["ones_row"] = np.ones((1, 128), np.float32)
    c["ones_row_bf"] = np.ones((1, 128), np.float32).astype(BF)
    base = q * TOK
    t64 = np.arange(QB, dtype=np.float64)
    kk64 = np.arange(128, dtype=np.float64)
    KILL = 1.0e9
    tab = np.zeros((NQB, 68, 3, QB), np.float32)
    reldelta = (256, 128, -256, -384)
    for jb in range(NQB):
        Q0 = base + QB * jb
        for ch in range(64):
            delta = Q0 - 128 * ch
            if -512 < delta < 384:
                tab[jb, ch, 0, :] = -KILL
                continue
            sg = 1.0 if delta >= 384 else -1.0
            tab[jb, ch, 0, :] = -sg * (Q0 + t64) + 257.0
            tab[jb, ch, 1, :] = sg
            tab[jb, ch, 2, :] = sg * 128.0 * ch
        for r, dl in enumerate(reldelta):
            sg = 1.0 if dl > 0 else -1.0
            tab[jb, 64 + r, 0, :] = -sg * (Q0 + t64) + 257.0
            tab[jb, 64 + r, 1, :] = sg
            tab[jb, 64 + r, 2, :] = sg * float(Q0 - dl)
    c["atab"] = tab

    mrel = np.zeros((128, NQB * 4 * QB), np.float32)
    for jb in range(NQB):
        Q0 = base + QB * jb
        for r, dl in enumerate(reldelta):
            j0 = Q0 - dl
            jj = j0 + np.arange(128)[:, None]
            dmat = (Q0 + np.arange(QB)[None, :]) - jj
            ok = (np.abs(dmat) > 256) & (jj >= 64) & (jj < SEQ)
            mrel[:, (jb * 4 + r) * QB:(jb * 4 + r + 1) * QB] = \
                np.where(ok, 0.0, -1.0e9)
    c["maskrel"] = mrel

    rows = base - 384 + np.arange(TOK + 640)
    c["kvwin_idx"] = np.ascontiguousarray(
        np.clip(rows, 0, SEQ - 1).astype(np.int32)
        .reshape((TOK + 640) // 128, 128).T)

    half = 64
    freqs = 1.0 / 10000.0 ** (np.arange(half, dtype=np.float64) / half)
    pos = base + np.arange(TOK, dtype=np.float64)
    ang = np.outer(pos, freqs)
    c["rope_sin"] = np.ascontiguousarray(
        np.sin(ang).astype(np.float32).reshape(NCH, 128, 64)
        .transpose(1, 0, 2).reshape(128, NCH * 64))
    c["rope_cos"] = np.ascontiguousarray(
        np.cos(ang).astype(np.float32).reshape(NCH, 128, 64)
        .transpose(1, 0, 2).reshape(128, NCH * 64))

    r3 = np.zeros((3, 16 * 128), np.float32)
    for h in range(16):
        s = float(SLOPES[h]) * SQD
        r3[0, h * 128:(h + 1) * 128] = s
        r3[1, h * 128:(h + 1) * 128] = s * np.arange(128)
        r3[2, h * 128:(h + 1) * 128] = s
    c["ramp3h"] = r3

    kj = np.ones((128, 1), np.float32)
    kj[:64] = 0.0
    c["kill_j64"] = kj

    u = np.zeros(64, np.float32)
    if q == 0:
        u[:] = 1.0
    c["uB64"] = np.broadcast_to(u, (128, 64)).astype(np.float32).copy()
    c["mB64"] = np.broadcast_to(1.0 - u, (128, 64)).astype(np.float32).copy()
    c["uni_flag"] = u.reshape(1, 64).copy()
    c["uni_sum"] = (u * float(SEQ)).reshape(1, 64).copy()

    ltri = (np.arange(128)[:, None] < np.arange(128)[None, :])
    c["ltri"] = ltri.astype(np.float32).astype(BF)
    c["tokf"] = (np.arange(NCH)[None, :] * 128
                 + np.arange(128)[:, None]).astype(np.float32)
    c["base_r"] = (np.arange(NE, dtype=np.float32) * CAP).reshape(1, NE)
    c["lim_r"] = (np.arange(NE, dtype=np.float32) * CAP + CAP).reshape(1, NE)
    c["mfill"] = np.full((128, 256), 1.0e5, np.float32)
    return c



def build_program():
    nc = bass.Bass("TRN2", target_bir_lowering=False)

    def din(name, shape, dt=F32):
        return nc.dram_tensor(name, shape, dt, kind="ExternalInput").ap()

    io = {}
    io["x_own"] = din("x_own", [TOK, D])
    io["x_halo"] = din("x_halo", [HALO, D])
    io["wq_a"] = din("wq_a", [16, 128, D])
    io["wk_a"] = din("wk_a", [128, 16 * 512])
    io["wv_a"] = din("wv_a", [128, 16 * 512])
    io["wo_a"] = din("wo_a", [128, 16 * 2048])
    bo_a = din("bo_a", [1, D])
    valid_a = din("valid_a", [128, LCH], BF16)
    mask_a = din("mask_a", [4, 128, QB], BF16)
    ones_col = din("ones_col", [128, 1], BF16)
    ones_row = din("ones_row", [1, 128])
    ones_row_bf = din("ones_row_bf", [1, 128], BF16)

    io["wq_b"] = din("wq_b", [128, 16 * D])
    io["wkv_b"] = din("wkv_b", [128, 16 * D])
    io["wo_b"] = din("wo_b", [128, 16 * D])
    bo_b = din("bo_b", [1, D])
    io["rope_sin"] = din("rope_sin", [128, 16 * 64])
    io["rope_cos"] = din("rope_cos", [128, 16 * 64])
    io["atab"] = din("atab", [NQB, 68, 3, QB])
    io["maskrel"] = din("maskrel", [128, NQB * 4 * QB])
    io["kvwin_idx"] = din("kvwin_idx", [128, (TOK + 640) // 128], I32)
    io["ramp3h"] = din("ramp3h", [3, 16 * 128])
    io["kill_j64"] = din("kill_j64", [128, 1])
    io["uB64"] = din("uB64", [128, 64])
    io["mB64"] = din("mB64", [128, 64])
    io["uni_flag"] = din("uni_flag", [1, 64])
    io["uni_sum"] = din("uni_sum", [1, 64])
    io["gate_wT"] = din("gate_wT", [128, 16 * NE])
    io["w1T"] = din("w1T", [NE, 16, 128, 2048])
    io["w3T"] = din("w3T", [NE, 16, 128, 2048])
    io["w2T"] = din("w2T", [NE, 128, 16 * 2048])
    ltri = din("ltri", [128, 128], BF16)
    tokf = din("tokf", [128, NCH])
    base_r = din("base_r", [1, NE])
    io["lim_r"] = din("lim_r", [1, NE])
    mfill = din("mfill", [128, 256])

    io["out_x"] = nc.dram_tensor("out_x", [TOK, D], F32,
                                 kind="ExternalOutput").ap()
    io["xn_dram"] = nc.dram_tensor("xn_dram", [LTOK, D], BF16).ap()
    io["q_dram"] = nc.dram_tensor("q_dram", [D, D], BF16).ap()
    io["kv_contrib"] = nc.dram_tensor("kv_contrib", [TOK, D], BF16).ap()
    io["kv_all"] = nc.dram_tensor("kv_all", [4 * TOK, D], BF16).ap()
    io["kv_win"] = nc.dram_tensor("kv_win", [TOK + 640, D], BF16).ap()
    io["seg_x"] = nc.dram_tensor("seg_x", [NE * CAP, D], BF16).ap()
    io["seg_meta"] = nc.dram_tensor("seg_meta", [NE * CAP, 2], F32).ap()
    io["y01"] = nc.dram_tensor("y01", [2 * TOK, D], F32).ap()

    stack = contextlib.ExitStack()
    with nc.Block() as block:
        C = Ctx(nc, block, stack)

        onec = C.sbuf("onec", [128, 1], BF16)
        oner = C.sbuf("oner", [1, 128], F32)
        onerb = C.sbuf("onerb", [1, 128], BF16)
        maska_sb = C.sbuf("maska_sb", [128, 4 * QB], BF16)
        valida_sb = C.sbuf("valida_sb", [128, LCH], BF16)
        bo_a_sb = C.sbuf("bo_a_sb", [1, D], BF16)
        ltri_sb = C.sbuf("ltri_sb", [128, 128], BF16)
        tokf_sb = C.sbuf("tokf_sb", [128, NCH], F32)
        base_sb = C.sbuf("base_sb", [1, NE], F32)

        s_c = C.sem("consts")

        def ldc(sy):
            for dst, src in [
                (onec[:], ones_col), (oner[:], ones_row), (onerb[:], ones_row_bf),
                (maska_sb[:].rearrange("p (m t) -> p m t", m=4),
                 mask_a.rearrange("m p t -> p m t")),
                (valida_sb[:], valid_a),
                (ltri_sb[:], ltri), (tokf_sb[:], tokf), (base_sb[:], base_r),
            ]:
                s_c.inc(sy.dma_start(out=dst, in_=src), 16)

        block.sync(ldc)

        s_cg = C.sem("constsg")

        bo_b_sb = C.sbuf("bo_b_sb", [1, D], BF16)
        io["bo_b_sb"] = bo_b_sb

        def ldc2(g):
            s_cg.inc(g.dma_start(out=bo_a_sb[:], in_=bo_a), 16)
            s_cg.inc(g.dma_start(out=bo_b_sb[:], in_=bo_b), 16)

        block.gpsimd(ldc2)

        def regs(g):
            r1 = C.stack.enter_context(g.register("bnd_seg"))
            g.reg_mov(r1, NE * CAP - 1)
            io["bnd_seg"] = g.snap(r1)
            r2 = C.stack.enter_context(g.register("bnd_y"))
            g.reg_mov(r2, 2 * TOK - 1)
            io["bnd_y"] = g.snap(r2)

        block.gpsimd(regs)

        def ldc3(sy):
            s_c.inc(sy.dma_start(out=io["out_x"], in_=io["x_own"]), 16)

        block.sync(ldc3)

        zsb = C.sbuf("zsb", [128, D], BF16)
        s_zi = C.sem("zinit")
        io["s_zi"] = s_zi

        def zi(v):
            s_zi.inc(v.memset(zsb[:], 0.0))

        block.vector(zi)

        def zid(sy):
            sy.wait_ge(s_zi.h, 1)
            for r in range(NE * CAP // 128):
                s_zi.inc(sy.dma_start(
                    out=io["seg_x"][r * 128:(r + 1) * 128, :], in_=zsb[:]), 16)

        block.sync(zid)
        io["zi_n"] = s_zi.n
        CWAIT = s_c.n

        io.update(dict(onec=onec, oner=oner, onerb=onerb, maska=maska_sb,
                       valida=valida_sb, bo_a=bo_a_sb, ltri=ltri_sb,
                       tokf=tokf_sb, base_r=base_sb, mfill=mfill))

        lnstack = contextlib.ExitStack()
        io["ln1"] = emit_ln(
            C, "ln1", LCH,
            lambda k: (io["x_halo"][k * 128:(k + 1) * 128, :] if k < 8
                       else io["x_own"][(k - 8) * 128:(k - 7) * 128, :]),
            io["xn_dram"], 0, [(s_c, CWAIT), (s_cg, s_cg.n)], stack=lnstack)
        lnstack.close()
        if DEV_STOP == "ln":
            fin_o = io["ln1"]
        elif _os3.environ.get("DEV_SKIP_A"):
            fin_o = emit_moe(C, io, "m2", io["ln1"])
        elif _os3.environ.get("DEV_ONLY_B"):
            fin_o = emit_attn_b(C, io, io["ln1"])
        else:
            fin_o = emit_attn_a(C, io)
            if DEV_STOP == "full" and N_PHASES >= 2:
                fin_o = emit_moe(C, io, "m2", fin_o)
            if DEV_STOP == "full" and N_PHASES >= 3:
                fin_o = emit_attn_b(C, io, fin_o)
            if DEV_STOP == "full" and N_PHASES >= 4:
                fin_o = emit_moe(C, io, "m4", fin_o)

        def fin(g):
            fin_o.barrier(g)

        block.gpsimd(fin)

    stack.close()
    return nc


def build_in_maps(L0_attn_a_x, L0_attn_a_Wq, L0_attn_a_Wk, L0_attn_a_Wv,
                  L0_attn_a_Wo, L0_attn_a_bo, L0_attn_b_Wq, L0_attn_b_Wkv,
                  L0_attn_b_Wo, L0_attn_b_bo, L0_ffn_gate_w, L0_ffn_w1,
                  L0_ffn_w3, L0_ffn_w2):
    x = np.asarray(L0_attn_a_x, dtype=np.float32)
    wq = np.asarray(L0_attn_a_Wq, np.float32)
    wk = np.asarray(L0_attn_a_Wk, np.float32)
    wv = np.asarray(L0_attn_a_Wv, np.float32)
    wo = np.asarray(L0_attn_a_Wo, np.float32)
    gw = np.asarray(L0_ffn_gate_w, np.float32)
    w1 = np.asarray(L0_ffn_w1, np.float32)
    w3 = np.asarray(L0_ffn_w3, np.float32)
    w2 = np.asarray(L0_ffn_w2, np.float32)
    shared = {
        "wq_a": np.ascontiguousarray(
            wq.reshape(16, 128, 16, 128).transpose(2, 1, 0, 3)
              .reshape(16, 128, D)),
        "wk_a": np.ascontiguousarray(
            wk.reshape(16, 128, 512).transpose(1, 0, 2).reshape(128, 16 * 512)),
        "wv_a": np.ascontiguousarray(
            wv.reshape(16, 128, 512).transpose(1, 0, 2).reshape(128, 16 * 512)),
        "wo_a": np.ascontiguousarray(
            wo.reshape(16, 128, D).transpose(1, 0, 2).reshape(128, 16 * D)),
        "bo_a": np.ascontiguousarray(L0_attn_a_bo, np.float32).reshape(1, D),
        "wq_b": np.ascontiguousarray(
            np.asarray(L0_attn_b_Wq, np.float32).reshape(16, 128, D)
              .transpose(1, 0, 2).reshape(128, 16 * D)),
        "wkv_b": np.ascontiguousarray(
            np.asarray(L0_attn_b_Wkv, np.float32).reshape(16, 128, D)
              .transpose(1, 0, 2).reshape(128, 16 * D)),
        "wo_b": np.ascontiguousarray(
            np.asarray(L0_attn_b_Wo, np.float32).reshape(16, 128, D)
              .transpose(1, 0, 2).reshape(128, 16 * D)),
        "bo_b": np.ascontiguousarray(L0_attn_b_bo, np.float32).reshape(1, D),
        "gate_wT": np.ascontiguousarray(
            gw.T.reshape(16, 128, NE).transpose(1, 0, 2).reshape(128, 16 * NE)),
        "w1T": np.ascontiguousarray(
            w1.reshape(NE, 16, 128, 16, 128).transpose(0, 1, 4, 3, 2)
              .reshape(NE, 16, 128, D)),
        "w3T": np.ascontiguousarray(
            w3.reshape(NE, 16, 128, 16, 128).transpose(0, 1, 4, 3, 2)
              .reshape(NE, 16, 128, D)),
        "w2T": np.ascontiguousarray(
            w2.transpose(0, 2, 1).reshape(NE, 16, 128, D)
              .transpose(0, 2, 1, 3).reshape(NE, 128, 16 * D)),
    }

    in_maps = []
    for core in range(NCORES):
        b, q = divmod(core, 4)
        m = dict(shared)
        m["x_own"] = np.ascontiguousarray(x[b, q * TOK:(q + 1) * TOK])
        halo = np.zeros((HALO, D), np.float32)
        if q > 0:
            halo[:] = x[b, q * TOK - HALO:q * TOK]
        m["x_halo"] = halo
        m.update(_host_consts(core))
        in_maps.append(m)
    return in_maps


def kernel(**kw):
    in_maps = build_in_maps(**kw)
    nc = build_program()
    res = run_bass_kernel_spmd(nc, in_maps, core_ids=list(range(NCORES)))
    out = np.empty((2, SEQ, D), np.float32)
    for core in range(NCORES):
        b, q = divmod(core, 4)
        out[b, q * TOK:(q + 1) * TOK] = res.results[core]["out_x"]
    return out


